# version 18
# baseline (speedup 1.0000x reference)
import numpy as np
import concourse.bass as bass
import concourse.mybir as mybir
from concourse.bass_utils import run_bass_kernel_spmd

F32 = mybir.dt.float32
BF16 = mybir.dt.bfloat16
AF = mybir.ActivationFunctionType
ALU = mybir.AluOpType
AX = mybir.AxisListType

D = 1024
KC = 8
T = 4096
NT = 512
NTIL = T // NT
EPS = 1e-5

ENGINES = ("sync", "gpsimd", "tensor", "scalar", "vector")

BIAS_SPEC = [
    ("L0_attn_norm1_w", 8), ("L0_attn_norm1_b", 8), ("L0_attn_gate_b", 16),
    ("L0_attn_state_b", 8), ("L0_attn_out_b", 8), ("L0_attn_norm2_w", 8),
    ("L0_attn_norm2_b", 8), ("L0_attn_mlp_b1", 32), ("L0_attn_mlp_b2", 8),
    ("L0_ffn_router_b", 1), ("L0_ffn_load_balancer", 1),
    ("L0_ffn_me_eq_b", 8), ("L0_ffn_me_bv", 8), ("L0_ffn_me_bo", 8),
    ("L0_ffn_me_c1b", 16), ("L0_ffn_me_c2b", 8),
    ("L0_ffn_ce_syn_b", 8), ("L0_ffn_ce_bv", 8), ("L0_ffn_ce_bo", 8),
    ("L0_ffn_ce_n1g", 8), ("L0_ffn_ce_n1b", 8), ("L0_ffn_ce_f1b", 16),
    ("L0_ffn_ce_f2b", 8), ("L0_ffn_ce_n2g", 8), ("L0_ffn_ce_n2b", 8),
    ("L0_ffn_ce_gen_b", 8), ("L0_ffn_ml_b1", 32), ("L0_ffn_ml_b2", 8),
]
BIAS_OFF = {}
_off = 0
for _n, _c in BIAS_SPEC:
    BIAS_OFF[_n] = _off
    _off += _c
NBIAS = _off

WSHAPE = {
    "L0_attn_gate_w": (D, 2048), "L0_attn_state_w": (D, D),
    "L0_attn_out_w": (D, D), "L0_attn_mlp_w1": (D, 4096),
    "L0_attn_mlp_w2": (4096, D), "L0_ffn_router_w": (D, 4),
    "L0_ffn_sw_w1": (D, 4096), "L0_ffn_sw_w3": (D, 4096),
    "L0_ffn_sw_w2": (4096, D), "L0_ffn_me_eq_w": (D, D),
    "L0_ffn_me_wv": (D, D), "L0_ffn_me_wo": (D, D),
    "L0_ffn_me_c1w": (D, 2048), "L0_ffn_me_c2w": (2048, D),
    "L0_ffn_ce_syn_w": (D, D), "L0_ffn_ce_wv": (D, D),
    "L0_ffn_ce_wo": (D, D), "L0_ffn_ce_f1w": (D, 2048),
    "L0_ffn_ce_f2w": (2048, D), "L0_ffn_ce_gen_w": (D, D),
    "L0_ffn_ml_w1": (D, 4096), "L0_ffn_ml_w2": (4096, D),
}
WEIGHT_NAMES = list(WSHAPE)

R_MU1, R_RS1, R_P, R_W0, R_W1, R_W2, R_W3 = range(7)


class Prog:

    def __init__(self, nc):
        self.nc = nc
        self.streams = {e: [] for e in ENGINES}
        self.sems = {}
        self.counts = {}
        self.lastwait = {e: {} for e in ENGINES}

    def new_sem(self, ctx, name):
        self.sems[name] = ctx.enter_context(self.nc.semaphore(name))
        self.counts[name] = 0

    def init_pools(self, ctx, npool=20, nglob=12, nw=8):
        self.npool, self.nw = npool, nw
        for i in range(npool):
            self.new_sem(ctx, f"P{i}")
        for i in range(nglob):
            self.new_sem(ctx, f"G{i}")
        for i in range(nw):
            self.new_sem(ctx, f"W{i}")
        self.phase_used = 0
        self.glob_used = 0
        self.w_used = 0

    def buf_sem(self):
        assert self.phase_used < self.npool, "per-phase DMA sem pool exhausted"
        n = f"P{self.phase_used}"
        self.phase_used += 1
        return n

    def w_sem(self):
        assert self.w_used < self.nw, "weight (SWDGE) sem pool exhausted"
        n = f"W{self.w_used}"
        self.w_used += 1
        return n

    def g_sem(self):
        n = f"G{self.glob_used}"
        self.glob_used += 1
        return n

    def new_phase(self):
        self.phase_used = 0
        self.w_used = 0

    def wait(self, eng, sem, val):
        if not val or val <= 0:
            return
        if self.lastwait[eng].get(sem, 0) >= val:
            return
        self.lastwait[eng][sem] = val
        h = self.sems[sem]
        self.streams[eng].append(lambda e, h=h, v=val: e.wait_ge(h, v))

    def waits(self, eng, pts):
        for sem, val in pts.items():
            self.wait(eng, sem, val)

    def do(self, eng, maker, inc=None, incby=1):
        if inc is None:
            self.streams[eng].append(lambda e, m=maker: m(e))
            return None
        self.counts[inc] += incby
        cnt = self.counts[inc]
        h = self.sems[inc]
        self.streams[eng].append(lambda e, m=maker, h=h, b=incby: m(e).then_inc(h, b))
        return {inc: cnt}

    def fence(self):
        snap = dict(self.counts)
        for eng in ENGINES:
            self.waits(eng, snap)

    def finalize(self, block):
        for eng in ENGINES:
            stream = self.streams[eng]
            getattr(block, eng)(lambda e, s=stream: [f(e) for f in s])


class Trk:
    def __init__(self, sem=None):
        self.readers = {}
        self.writer = {}
        self.sem = sem

    def read_at(self, pt):
        if pt:
            for s, v in pt.items():
                self.readers[s] = max(self.readers.get(s, 0), v)

    def write_at(self, pt):
        if pt:
            for s, v in pt.items():
                self.writer[s] = max(self.writer.get(s, 0), v)

    def before_write(self):
        out = dict(self.writer)
        for s, v in self.readers.items():
            out[s] = max(out.get(s, 0), v)
        return out

    def before_read(self):
        return dict(self.writer)


def merge_pts(*pts):
    out = {}
    for p in pts:
        if p:
            for s, v in p.items():
                out[s] = max(out.get(s, 0), v)
    return out


def build_nc(dbg=()):
    nc = bass.Bass("TRN2", target_bir_lowering=False)
    from contextlib import ExitStack, ExitStack as ES

    x_d = nc.dram_tensor("x", [D, T], F32, kind="ExternalInput").ap()
    biasv_d = nc.dram_tensor("biasv", [128, NBIAS], F32, kind="ExternalInput").ap()
    fold_d = nc.dram_tensor("foldmask", [128, 16], F32, kind="ExternalInput").ap()
    ident_d = nc.dram_tensor("identity", [128, 128], F32, kind="ExternalInput").ap()
    w_d = {n: nc.dram_tensor(n, list(WSHAPE[n]), F32, kind="ExternalInput").ap()
           for n in WEIGHT_NAMES}
    out_d = nc.dram_tensor("out", [D, T], F32, kind="ExternalOutput").ap()

    def scratch(name, shape, dt):
        kind = "ExternalOutput" if name in dbg else "Internal"
        return nc.dram_tensor(name, shape, dt, kind=kind).ap()

    t1_sp = scratch("t1_sp", [D, T], F32)
    yloc_sp = scratch("yloc_sp", [D, T], BF16)
    t_sp = scratch("t_sp", [D, T], F32)
    x2n_sp = scratch("x2n_sp", [D, T], BF16)
    x1_sp = scratch("x1_sp", [D, T], F32)
    x1n_sp = scratch("x1n_sp", [D, T], BF16)
    h2_sp = scratch("h2_sp", [D, T], BF16)
    acc0_sp = scratch("acc0_sp", [D, T], F32)
    acc1_sp = scratch("acc1_sp", [D, T], F32)
    rows_sp = scratch("rows_sp", [8, T], F32)
    cc_in = scratch("cc_in", [128 * 9], F32)
    cc_out = scratch("cc_out", [8, 128 * 9], F32)
    hin_dbg = scratch("hin_dbg", [128, 9], F32) if "hin_dbg" in dbg else None

    dtrk = {nm: [Trk() for _ in range(NTIL)] for nm in
            ("t1_sp", "yloc_sp", "t_sp", "x2n_sp", "x1_sp", "x1n_sp", "h2_sp",
             "acc0_sp", "acc1_sp", "rows_sp")}
    dtrk.update({nm: Trk() for nm in ("cc_in", "cc_out")})

    ctx = ExitStack()
    pr = Prog(nc)
    for s in ("TS", "VS", "SS", "CC"):
        pr.new_sem(ctx, s)
    pr.init_pools(ctx)

    sb = lambda nm, sh, dt=F32: ctx.enter_context(nc.sbuf_tensor(nm, sh, dt))
    psum = lambda nm: ctx.enter_context(nc.psum_tensor(nm, [128, NT], F32))

    biasv = sb("biasv_s", [128, NBIAS])
    ones_col = sb("ones_col", [128, 1], BF16)
    ones_colf = sb("ones_colf", [128, 1])
    ones_row = sb("ones_row", [1, 128])
    consts = sb("consts", [1, 4])
    biasrl = sb("biasrl", [4, 1])
    ph12 = ES()
    sb12 = lambda nm, sh, dt=F32: ph12.enter_context(nc.sbuf_tensor(nm, sh, dt))
    hstate = sb12("hstate", [128, KC], BF16)
    h_in = sb12("h_in", [128, KC])
    Pcarry = sb12("Pcarry", [1, 1])
    foldm = sb12("foldm", [128, 16])
    mean_bs = sb12("mean_bs", [1, 1])
    rvec = sb12("rvec", [128, KC], BF16)
    msg = sb12("msg", [128, 9])
    gath = sb12("gath", [128, 72])
    aeff = sb12("aeff", [128, 1])
    hmsk = sb12("hmsk", [128, 8])

    biasv_trk, foldm_trk = Trk(pr.g_sem()), Trk(pr.g_sem())
    hstate_trk, hin_trk, Pc_trk = Trk(), Trk(), Trk()
    msg_trk, gath_trk = Trk(pr.g_sem()), Trk(pr.g_sem())
    meanb_trk, rvec_trk, biasrl_trk = Trk(), Trk(), Trk()

    psR = psum("psR")
    psB = [psum(f"psB{i}") for i in range(4)]
    psM = [psum(f"psM{i}") for i in range(3)]
    psR_trk = Trk()
    psR2_trk = Trk()
    psB_trk = [Trk() for _ in range(4)]
    psM_trk = [Trk() for _ in range(3)]
    psm_rot = [0]

    def next_psM():
        i = psm_rot[0] % 3
        psm_rot[0] += 1
        return psM[i], psM_trk[i]

    def dma(q, out_ap, in_ap, wtrks=(), rtrks=(), war=True):
        sem = None
        for t in list(wtrks) + list(rtrks):
            if t.sem is not None:
                sem = t.sem
                break
        assert sem is not None, "dma needs a tracked buffer with a sem"
        if war:
            for t in wtrks:
                pr.waits(q, t.before_write())
        for t in rtrks:
            pr.waits(q, t.before_read())
        pt = pr.do(q, lambda e, o=out_ap, i=in_ap: e.dma_start(out=o, in_=i),
                   inc=sem, incby=16)
        for t in wtrks:
            t.write_at(pt)
        for t in rtrks:
            t.read_at(pt)
        return pt

    _memloc_cache = {}

    def addr_range(t):
        nm = t.name
        if nm not in _memloc_cache:
            for a in nc.m.functions[0].allocations:
                try:
                    ml = a.memorylocations[0]
                except Exception:
                    continue
                _memloc_cache[ml.name] = (ml.addr, ml.size())
        lo, sz = _memloc_cache[nm]
        return lo, lo + sz

    _dead = []

    def retire(*pairs):
        snap = {s: pr.counts[s] for s in ("TS", "VS", "SS")}
        for t, trk in pairs:
            lo, hi = addr_range(t)
            pts = trk.before_write() if trk is not None else dict(snap)
            _dead.append((lo, hi, pts))

    def fresh(t, sem=None):
        trk = Trk(sem)
        lo, hi = addr_range(t)
        for a, b, pts in _dead:
            if a < hi and lo < b:
                trk.write_at(pts)
        if sem is not None:
            trk.write_at({sem: pr.counts[sem]})
        return trk

    def mm_acc(lhsT_fn, rhs_fn, nk, pbank, ptrk, deps=(), rtrks=(), cols=NT,
               rows=128, kdeps=None, rtrks_rec=()):
        pr.waits("tensor", ptrk.before_write())
        for dp in deps:
            if dp:
                pr.waits("tensor", dp)
        if kdeps is None:
            rtrks = list(rtrks) + list(rtrks_rec)
            rtrks_rec = ()
        for t in rtrks:
            pr.waits("tensor", t.before_read())
        pt = None
        for k in range(nk):
            if kdeps is not None and kdeps[k]:
                pr.waits("tensor", kdeps[k])
            last = k == nk - 1
            pt = pr.do("tensor",
                       lambda e, l=lhsT_fn(k), r=rhs_fn(k), s=(k == 0), p=last,
                       pb=pbank, rw=rows, cl=cols:
                       e.matmul(pb[0:rw, 0:cl], l, r, start=s, stop=p),
                       inc="TS" if last else None)
        ptrk.write_at(pt)
        for t in rtrks:
            t.read_at(pt)
        for t in rtrks_rec:
            t.read_at(pt)
        return pt

    def bcast(row_ap, pbi, deps=(), rtrks=(), cols=NT):
        pb, ptrk = psB[pbi], psB_trk[pbi]
        pr.waits("tensor", ptrk.before_write())
        for dp in deps:
            if dp:
                pr.waits("tensor", dp)
        for t in rtrks:
            pr.waits("tensor", t.before_read())
        pt = pr.do("tensor", lambda e, r=row_ap, p=pb, c=cols:
                   e.matmul(p[0:128, 0:c], ones_row[:], r, start=True, stop=True),
                   inc="TS")
        ptrk.write_at(pt)
        for t in rtrks:
            t.read_at(pt)
        return pt

    def _eng_op(eng, sem, maker, deps=(), reads=(), writes=()):
        for dp in deps:
            if dp:
                pr.waits(eng, dp)
        for t in reads:
            pr.waits(eng, t.before_read())
        for t in writes:
            pr.waits(eng, t.before_write())
        pt = pr.do(eng, maker, inc=sem)
        for t in reads:
            t.read_at(pt)
        for t in writes:
            t.write_at(pt)
        return pt

    def vec(maker, deps=(), reads=(), writes=()):
        return _eng_op("vector", "VS", maker, deps, reads, writes)

    def sca(maker, deps=(), reads=(), writes=()):
        return _eng_op("scalar", "SS", maker, deps, reads, writes)

    def rsqrt_row(dst_ap, src_ap, deps, trk=None):
        p1 = sca(lambda e, d=dst_ap, s=src_ap:
                 e.activation(d, s, AF.Ln, bias=consts[:, 0:1]), deps=deps)
        p2 = sca(lambda e, d=dst_ap: e.activation(d, d, AF.Exp, scale=-0.5),
                 deps=[p1])
        if trk is not None:
            trk.write_at(p2)
        return p2

    def load_weight(name, dst_sb, dst_trk, dcols, src_lo=None, src_hi=None,
                    krange=None, pts=None):
        wa = w_d[name]
        din = WSHAPE[name][0]
        ks = list(krange) if krange is not None else list(range(din // 128))
        lo = 0 if src_lo is None else src_lo
        hi = WSHAPE[name][1] if src_hi is None else src_hi
        assert hi - lo == dcols
        pt = None
        for ki, k in enumerate(ks):
            pt = dma("gpsimd",
                     dst_sb[:, ki * dcols:(ki + 1) * dcols],
                     wa[k * 128:(k + 1) * 128, lo:hi],
                     wtrks=(dst_trk,), war=(ki == 0))
        if pts is not None:
            pts.extend([pt] * len(ks))
        return pt

    def wslice(wsb, dcols, k, m):
        base = k * dcols + m * 128
        return wsb[:, base: base + 128]

    def bias_col(name, c):
        off = BIAS_OFF[name]
        return biasv[:, off + c: off + c + 1]

    x_v = x_d.rearrange("(c p) t -> p c t", p=128)
    t1_v = t1_sp.rearrange("(c p) t -> p c t", p=128)
    yloc_v = yloc_sp.rearrange("(c p) t -> p c t", p=128)
    t_v = t_sp.rearrange("(c p) t -> p c t", p=128)
    x2n_v = x2n_sp.rearrange("(c p) t -> p c t", p=128)
    x1_v = x1_sp.rearrange("(c p) t -> p c t", p=128)
    x1n_v = x1n_sp.rearrange("(c p) t -> p c t", p=128)
    h2_v = h2_sp.rearrange("(c p) t -> p c t", p=128)
    acc_v = [acc0_sp.rearrange("(c p) t -> p c t", p=128),
             acc1_sp.rearrange("(c p) t -> p c t", p=128)]
    out_v = out_d.rearrange("(c p) t -> p c t", p=128)

    def dram_tile(ap3, j):
        return ap3[:, :, j * NT:(j + 1) * NT]

    def sb_tile3(t):
        return t[:].rearrange("p (c n) -> p c n", n=NT)

    def chunk(t, c, width=NT):
        return t[:, c * width:(c + 1) * width]

    def ln_stats1(xb_t, xb_trk, r_mu, r_v, rows_trk, mubi, rsbi,
                  deps=(), src_f32=None, src_trk=None, defer=False):
        pt_mean = mm_acc(lambda k: ones_col[:], lambda k: chunk(xb_t, k), KC,
                         psR, psR_trk, deps=deps, rtrks=[xb_trk], rows=1)
        sq_src = xb_t if src_f32 is None else src_f32
        sq_sc = sq_ve = None
        for c in range(KC):
            rd = [src_trk] if (src_f32 is not None and src_trk is not None
                              and c < 2) else ()
            if c % 2 == 0:
                sq_sc = sca(lambda e, c=c, s=sq_src: e.activation(
                    chunk(xb_t, c), chunk(s, c), AF.Square),
                    deps=[pt_mean] if c == 0 else (), reads=rd,
                    writes=[xb_trk] if c == 0 else ())
            else:
                sq_ve = vec(lambda e, c=c, s=sq_src: e.tensor_mul(
                    chunk(xb_t, c), chunk(s, c), chunk(s, c)),
                    deps=[pt_mean] if c == 1 else (), reads=rd,
                    writes=[xb_trk] if c == 1 else ())
        xb_trk.write_at(sq_sc)
        xb_trk.write_at(sq_ve)
        if src_trk is not None:
            src_trk.read_at(sq_sc)
            src_trk.read_at(sq_ve)
        pr.waits("tensor", psR2_trk.before_write())
        pt_ss = mm_acc2(lambda k: ones_col[:], lambda k: chunk(xb_t, k), KC,
                        deps=[sq_sc, sq_ve], rtrks=[xb_trk])
        pt_mu = vec(lambda e, r=r_mu: e.tensor_scalar_mul(r, psR[0:1, 0:NT],
                                                          1.0 / 1024.0),
                    deps=[pt_mean], writes=[rows_trk])
        psR_trk.read_at(pt_mu)
        pt_m2 = vec(lambda e, r=r_v: e.tensor_scalar_mul(r, psR[32:33, 0:NT],
                                                         1.0 / 1024.0),
                    deps=[pt_ss], writes=[rows_trk])
        psR2_trk.read_at(pt_m2)
        pt_q = vec(lambda e, b=r_mu: e.tensor_mul(psR[0:1, 0:NT], b, b),
                   deps=[pt_mu])
        psR_trk.write_at(pt_q)
        pt_var = vec(lambda e, a=r_v: e.tensor_sub(a, a, psR[0:1, 0:NT]),
                     deps=[pt_m2, pt_q], writes=[rows_trk])
        psR_trk.read_at(pt_var)
        pt_rs = rsqrt_row(r_v, r_v, deps=[pt_var], trk=rows_trk)

        def do_b():
            pt_mub = bcast(r_mu, mubi, deps=[pt_mu], rtrks=[rows_trk])
            pt_rsb = bcast(r_v, rsbi, deps=[pt_rs], rtrks=[rows_trk])
            return pt_mub, pt_rsb
        if defer:
            return do_b
        return do_b()

    def mm_acc2(lhsT_fn, rhs_fn, nk, deps=(), rtrks=()):
        for dp in deps:
            if dp:
                pr.waits("tensor", dp)
        for t in rtrks:
            pr.waits("tensor", t.before_read())
        pt = None
        for k in range(nk):
            last = k == nk - 1
            pt = pr.do("tensor",
                       lambda e, l=lhsT_fn(k), r=rhs_fn(k), s=(k == 0), p=last:
                       e.matmul(psR[32:33, 0:NT], l, r, start=s, stop=p),
                       inc="TS" if last else None)
        psR2_trk.write_at(pt)
        for t in rtrks:
            t.read_at(pt)
        return pt

    dma("sync", biasv[:], biasv_d, wtrks=(biasv_trk,))
    dma("sync", foldm[:], fold_d, wtrks=(foldm_trk,))
    vec(lambda e: e.memset(ones_col[:], 1.0))
    vec(lambda e: e.memset(ones_colf[:], 1.0))
    vec(lambda e: e.memset(ones_row[:], 1.0))
    vec(lambda e: e.memset(consts[:, 0:1], EPS))
    vec(lambda e: e.memset(consts[:, 1:2], 1.0))
    vec(lambda e: e.memset(Pcarry[:], 1.0), writes=[Pc_trk])
    vec(lambda e: e.memset(hstate[:], 0.0), writes=[hstate_trk])
    pt_hin0 = vec(lambda e: e.memset(h_in[:], 0.0), writes=[hin_trk])
    off_rb = BIAS_OFF["L0_ffn_router_b"]
    off_lb = BIAS_OFF["L0_ffn_load_balancer"]
    vec(lambda e: e.tensor_add(biasrl[:], biasv[0:4, off_rb:off_rb + 1],
                               biasv[0:4, off_lb:off_lb + 1]),
        reads=[biasv_trk], writes=[biasrl_trk])
    pr.fence()

    ph1 = ES()
    sb1 = lambda nm, sh, dt=F32: ph1.enter_context(nc.sbuf_tensor(nm, sh, dt))
    gate_w = sb1("gate_w", [128, KC * 2048], BF16)
    state_w = sb1("state_w", [128, KC * 1024], BF16)
    pr.new_phase()
    gate_w_trk, state_w_trk = fresh(gate_w, pr.w_sem()), fresh(state_w, pr.w_sem())
    pw_gate = load_weight("L0_attn_gate_w", gate_w, gate_w_trk, 2048)
    pw_state = load_weight("L0_attn_state_w", state_w, state_w_trk, 1024)

    x_t = [sb1(f"x_t{i}", [128, KC * NT]) for i in range(2)]
    x_t_trk = [fresh(x_t[0], pr.buf_sem()), fresh(x_t[1], pr.buf_sem())]
    x_bt = [sb1(f"x_bt{i}", [128, KC * NT], BF16) for i in range(2)]
    x_bt_trk = [fresh(x_bt[0], pr.w_sem()), fresh(x_bt[1], pr.w_sem())]
    xc_t = sb1("xc_t", [128, KC * NT])
    xc_trk = fresh(xc_t)
    t1_t = sb1("t1_t", [128, KC * NT])
    t1_trk = fresh(t1_t, pr.buf_sem())
    xn_t = sb1("xn_t", [128, KC * NT], BF16)
    xn_trk = fresh(xn_t)
    gate_s = sb1("gate_s", [128, KC * NT], BF16)
    gg_s = sb1("gg_s", [128, KC * NT], BF16)
    gate_s_trk, gg_trk = fresh(gate_s), fresh(gg_s)
    s1_c = sb1("s1_c", [128, NT])
    s1_trk = fresh(s1_c)
    dbu_c = sb1("dbu_c", [128, NT])
    dbu_trk = fresh(dbu_c)
    y_t = sb1("y_t", [128, KC * NT], BF16)
    y_trk = fresh(y_t, pr.buf_sem())
    rvec_f = sb1("rvec_f", [128, KC])
    rvec_f_trk = fresh(rvec_f)
    zrow = sb1("zrow", [1, NT])
    rowst = sb1("rowst", [1, 10 * NT])
    rows_trk = fresh(rowst, pr.buf_sem())

    rowsd = sb1("rowsd", [1, 4 * NT])
    rowsd_trk = fresh(rowsd, pr.buf_sem())
    vec(lambda e: e.memset(zrow[:], 0.0))

    def row(i):
        return rowst[:, i * NT:(i + 1) * NT]

    def rowd(i):
        return rowsd[:, i * NT:(i + 1) * NT]

    pv = None
    for k in range(KC):
        pv = vec(lambda e, k=k: e.reduce_sum(rvec_f[:, k:k + 1],
                                             state_w[:, k * 1024:(k + 1) * 1024],
                                             axis=AX.X),
                 deps=[pw_state] if k == 0 else (), writes=[rvec_f_trk])
    state_w_trk.write_at(pw_state)
    state_w_trk.read_at(pv)
    pt_rvec = sca(lambda e: e.activation(rvec[:], rvec_f[:], AF.Identity,
                                         scale=1.0 / 1024.0),
                  reads=[rvec_f_trk], writes=[rvec_trk])
    off_sb_ = BIAS_OFF["L0_attn_state_b"]
    pr.waits("tensor", psR_trk.before_write())
    pr.waits("tensor", biasv_trk.before_read())
    pt_mb = pr.do("tensor", lambda e: e.matmul(
        psR[0:1, 0:8], ones_colf[:], biasv[0:128, off_sb_:off_sb_ + 8],
        start=True, stop=True), inc="TS")
    psR_trk.write_at(pt_mb)
    biasv_trk.read_at(pt_mb)
    p1 = vec(lambda e: e.reduce_sum(mean_bs[:], psR[0:1, 0:8], axis=AX.X),
             deps=[pt_mb], writes=[meanb_trk])
    psR_trk.read_at(p1)
    pt_meanb = vec(lambda e: e.tensor_scalar_mul(mean_bs[:], mean_bs[:],
                                                 1.0 / 1024.0),
                   deps=[p1], reads=[meanb_trk], writes=[meanb_trk])

    def st_stats(j):
        par = j % 2
        xt, xt_trk = x_t[par], x_t_trk[par]
        xbt, xbt_trk = x_bt[par], x_bt_trk[par]
        dma("sync", sb_tile3(xt), dram_tile(x_v, j), wtrks=(xt_trk,))
        dma("gpsimd", sb_tile3(xbt), dram_tile(x_v, j), wtrks=(xbt_trk,))
        pt_mean = mm_acc(lambda k: ones_col[:], lambda k, b=xbt: chunk(b, k), KC,
                         psR, psR_trk, rtrks=[xbt_trk], rows=1)
        sq_pt = None
        for c in range(KC):
            sq_pt = sca(lambda e, c=c, b=xbt: e.activation(
                chunk(b, c), chunk(b, c), AF.Square),
                deps=[pt_mean] if c == 0 else (),
                writes=[xbt_trk] if c == 0 else ())
        xbt_trk.write_at(sq_pt)
        pt_ss = mm_acc2(lambda k: ones_col[:], lambda k, b=xbt: chunk(b, k), KC,
                        deps=[sq_pt], rtrks=[xbt_trk])
        pt_mu = vec(lambda e: e.tensor_scalar_mul(row(0), psR[0:1, 0:NT],
                                                  1.0 / 1024.0),
                    deps=[pt_mean], writes=[rows_trk])
        psR_trk.read_at(pt_mu)
        dma("scalar", rows_sp[R_MU1:R_MU1 + 1, j * NT:(j + 1) * NT], row(0),
            rtrks=(rows_trk,), wtrks=(dtrk["rows_sp"][j],))
        pt_m2 = vec(lambda e: e.tensor_scalar_mul(row(1), psR[32:33, 0:NT],
                                                  1.0 / 1024.0),
                    deps=[pt_ss], writes=[rows_trk])
        psR2_trk.read_at(pt_m2)
        pt_q = vec(lambda e: e.tensor_mul(row(8), row(0), row(0)), deps=[pt_mu])
        pt_var = vec(lambda e: e.tensor_sub(row(1), row(1), row(8)),
                     deps=[pt_m2, pt_q], writes=[rows_trk])
        pt_rs1 = rsqrt_row(row(2), row(1), deps=[pt_var], trk=rows_trk)
        dma("scalar", rows_sp[R_RS1:R_RS1 + 1, j * NT:(j + 1) * NT], row(2),
            rtrks=(rows_trk,), wtrks=(dtrk["rows_sp"][j],))
        pt_v = vec(lambda e: e.tensor_scalar_add(row(8), row(1), consts[:, 0:1]),
                   deps=[pt_var])
        pt_r = vec(lambda e: e.reciprocal(row(9), row(8)), deps=[pt_v])
        pt_v2 = vec(lambda e: e.tensor_mul(row(3), row(1), row(9)), deps=[pt_r])
        pt_rs2 = rsqrt_row(row(3), row(3), deps=[pt_v2], trk=rows_trk)
        pt_srow = vec(lambda e: e.tensor_mul(row(3), row(2), row(3)),
                      deps=[pt_rs1, pt_rs2], writes=[rows_trk])
        return pt_mu, pt_rs1, pt_srow

    def mk_apply(j, sstats):
        pt_mu, pt_rs1, pt_srow = sstats
        par = j % 2
        xt, xt_trk = x_t[par], x_t_trk[par]
        state = {"xn_pts": [None] * KC, "xc_pts": [None] * KC, "done": False}

        def step(m):
            if m == 0:
                state["mub"] = bcast(row(0), 0, deps=[pt_mu], rtrks=[rows_trk])
            if m < 4:
                for c in (2 * m, 2 * m + 1):
                    pt_xc = vec(lambda e, c=c, xt=xt: e.tensor_sub(
                        chunk(xc_t, c), chunk(xt, c), psB[0][0:128, 0:NT]),
                        deps=[state["mub"]] if c == 0 else (),
                        reads=[xt_trk] if c == 0 else (),
                        writes=[xc_trk] if c == 0 else ())
                    state["xc_pts"][c] = pt_xc
                    if c == KC - 1:
                        xc_trk.write_at(pt_xc)
                        xt_trk.read_at(pt_xc)
                        psB_trk[0].read_at(pt_xc)
            elif m == 4:
                state["rs1b"] = bcast(row(2), 0, deps=[pt_rs1, state["xc_pts"][7]],
                                      rtrks=[rows_trk])
                state["srb"] = bcast(row(3), 1, deps=[pt_srow], rtrks=[rows_trk])
            if m >= 4:
                for c in (2 * (m - 4), 2 * (m - 4) + 1):
                    pt_t1 = vec(lambda e, c=c: e.tensor_mul(
                        chunk(t1_t, c), chunk(xc_t, c), psB[0][0:128, 0:NT]),
                        deps=([state["rs1b"], state["xc_pts"][c]] if c < 2
                              else [state["xc_pts"][c]]),
                        writes=[t1_trk] if c == 0 else ())
                    pt_rr = vec(lambda e, c=c: e.tensor_mul(
                        chunk(xc_t, c), chunk(xc_t, c), psB[1][0:128, 0:NT]),
                        deps=[state["srb"], pt_t1] if c == 0 else [pt_t1])
                    xn_pt = sca(lambda e, c=c: e.activation(
                        chunk(xn_t, c), chunk(xc_t, c), AF.Identity,
                        scale=bias_col("L0_attn_norm1_w", c),
                        bias=bias_col("L0_attn_norm1_b", c)),
                        deps=[pt_rr],
                        writes=[xn_trk] if c == 0 else ())
                    state["xn_pts"][c] = xn_pt
                    if c == KC - 1:
                        t1_trk.write_at(pt_t1)
                        psB_trk[0].read_at(pt_t1)
                        xc_trk.write_at(pt_rr)
                        psB_trk[1].read_at(pt_rr)
                        xn_trk.write_at(xn_pt)
                        xc_trk.read_at(xn_pt)
                        dma("sync", dram_tile(t1_v, j), sb_tile3(t1_t),
                            rtrks=(t1_trk,), wtrks=(dtrk["t1_sp"][j],))
                        state["done"] = True
        return step, state

    def st_delta(j, apl):
        pt_dm = mm_acc(lambda k: rvec[:, k:k + 1], lambda k: chunk(xn_t, k), KC,
                       psR, psR_trk, kdeps=apl["xn_pts"],
                       rtrks_rec=[xn_trk], rtrks=[rvec_trk], rows=1)
        pt_z = vec(lambda e: e.tensor_scalar_add(rowd(0), psR[0:1, 0:NT],
                                                 mean_bs[:]),
                   deps=[pt_dm, pt_meanb], writes=[rowsd_trk])
        psR_trk.read_at(pt_z)
        pt_az = sca(lambda e: e.activation(rowd(2), rowd(0), AF.Abs),
                    deps=[pt_z])
        pt_ez = sca(lambda e: e.activation(rowd(2), rowd(2), AF.Exp, scale=-1.0),
                    deps=[pt_az])
        pt_lz = sca(lambda e: e.activation(rowd(2), rowd(2), AF.Ln,
                                           bias=consts[:, 1:2]), deps=[pt_ez])
        pt_rz = sca(lambda e: e.activation(rowd(3), rowd(0), AF.Relu),
                    deps=[pt_z])
        pt_delta = vec(lambda e: e.tensor_add(rowd(0), rowd(2), rowd(3)),
                       deps=[pt_lz, pt_rz], writes=[rowsd_trk])
        pt_dA = sca(lambda e: e.activation(rowd(1), rowd(0), AF.Exp, scale=-1.0),
                    deps=[pt_delta])
        pt_P = vec(lambda e: e.tensor_tensor_scan(rowd(3), rowd(1), zrow[:],
                                                  Pcarry[:], ALU.mult, ALU.add),
                   deps=[pt_dA], reads=[Pc_trk], writes=[rowsd_trk])
        pt_Pc = vec(lambda e: e.tensor_copy(Pcarry[:], rowd(3)[:, NT - 1:NT]),
                    deps=[pt_P], writes=[Pc_trk])
        dma("scalar", rows_sp[R_P:R_P + 1, j * NT:(j + 1) * NT], rowd(3),
            rtrks=(rowsd_trk,), wtrks=(dtrk["rows_sp"][j],))
        return pt_delta, pt_dA

    def st_gate(j, apl):
        uin_pts = [None] * KC
        gg_pts = [None] * KC
        for m in range(16):
            pb, ptrk = next_psM()
            pt_g = mm_acc(lambda k, m=m: wslice(gate_w, 2048, k, m),
                          lambda k: chunk(xn_t, k), KC, pb, ptrk,
                          kdeps=apl["xn_pts"], rtrks_rec=[xn_trk],
                          rtrks=[gate_w_trk])
            dstt, dtrk_ = (gate_s, gate_s_trk) if m < 8 else (gg_s, gg_trk)
            pe = sca(lambda e, pb=pb, c=m % 8, dstt=dstt, m=m: e.activation(
                chunk(dstt, c), pb[0:128, 0:NT], AF.Silu,
                bias=bias_col("L0_attn_gate_b", m)),
                deps=[pt_g], writes=[dtrk_])
            ptrk.read_at(pe)
            if m < 8:
                pt_ui = vec(lambda e, c=m: e.tensor_mul(
                    chunk(gate_s, c), chunk(xn_t, c), chunk(gate_s, c)),
                    deps=[pe])
                uin_pts[m] = pt_ui
                if m == 7:
                    gate_s_trk.write_at(pt_ui)
                    xn_trk.read_at(pt_ui)
            else:
                gg_pts[m - 8] = pe
        return uin_pts, gg_pts

    def st_state(j, uin_pts, gg_pts, dpts, apply_step):
        pt_delta, pt_dA = dpts
        pt_db = bcast(rowd(0), 2, deps=[pt_delta], rtrks=[rowsd_trk])
        pt_dab = bcast(rowd(1), 3, deps=[pt_dA], rtrks=[rowsd_trk])
        for m in range(8):
            pbu, ptu = next_psM()
            pt_u = mm_acc(lambda k, m=m: wslice(state_w, 1024, k, m),
                          lambda k: chunk(gate_s, k), KC, pbu, ptu,
                          kdeps=uin_pts, rtrks_rec=[gate_s_trk],
                          rtrks=[state_w_trk])
            pbb, ptb = next_psM()
            pt_bm = mm_acc(lambda k, m=m: wslice(state_w, 1024, k, m),
                           lambda k: chunk(gg_s, k), KC, pbb, ptb,
                           kdeps=gg_pts, rtrks_rec=[gg_trk],
                           rtrks=[state_w_trk])
            pt_s1 = sca(lambda e, pbu=pbu, m=m: e.add(
                s1_c[:], pbu[0:128, 0:NT], bias_col("L0_attn_state_b", m)),
                deps=[pt_u], writes=[s1_trk])
            ptu.read_at(pt_s1)
            pt_s2 = vec(lambda e, pbb=pbb, m=m: e.scalar_tensor_tensor(
                dbu_c[:], pbb[0:128, 0:NT], bias_col("L0_attn_state_b", m),
                s1_c[:], ALU.add, ALU.mult),
                deps=[pt_bm, pt_s1], reads=[s1_trk], writes=[dbu_trk])
            ptb.read_at(pt_s2)
            pt_dbu = vec(lambda e: e.tensor_mul(dbu_c[:], dbu_c[:],
                                                psB[2][0:128, 0:NT]),
                         deps=[pt_db, pt_s2], writes=[dbu_trk])
            psB_trk[2].read_at(pt_dbu)
            pt_sc = vec(lambda e, m=m: e.tensor_tensor_scan(
                chunk(y_t, m), psB[3][0:128, 0:NT], dbu_c[:],
                hstate[:, m:m + 1], ALU.mult, ALU.add),
                deps=[pt_dab, pt_dbu], reads=[hstate_trk, dbu_trk],
                writes=[y_trk])
            psB_trk[3].read_at(pt_sc)
            vec(lambda e, m=m: e.tensor_copy(hstate[:, m:m + 1],
                                             chunk(y_t, m)[:, NT - 1:NT]),
                deps=[pt_sc], reads=[y_trk], writes=[hstate_trk])
            if apply_step is not None:
                apply_step(m)
        dma("scalar", dram_tile(yloc_v, j), sb_tile3(y_t),
            rtrks=(y_trk,), wtrks=(dtrk["yloc_sp"][j],))

    sstats = st_stats(0)
    apply_step, apl = mk_apply(0, sstats)
    for m_ in range(8):
        apply_step(m_)
    dpts = st_delta(0, apl)
    for j in range(NTIL):
        up, gp = st_gate(j, apl)
        if j + 1 < NTIL:
            sstats = st_stats(j + 1)
            apply_next, apl_next = mk_apply(j + 1, sstats)
        else:
            apply_next, apl_next = None, None
        st_state(j, up, gp, dpts, apply_next)
        if apl_next is not None:
            apl = apl_next
            dpts = st_delta(j + 1, apl)

    def emit_collective():
        pr.waits("tensor", psB_trk[0].before_write())
        pr.waits("tensor", Pc_trk.before_read())
        pt_ar = pr.do("tensor", lambda e: e.matmul(
            psB[0][0:128, 0:1], ones_row[:], Pcarry[:], start=True, stop=True),
            inc="TS")
        psB_trk[0].write_at(pt_ar)
        Pc_trk.read_at(pt_ar)
        pm1 = vec(lambda e: e.tensor_copy(msg[:, 0:1], psB[0][0:128, 0:1]),
                  deps=[pt_ar], writes=[msg_trk])
        psB_trk[0].read_at(pm1)
        pm2 = vec(lambda e: e.tensor_copy(msg[:, 1:9], hstate[:]),
                  reads=[hstate_trk], writes=[msg_trk])
        dma("sync", cc_in.rearrange("(p c) -> p c", c=9), msg[:],
            rtrks=(msg_trk,), wtrks=(dtrk["cc_in"],))
        pr.waits("gpsimd", dtrk["cc_in"].before_read())
        pt_cc = pr.do("gpsimd", lambda e: e.collective_compute(
            "AllGather", ALU.bypass, replica_groups=[list(range(8))],
            ins=[cc_in], outs=[cc_out]), inc="CC")
        dtrk["cc_in"].read_at(pt_cc)
        dtrk["cc_out"].write_at(pt_cc)
        dma("scalar", gath[:].rearrange("p (r c) -> p r c", c=9),
            cc_out.rearrange("r (p c) -> p r c", c=9),
            wtrks=(gath_trk,), rtrks=(dtrk["cc_out"],))
        pt_f = pt_hin0
        for jj in range(8):
            pa = vec(lambda e, jj=jj: e.tensor_scalar(
                aeff[:], gath[:, jj * 9:jj * 9 + 1], foldm[:, jj:jj + 1],
                foldm[:, 8 + jj:9 + jj], ALU.mult, ALU.add),
                deps=[pt_f],
                reads=[gath_trk, foldm_trk] if jj == 0 else ())
            ph_ = vec(lambda e, jj=jj: e.tensor_scalar_mul(
                hmsk[:], gath[:, jj * 9 + 1:jj * 9 + 9], foldm[:, jj:jj + 1]),
                deps=[pt_f])
            pt_f = vec(lambda e: e.scalar_tensor_tensor(
                h_in[:], h_in[:], aeff[:], hmsk[:], ALU.mult, ALU.add),
                deps=[pa, ph_, pt_f])
        hin_trk.write_at(pt_f)
        gath_trk.read_at(pt_f)
        if hin_dbg is not None:
            pdbg = vec(lambda e: e.tensor_copy(msg[:, 0:8], h_in[:]), deps=[pt_f],
                       writes=[msg_trk])
            dma("sync", hin_dbg, msg[:].rearrange("p c -> p c") if False else msg[:],
                rtrks=(msg_trk,))

    retire((gate_w, gate_w_trk), (state_w, state_w_trk),
           (x_t[0], x_t_trk[0]), (x_t[1], x_t_trk[1]),
           (x_bt[0], x_bt_trk[0]), (x_bt[1], x_bt_trk[1]),
           (xc_t, xc_trk), (t1_t, t1_trk), (xn_t, xn_trk),
           (gate_s, gate_s_trk), (gg_s, gg_trk),
           (s1_c, s1_trk), (dbu_c, dbu_trk), (y_t, y_trk),
           (rvec_f, rvec_f_trk), (zrow, None),
           (rowst, rows_trk), (rowsd, rowsd_trk))
    ph1.close()

    ph2 = ES()
    sb2 = lambda nm, sh, dt=F32: ph2.enter_context(nc.sbuf_tensor(nm, sh, dt))
    t1b_t = [sb2(f"t1b{i}", [128, KC * NT]) for i in range(2)]
    yb_t = [sb2(f"yb{i}", [128, KC * NT], BF16) for i in range(2)]
    out_w = sb2("out_w", [128, KC * 1024], BF16)
    pr.new_phase()
    t1b_trk = [fresh(t1b_t[0], pr.buf_sem()), fresh(t1b_t[1], pr.buf_sem())]
    yb_trk = [fresh(yb_t[0], pr.buf_sem()), fresh(yb_t[1], pr.buf_sem())]
    out_w_trk = fresh(out_w, pr.w_sem())
    ow_pts = []
    load_weight("L0_attn_out_w", out_w, out_w_trk, 1024, pts=ow_pts)

    def ow_sl(k, m):
        return wslice(out_w, 1024, k, m)
    tt_t = [sb2(f"tt_t{i}", [128, KC * NT]) for i in range(2)]
    tt_trk = [fresh(tt_t[0], pr.buf_sem()), fresh(tt_t[1], pr.buf_sem())]
    tb_t = [sb2(f"tb_t{i}", [128, KC * NT], BF16) for i in range(2)]
    tb_trk = [fresh(tb_t[0]), fresh(tb_t[1])]
    xc2_t = sb2("xc2_t", [128, KC * NT])
    xc2_trk = fresh(xc2_t)
    x2n_t = sb2("x2n_t", [128, KC * NT], BF16)
    x2n_trk = fresh(x2n_t, pr.buf_sem())
    pbc = sb2("pbc", [128, NT])
    pbc_trk = fresh(pbc, pr.buf_sem())
    rows2 = sb2("rows2", [1, 2 * NT])
    rows2_trk = fresh(rows2)

    hin_b = sb2("hin_b", [128, KC], BF16)
    hin_b_trk = fresh(hin_b)
    w_h = sb2("w_h", [128, KC])
    w_h_trk = fresh(w_h)
    def emit_wh():
        pt_hb = vec(lambda e: e.tensor_copy(hin_b[:], h_in[:]),
                    reads=[hin_trk], writes=[hin_b_trk])
        for mc_ in range(KC):
            pb_, ptrk_ = next_psM()
            pt_wh = mm_acc(lambda k, mc_=mc_: ow_sl(k, mc_),
                           lambda k: hin_b[:, k:k + 1], KC, pb_, ptrk_,
                           deps=[pt_hb], kdeps=ow_pts,
                           rtrks_rec=[out_w_trk],
                           rtrks=[hin_b_trk], cols=1)
            pe_ = vec(lambda e, pb_=pb_, mc_=mc_: e.tensor_copy(
                w_h[:, mc_:mc_ + 1], pb_[0:128, 0:1]),
                deps=[pt_wh], writes=[w_h_trk] if mc_ == 0 else ())
            w_h_trk.write_at(pe_)
            ptrk_.read_at(pe_)

    def ph2_head(j, gen=None):
        par = j % 2
        t1t, t1trk = t1b_t[par], t1b_trk[par]
        yt, ytrk = yb_t[par], yb_trk[par]
        tt, tttrk = tt_t[par], tt_trk[par]
        dma("sync", sb_tile3(t1t), dram_tile(t1_v, j), wtrks=(t1trk,),
            rtrks=(dtrk["t1_sp"][j],))
        dma("sync", sb_tile3(yt), dram_tile(yloc_v, j), wtrks=(ytrk,),
            rtrks=(dtrk["yloc_sp"][j],))
        for m in range(8):
            pb, ptrk = next_psM()
            pt_s = mm_acc(lambda k, m=m: ow_sl(k, m),
                          lambda k, yt=yt: chunk(yt, k), KC, pb, ptrk,
                          kdeps=ow_pts,
                          rtrks_rec=[out_w_trk],
                          rtrks=[ytrk])
            pt_t = vec(lambda e, pb=pb, m=m, t1t=t1t, tt=tt:
                       e.scalar_tensor_tensor(
                chunk(tt, m), pb[0:128, 0:NT], bias_col("L0_attn_out_b", m),
                chunk(t1t, m), ALU.add, ALU.add),
                deps=[pt_s], reads=[t1trk] if m == 0 else (),
                writes=[tttrk] if m == 0 else ())
            ptrk.read_at(pt_t)
            tttrk.write_at(pt_t)
            t1trk.read_at(pt_t)
            if gen is not None:
                next(gen, None)

    def ph2_tail(j):
        par = j % 2
        tt, tttrk = tt_t[par], tt_trk[par]
        tb, tbtrk = tb_t[par], tb_trk[par]
        psrc = bass.AP(rows_sp.tensor, R_P * T + j * NT, [[0, 128], [1, NT]])
        pt_pb = dma("sync", pbc[:], psrc, wtrks=(pbc_trk,),
                    rtrks=(dtrk["rows_sp"][j],))
        pt_t2 = None
        for m in range(8):
            pt_t2 = vec(lambda e, m=m, tt=tt: e.scalar_tensor_tensor(
                chunk(tt, m), pbc[0:128, 0:NT], w_h[:, m:m + 1],
                chunk(tt, m), ALU.mult, ALU.add),
                deps=[pt_pb] if m == 0 else (),
                reads=([w_h_trk, tttrk, pbc_trk] if m == 0 else ()),
                writes=[tttrk] if m == 0 else ())
            tttrk.write_at(pt_t2)
        pbc_trk.read_at(pt_t2)
        yield
        dma("sync", dram_tile(t_v, j), sb_tile3(tt), rtrks=(tttrk,),
            wtrks=(dtrk["t_sp"][j],))
        c_sc = c_ve = None
        for m in range(KC):
            if m % 2 == 0:
                c_sc = sca(lambda e, m=m, tb=tb, tt=tt:
                           e.copy(chunk(tb, m), chunk(tt, m)),
                           reads=[tttrk] if m == 0 else (),
                           writes=[tbtrk] if m == 0 else ())
            else:
                c_ve = vec(lambda e, m=m, tb=tb, tt=tt:
                           e.tensor_copy(chunk(tb, m), chunk(tt, m)),
                           reads=[tttrk] if m == 1 else (),
                           writes=[tbtrk] if m == 1 else ())
        tbtrk.write_at(c_sc)
        tbtrk.write_at(c_ve)
        tttrk.read_at(c_sc)
        tttrk.read_at(c_ve)
        yield
        do_b = ln_stats1(tb, tbtrk, rows2[:, 0:NT],
                         rows2[:, NT:2 * NT], rows2_trk, 1, 2,
                         deps=[c_sc, c_ve], defer=True)
        yield
        pt_mub, pt_rsb = do_b()
        pt_xc = None
        for c in range(KC):
            pt_xc = vec(lambda e, c=c, tt=tt: e.tensor_sub(
                chunk(xc2_t, c), chunk(tt, c), psB[1][0:128, 0:NT]),
                deps=[pt_mub] if c == 0 else (),
                reads=[tttrk] if c == 0 else (),
                writes=[xc2_trk] if c == 0 else ())
        xc2_trk.write_at(pt_xc)
        tttrk.read_at(pt_xc)
        psB_trk[1].read_at(pt_xc)
        yield
        pt_r2 = None
        for c in range(KC):
            pt_r2 = vec(lambda e, c=c: e.tensor_mul(chunk(xc2_t, c),
                                                    chunk(xc2_t, c),
                                                    psB[2][0:128, 0:NT]),
                        deps=[pt_rsb, pt_xc] if c == 0 else ())
        xc2_trk.write_at(pt_r2)
        psB_trk[2].read_at(pt_r2)
        yield
        xn2_pt = None
        for c in range(KC):
            xn2_pt = sca(lambda e, c=c: e.activation(
                chunk(x2n_t, c), chunk(xc2_t, c), AF.Identity,
                scale=bias_col("L0_attn_norm2_w", c),
                bias=bias_col("L0_attn_norm2_b", c)),
                deps=[pt_r2] if c == 0 else (),
                writes=[x2n_trk] if c == 0 else ())
        x2n_trk.write_at(xn2_pt)
        xc2_trk.read_at(xn2_pt)
        dma("scalar", dram_tile(x2n_v, j), sb_tile3(x2n_t), rtrks=(x2n_trk,),
            wtrks=(dtrk["x2n_sp"][j],))
        yield

    for j in range(NTIL):
        g2 = ph2_tail(j - 1) if j > 0 else None
        ph2_head(j, g2)
        if j == 0:
            emit_collective()
            emit_wh()
        if g2 is not None:
            for _ in g2:
                pass
    for _ in ph2_tail(NTIL - 1):
        pass

    retire((out_w, out_w_trk), (t1b_t[0], t1b_trk[0]),
           (t1b_t[1], t1b_trk[1]), (yb_t[0], yb_trk[0]), (yb_t[1], yb_trk[1]),
           (tt_t[0], tt_trk[0]), (tt_t[1], tt_trk[1]), (tb_t[0], tb_trk[0]),
           (tb_t[1], tb_trk[1]), (xc2_t, xc2_trk), (x2n_t, x2n_trk),
           (pbc, pbc_trk), (rows2, rows2_trk), (hin_b, hin_b_trk),
           (w_h, w_h_trk))
    ph2.close()
    retire((hstate, hstate_trk), (h_in, hin_trk), (Pcarry, Pc_trk),
           (foldm, foldm_trk), (mean_bs, meanb_trk), (rvec, rvec_trk),
           (gath, gath_trk), (msg, msg_trk), (aeff, None), (hmsk, None))
    ph12.close()

    ph3 = ES()
    sb3 = lambda nm, sh, dt=F32: ph3.enter_context(nc.sbuf_tensor(nm, sh, dt))
    x2nb_t = [sb3(f"x2nb{i}", [128, KC * NT], BF16) for i in range(2)]
    w1 = sb3("mlp_w1", [128, KC * 4096], BF16)
    w2 = sb3("mlp_w2", [128, 32 * 1024], BF16)
    pr.new_phase()
    x2nb_trk = [fresh(x2nb_t[0], pr.buf_sem()), fresh(x2nb_t[1], pr.buf_sem())]
    w1_trk, w2_trk = fresh(w1, pr.w_sem()), fresh(w2, pr.w_sem())
    w1_pts = []
    load_weight("L0_attn_mlp_w1", w1, w1_trk, 4096, pts=w1_pts)
    pw2 = load_weight("L0_attn_mlp_w2", w2, w2_trk, 1024)

    def w1_sl(k, mc):
        return wslice(w1, 4096, k, mc)
    h_t = sb3("h_t", [128, 32 * NT], BF16)
    h_trk = fresh(h_t)
    sc_c = [sb3(f"sc_c{i}", [128, NT]) for i in range(2)]
    sc_trk = [fresh(sc_c[0], pr.buf_sem()), fresh(sc_c[1], pr.buf_sem())]
    ao_t = sb3("ao_t", [128, KC * NT])
    ao_trk = fresh(ao_t, pr.buf_sem())
    rows3 = sb3("rows3", [1, 2 * NT])
    rows3_trk = fresh(rows3)

    def row3(i):
        return rows3[:, i * NT:(i + 1) * NT]

    def ph3_headA(j, gen=None):
        par = j % 2
        xb, xbtrk = x2nb_t[par], x2nb_trk[par]
        dma("sync", sb_tile3(xb), dram_tile(x2n_v, j), wtrks=(xbtrk,),
            rtrks=(dtrk["x2n_sp"][j],))
        h_pts = [None] * 32
        for mc in range(32):
            pb, ptrk = next_psM()
            pt_h = mm_acc(lambda k, mc=mc: w1_sl(k, mc),
                          lambda k, xb=xb: chunk(xb, k), KC, pb, ptrk,
                          kdeps=w1_pts, rtrks_rec=[w1_trk],
                          rtrks=[xbtrk])
            pe = sca(lambda e, pb=pb, mc=mc: e.activation(
                chunk(h_t, mc), pb[0:128, 0:NT], AF.Gelu,
                bias=bias_col("L0_attn_mlp_b1", mc)),
                deps=[pt_h], writes=[h_trk] if mc == 0 else ())
            h_trk.write_at(pe)
            ptrk.read_at(pe)
            h_pts[mc] = pe
            if gen is not None and mc % 3 == 2:
                next(gen, None)
        return h_pts

    def ph3_headB(j, h_pts):
        for m in range(8):
            pb, ptrk = next_psM()
            pt_o = mm_acc(lambda k, m=m: wslice(w2, 1024, k, m),
                          lambda k: chunk(h_t, k), 32, pb, ptrk,
                          deps=[pw2], kdeps=h_pts, rtrks_rec=[h_trk],
                          rtrks=[w2_trk])
            sc, sct = sc_c[m % 2], sc_trk[m % 2]
            dma("sync", sc[:], t_v[:, m, j * NT:(j + 1) * NT],
                wtrks=(sct,), rtrks=(dtrk["t_sp"][j],))
            pt_a = vec(lambda e, pb=pb, m=m, sc=sc: e.scalar_tensor_tensor(
                chunk(ao_t, m), pb[0:128, 0:NT], bias_col("L0_attn_mlp_b2", m),
                sc[:], ALU.add, ALU.add),
                deps=[pt_o], reads=[sct],
                writes=[ao_trk] if m == 0 else ())
            ptrk.read_at(pt_a)
            ao_trk.write_at(pt_a)

    def ph3_tailprod(j):
        par = j % 2
        stg, stgtrk = x2nb_t[par], x2nb_trk[par]
        c_sc = c_ve = None
        for c in range(KC):
            if c % 2 == 0:
                c_sc = sca(lambda e, c=c, stg=stg: e.copy(chunk(stg, c),
                                                          chunk(ao_t, c)),
                           reads=[ao_trk] if c == 0 else (),
                           writes=[stgtrk] if c == 0 else ())
            else:
                c_ve = vec(lambda e, c=c, stg=stg: e.tensor_copy(
                    chunk(stg, c), chunk(ao_t, c)),
                    reads=[ao_trk] if c == 1 else (),
                    writes=[stgtrk] if c == 1 else ())
        stgtrk.write_at(c_sc)
        stgtrk.write_at(c_ve)
        ao_trk.read_at(c_sc)
        ao_trk.read_at(c_ve)
        yield
        do_b = ln_stats1(stg, stgtrk, row3(0), row3(1),
                         rows3_trk, 0, 1, deps=[c_sc, c_ve], defer=True)
        yield
        pt_mub, pt_rsb = do_b()
        pt_xc = None
        for c in range(KC):
            pt_xc = vec(lambda e, c=c: e.tensor_sub(chunk(ao_t, c),
                                                    chunk(ao_t, c),
                                                    psB[0][0:128, 0:NT]),
                        deps=[pt_mub] if c == 0 else (),
                        writes=[ao_trk] if c == 0 else ())
        ao_trk.write_at(pt_xc)
        psB_trk[0].read_at(pt_xc)
        yield
        pt_x1 = None
        for c in range(KC):
            pt_m = vec(lambda e, c=c: e.tensor_mul(chunk(ao_t, c),
                                                   chunk(ao_t, c),
                                                   psB[1][0:128, 0:NT]),
                       deps=[pt_rsb, pt_xc] if c == 0 else ())
            sc, sct = sc_c[c % 2], sc_trk[c % 2]
            dma("sync", sc[:], x_v[:, c, j * NT:(j + 1) * NT],
                wtrks=(sct,))
            pt_x1 = vec(lambda e, c=c, sc=sc: e.tensor_add(chunk(ao_t, c),
                                                           chunk(ao_t, c),
                                                           sc[:]),
                        deps=[pt_m], reads=[sct])
        ao_trk.write_at(pt_x1)
        psB_trk[1].read_at(pt_x1)
        dma("sync", dram_tile(x1_v, j), sb_tile3(ao_t), rtrks=(ao_trk,),
            wtrks=(dtrk["x1_sp"][j],))
        yield
        c_sc = c_ve = None
        for c in range(KC):
            if c % 2 == 0:
                c_sc = sca(lambda e, c=c, stg=stg: e.copy(chunk(stg, c),
                                                          chunk(ao_t, c)),
                           reads=[ao_trk] if c == 0 else (),
                           writes=[stgtrk] if c == 0 else ())
            else:
                c_ve = vec(lambda e, c=c, stg=stg: e.tensor_copy(
                    chunk(stg, c), chunk(ao_t, c)),
                    reads=[ao_trk] if c == 1 else (),
                    writes=[stgtrk] if c == 1 else ())
        stgtrk.write_at(c_sc)
        stgtrk.write_at(c_ve)
        ao_trk.read_at(c_sc)
        ao_trk.read_at(c_ve)
        yield
        do_b = ln_stats1(stg, stgtrk, row3(0), row3(1),
                         rows3_trk, 0, 1, deps=[c_sc, c_ve], defer=True)
        yield
        pt_mub, pt_rsb = do_b()
        pt_xc = None
        for c in range(KC):
            pt_xc = vec(lambda e, c=c: e.tensor_sub(chunk(ao_t, c),
                                                    chunk(ao_t, c),
                                                    psB[0][0:128, 0:NT]),
                        deps=[pt_mub] if c == 0 else (),
                        writes=[ao_trk] if c == 0 else ())
        ao_trk.write_at(pt_xc)
        psB_trk[0].read_at(pt_xc)
        yield
        pt_n = None
        for c in range(KC):
            pt_n = vec(lambda e, c=c, stg=stg: e.tensor_mul(
                chunk(stg, c), chunk(ao_t, c), psB[1][0:128, 0:NT]),
                deps=[pt_rsb, pt_xc] if c == 0 else (),
                writes=[stgtrk] if c == 0 else ())
        stgtrk.write_at(pt_n)
        ao_trk.read_at(pt_n)
        psB_trk[1].read_at(pt_n)
        dma("scalar", dram_tile(x1n_v, j), sb_tile3(stg), rtrks=(stgtrk,),
            wtrks=(dtrk["x1n_sp"][j],))
        yield

    gen3 = None
    for j in range(NTIL):
        gen3 = ph3_tailprod(j - 1) if j > 0 else None
        h_pts = ph3_headA(j, gen3)
        if gen3 is not None:
            for _ in gen3:
                pass
        ph3_headB(j, h_pts)
    for _ in ph3_tailprod(NTIL - 1):
        pass

    retire((w1, w1_trk), (w2, w2_trk), (x2nb_t[0], x2nb_trk[0]),
           (x2nb_t[1], x2nb_trk[1]), (h_t, h_trk), (sc_c[0], sc_trk[0]), (sc_c[1], sc_trk[1]),
           (ao_t, ao_trk), (rows3, rows3_trk))
    ph3.close()

    aold_c = [sb(f"aold_c{i}", [128, NT]) for i in range(2)]
    aold_trk = [fresh(aold_c[0], pr.buf_sem()), fresh(aold_c[1], pr.buf_sem())]
    wbc = [sb(f"wbc{i}", [128, NT]) for i in range(2)]
    wbct = [fresh(wbc[0], pr.buf_sem()), fresh(wbc[1], pr.buf_sem())]

    def wrow_dma_bcast(par, ridx, j, q="sync"):
        src = bass.AP(rows_sp.tensor, ridx * T + j * NT, [[0, 128], [1, NT]])
        return dma(q, wbc[par][:], src, wtrks=(wbct[par],),
                   rtrks=(dtrk["rows_sp"][j],))

    def acc_evac_rmw(pb, ptrk, m, j, bias_name, pt_wb, src_acc,
                     dst_t, dst_trk, pt_o=None):
        ac, at = aold_c[m % 2], aold_trk[m % 2]
        wb, wbt = wbc[j % 2], wbct[j % 2]
        dma("sync", ac[:], acc_v[src_acc][:, m, j * NT:(j + 1) * NT],
            wtrks=(at,), rtrks=(dtrk[f"acc{src_acc}_sp"][j],))
        pt_t = vec(lambda e, pb=pb, m=m, wb=wb: e.scalar_tensor_tensor(
            chunk(dst_t, m), pb[0:128, 0:NT], bias_col(bias_name, m),
            wb[0:128, 0:NT], ALU.add, ALU.mult),
            deps=[pt_o, pt_wb], reads=[wbt],
            writes=[dst_trk] if m == 0 else ())
        dst_trk.write_at(pt_t)
        ptrk.read_at(pt_t)
        pt_a = vec(lambda e, m=m, ac=ac: e.tensor_add(
            chunk(dst_t, m), chunk(dst_t, m), ac[:]),
            deps=[pt_t], reads=[at])
        dst_trk.write_at(pt_a)
        return pt_a

    def ph_e0(half):
        ph = ES()
        sbp = lambda nm, sh, dt=F32: ph.enter_context(
            nc.sbuf_tensor(f"{nm}_{half}", sh, dt))
        lo, hi = (0, 2048) if half == 0 else (2048, 4096)
        xn8_t = [sbp(f"e0x{i}", [128, KC * NT], BF16) for i in range(2)]
        if half == 0:
            rw = sbp("router_w", [128, KC * 4], BF16)
            ident4 = sbp("ident4", [128, 128])
            lsb = sbp("lsb", [4, NT])
            lt = sbp("lt", [128, 16])
            gbuf = sbp("gbuf", [128, 64])
            wtok = sbp("wtok", [128, 12])
            wrows = sbp("wrows", [4, NT])
        w1h = sbp("sw1h", [128, KC * 2048], BF16)
        w3h = sbp("sw3h", [128, KC * 2048], BF16)
        w2h = sbp("sw2h", [128, 16 * 1024], BF16)
        pr.new_phase()
        xn8_trk = [fresh(xn8_t[0], pr.buf_sem()),
                   fresh(xn8_t[1], pr.buf_sem())]
        if half == 0:
            rwt = fresh(rw, pr.w_sem())
            pwr = load_weight("L0_ffn_router_w", rw, rwt, 4)
            ident4_trk = fresh(ident4, pr.buf_sem())
            dma("sync", ident4[:], ident_d, wtrks=(ident4_trk,))
            lsb_trk = fresh(lsb)
            lt_trk = fresh(lt)
            gbuf_trk = fresh(gbuf)
            wtok_trk = fresh(wtok)
            wrows_trk = fresh(wrows, pr.buf_sem())
        w1t, w3t, w2t = (fresh(w1h, pr.w_sem()), fresh(w3h, pr.w_sem()),
                         fresh(w2h, pr.w_sem()))
        pww1 = load_weight("L0_ffn_sw_w1", w1h, w1t, 2048, src_lo=lo, src_hi=hi)
        pww3 = load_weight("L0_ffn_sw_w3", w3h, w3t, 2048, src_lo=lo, src_hi=hi)
        pww2 = load_weight("L0_ffn_sw_w2", w2h, w2t, 1024,
                           krange=range(lo // 128, hi // 128))
        h0s = sbp("h0s", [128, 16 * NT], BF16)
        h0s_trk = fresh(h0s)
        acc_t = sbp("e0acc", [128, KC * NT])
        acc_trk = fresh(acc_t, pr.buf_sem())

        def e0_router(j, xn8, xn8t, xn_pts=None):
            pt_l = mm_acc(lambda k: rw[:, k * 4:(k + 1) * 4],
                          lambda k, xn8=xn8: chunk(xn8, k), KC, psR, psR_trk,
                          deps=[pwr], kdeps=xn_pts, rtrks_rec=[xn8t],
                          rtrks=[rwt], rows=4)
            pt_le = sca(lambda e: e.activation(lsb[:], psR[0:4, 0:NT],
                                               AF.Identity, bias=biasrl[:]),
                        deps=[pt_l], reads=[biasrl_trk],
                        writes=[lsb_trk])
            psR_trk.read_at(pt_le)
            pr.waits("tensor", psB_trk[2].before_write())
            pr.waits("tensor", ident4_trk.before_read())
            pr.waits("tensor", lsb_trk.before_read())
            pt_tr = None
            for g in range(4):
                pt_tr = pr.do("tensor", lambda e, g=g: e.transpose(
                    psB[2][0:128, g * 4:(g + 1) * 4],
                    lsb[0:4, g * 128:(g + 1) * 128], ident4[0:4, 0:4]),
                    inc="TS" if g == 3 else None)
            psB_trk[2].write_at(pt_tr)
            lsb_trk.read_at(pt_tr)
            ident4_trk.read_at(pt_tr)
            pt_lt = vec(lambda e: e.tensor_copy(lt[:], psB[2][0:128, 0:16]),
                        deps=[pt_tr], writes=[lt_trk])
            psB_trk[2].read_at(pt_lt)
            lt3 = lt[:].rearrange("p (g e) -> p g e", e=4)
            pairs = [(0, 1), (0, 2), (0, 3), (1, 2), (1, 3), (2, 3)]
            cmp_pts = {}
            for pi, (i, jd) in enumerate(pairs):
                vec(lambda e, pi=pi, i=i, jd=jd: e.tensor_tensor(
                    gbuf[:, pi * 4:(pi + 1) * 4], lt3[:, :, i], lt3[:, :, jd],
                    ALU.is_ge),
                    deps=[pt_lt],
                    writes=[gbuf_trk] if pi == 0 else ())
                cmp_pts[pi] = vec(lambda e, pi=pi, i=i, jd=jd: e.tensor_tensor(
                    gbuf[:, 24 + pi * 4:24 + (pi + 1) * 4],
                    lt3[:, :, i], lt3[:, :, jd], ALU.is_lt),
                    deps=[pt_lt])
            gbuf_trk.write_at(cmp_pts[5])
            seen = set()
            pt_rank = cmp_pts[5]
            for pi, (i, jd) in enumerate(pairs):
                for tgt, src_col in ((jd, pi * 4), (i, 24 + pi * 4)):
                    col = 48 + tgt * 4
                    if tgt not in seen:
                        seen.add(tgt)
                        pt_rank = vec(lambda e, col=col, sc=src_col:
                                      e.tensor_copy(
                            gbuf[:, col:col + 4], gbuf[:, sc:sc + 4]),
                            deps=[cmp_pts[pi], pt_rank])
                    else:
                        pt_rank = vec(lambda e, col=col, sc=src_col:
                                      e.tensor_add(
                            gbuf[:, col:col + 4], gbuf[:, col:col + 4],
                            gbuf[:, sc:sc + 4]),
                            deps=[cmp_pts[pi], pt_rank])
            gbuf_trk.write_at(pt_rank)
            pt_msk = vec(lambda e: e.tensor_single_scalar(
                gbuf[:, 48:64], gbuf[:, 48:64], 1.5, ALU.is_le),
                deps=[pt_rank], writes=[gbuf_trk])
            pt_mx = vec(lambda e: e.reduce_max(wtok[:, 0:4], lt3, axis=AX.X),
                        deps=[pt_lt], writes=[wtok_trk])
            mx_b = bass.AP(wtok, 0, [[12, 128], [1, 4], [0, 4]])
            pt_sub = vec(lambda e: e.tensor_sub(lt3, lt3, mx_b),
                         deps=[pt_mx], reads=[wtok_trk], writes=[lt_trk])
            pt_ex = sca(lambda e: e.activation(lt[:], lt[:], AF.Exp),
                        deps=[pt_sub], writes=[lt_trk])
            msk_b = bass.AP(gbuf, 48, [[64, 128], [1, 4], [4, 4]])
            pt_em = vec(lambda e: e.tensor_mul(lt3, lt3, msk_b),
                        deps=[pt_ex, pt_msk], writes=[lt_trk])
            gbuf_trk.read_at(pt_em)
            pt_s = vec(lambda e: e.reduce_sum(wtok[:, 4:8], lt3, axis=AX.X),
                       deps=[pt_em], writes=[wtok_trk])
            pt_rc = vec(lambda e: e.reciprocal(wtok[:, 8:12], wtok[:, 4:8]),
                        deps=[pt_s], writes=[wtok_trk])
            rc_b = bass.AP(wtok, 8, [[12, 128], [1, 4], [0, 4]])
            pt_w = vec(lambda e: e.tensor_mul(lt3, lt3, rc_b),
                       deps=[pt_rc], writes=[lt_trk])
            wtok_trk.read_at(pt_w)
            pr.waits("tensor", psB_trk[3].before_write())
            pr.waits("tensor", lt_trk.before_read())
            pt_tb = None
            for g in range(4):
                pt_tb = pr.do("tensor", lambda e, g=g: e.transpose(
                    psB[3][0:4, g * 128:(g + 1) * 128],
                    lt[:, g * 4:(g + 1) * 4], ident4[:]),
                    inc="TS" if g == 3 else None)
            psB_trk[3].write_at(pt_tb)
            lt_trk.read_at(pt_tb)
            ident4_trk.read_at(pt_tb)
            pt_wr = vec(lambda e: e.tensor_copy(wrows[:], psB[3][0:4, 0:NT]),
                        deps=[pt_tb], writes=[wrows_trk])
            psB_trk[3].read_at(pt_wr)
            for ei in range(4):
                dma("scalar", rows_sp[R_W0 + ei:R_W0 + ei + 1,
                                      j * NT:(j + 1) * NT],
                    wrows[ei:ei + 1, :], rtrks=(wrows_trk,),
                    wtrks=(dtrk["rows_sp"][j],))
            return wrow_dma_bcast(j % 2, R_W0, j)

        def e0_gemms(j, pt_w0b, xn8, xn8t, xn_pts=None):
            h0_pts = [None] * 16
            for mc in range(16):
                pb1, pt1 = next_psM()
                pm1 = mm_acc(lambda k, mc=mc: wslice(w1h, 2048, k, mc),
                             lambda k, xn8=xn8: chunk(xn8, k), KC, pb1, pt1,
                             deps=[pww1], kdeps=xn_pts, rtrks_rec=[xn8t],
                             rtrks=[w1t])
                pb2, pt2 = next_psM()
                pm2 = mm_acc(lambda k, mc=mc: wslice(w3h, 2048, k, mc),
                             lambda k, xn8=xn8: chunk(xn8, k), KC, pb2, pt2,
                             deps=[pww3], kdeps=xn_pts, rtrks_rec=[xn8t],
                             rtrks=[w3t])
                ps_ = sca(lambda e, pb1=pb1, mc=mc: e.activation(
                    chunk(h0s, mc), pb1[0:128, 0:NT], AF.Silu),
                    deps=[pm1], writes=[h0s_trk] if mc == 0 else ())
                h0s_trk.write_at(ps_)
                pt1.read_at(ps_)
                pv_ = vec(lambda e, pb2=pb2, mc=mc: e.tensor_mul(
                    chunk(h0s, mc), chunk(h0s, mc), pb2[0:128, 0:NT]),
                    deps=[pm2, ps_], writes=[h0s_trk])
                pt2.read_at(pv_)
                h0_pts[mc] = pv_
            wb, wbt = wbc[j % 2], wbct[j % 2]
            for m in range(8):
                pb, ptrk = next_psM()
                pt_o = mm_acc(lambda k, m=m: wslice(w2h, 1024, k, m),
                              lambda k: chunk(h0s, k), 16, pb, ptrk,
                              deps=[pww2], kdeps=h0_pts, rtrks_rec=[h0s_trk],
                              rtrks=[w2t])
                if half == 0:
                    pt_a = vec(lambda e, m=m, pb=pb, wb=wb: e.tensor_mul(
                        chunk(acc_t, m), pb[0:128, 0:NT], wb[0:128, 0:NT]),
                        deps=[pt_o, pt_w0b], reads=[wbt],
                        writes=[acc_trk] if m == 0 else ())
                    acc_trk.write_at(pt_a)
                    ptrk.read_at(pt_a)
                else:
                    ac, at = aold_c[m % 2], aold_trk[m % 2]
                    dma("sync", ac[:], acc_v[0][:, m, j * NT:(j + 1) * NT],
                        wtrks=(at,), rtrks=(dtrk["acc0_sp"][j],))
                    pt_t = vec(lambda e, m=m, pb=pb, wb=wb: e.tensor_mul(
                        chunk(acc_t, m), pb[0:128, 0:NT], wb[0:128, 0:NT]),
                        deps=[pt_o, pt_w0b], reads=[wbt],
                        writes=[acc_trk] if m == 0 else ())
                    acc_trk.write_at(pt_t)
                    ptrk.read_at(pt_t)
                    pt_a = vec(lambda e, m=m, ac=ac: e.tensor_add(
                        chunk(acc_t, m), chunk(acc_t, m), ac[:]),
                        deps=[pt_t], reads=[at])
                    acc_trk.write_at(pt_a)
            dma("scalar", dram_tile(acc_v[0 if half == 0 else 1], j),
                sb_tile3(acc_t), rtrks=(acc_trk,),
                wtrks=(dtrk[f"acc{0 if half == 0 else 1}_sp"][j],))

        for j in range(NTIL):
            par = j % 2
            xn8, xn8t = xn8_t[par], xn8_trk[par]
            dma("sync", sb_tile3(xn8), dram_tile(x1n_v, j), wtrks=(xn8t,),
                rtrks=(dtrk["x1n_sp"][j],))
            if half == 0:
                pt_w0b = e0_router(j, xn8, xn8t)
            else:
                pt_w0b = wrow_dma_bcast(j % 2, R_W0, j)
            e0_gemms(j, pt_w0b, xn8, xn8t)
        pairs = [(w1h, w1t), (w3h, w3t), (w2h, w2t), (h0s, h0s_trk),
                 (acc_t, acc_trk),
                 (xn8_t[0], xn8_trk[0]), (xn8_t[1], xn8_trk[1])]
        if half == 0:
            pairs += [(rw, rwt), (ident4, ident4_trk), (lsb, lsb_trk),
                      (lt, lt_trk), (gbuf, gbuf_trk), (wtok, wtok_trk),
                      (wrows, wrows_trk)]
        retire(*pairs)
        ph.close()

    ph_e0(0)
    ph_e0(1)

    def proj8(wsb, wtrk, pw, src_t, src_trk, dcols=1024, nk=KC, evac=None,
              kdeps=None, gen=None):
        pts = []
        for m in range(8):
            pb, ptrk = next_psM()
            if kdeps is None:
                pt_o = mm_acc(lambda k, m=m: wslice(wsb, dcols, k, m),
                              lambda k: chunk(src_t, k), nk, pb, ptrk,
                              deps=[pw], rtrks=[src_trk, wtrk])
            else:
                pt_o = mm_acc(lambda k, m=m: wslice(wsb, dcols, k, m),
                              lambda k: chunk(src_t, k), nk, pb, ptrk,
                              deps=[pw], kdeps=kdeps, rtrks_rec=[src_trk],
                              rtrks=[wtrk])
            pts.append(evac(m, pb, ptrk, pt_o))
            if gen is not None:
                next(gen, None)
        return pts

    ph5 = ES()
    sb5 = lambda nm, sh, dt=F32: ph5.enter_context(nc.sbuf_tensor(nm, sh, dt))
    eqw = sb5("eqw", [128, KC * 1024], BF16)
    mwv = sb5("me_wv", [128, KC * 1024], BF16)
    mwo = sb5("me_wo", [128, KC * 1024], BF16)
    c1w = sb5("c1w", [128, KC * 2048], BF16)
    c2w = sb5("c2w", [128, 16 * 1024], BF16)
    pr.new_phase()
    t5 = [fresh(w_, pr.w_sem()) for w_ in (eqw, mwv, mwo, c1w, c2w)]
    pw5 = [load_weight("L0_ffn_me_eq_w", eqw, t5[0], 1024),
           load_weight("L0_ffn_me_wv", mwv, t5[1], 1024),
           load_weight("L0_ffn_me_wo", mwo, t5[2], 1024),
           load_weight("L0_ffn_me_c1w", c1w, t5[3], 2048),
           load_weight("L0_ffn_me_c2w", c2w, t5[4], 1024)]
    x5 = sb5("x5", [128, KC * NT], BF16)
    x5_trk = fresh(x5, pr.buf_sem())
    eq_t = sb5("eq_t", [128, KC * NT], BF16)
    eq_trk = fresh(eq_t)
    v_t = sb5("v_t", [128, KC * NT], BF16)
    v_trk = fresh(v_t)
    sym_t = sb5("sym_t", [128, KC * NT], BF16)
    sym_trk = fresh(sym_t)
    h1_t = sb5("h1_t", [128, 16 * NT], BF16)
    h1_trk = fresh(h1_t)
    acc5_t = sb5("acc5_t", [128, KC * NT])
    acc5_trk = fresh(acc5_t, pr.buf_sem())
    for j in range(NTIL):
        dma("sync", sb_tile3(x5), dram_tile(x1n_v, j), wtrks=(x5_trk,),
            rtrks=(dtrk["x1n_sp"][j],))
        pt_wb = wrow_dma_bcast(j % 2, R_W1, j)

        def evac_act(dst_t, dst_trk, bias_name, func=AF.Identity):
            def f(m, pb, ptrk, pt_o):
                pe = sca(lambda e, pb=pb, m=m: e.activation(
                    chunk(dst_t, m), pb[0:128, 0:NT], func,
                    bias=bias_col(bias_name, m)),
                    deps=[pt_o], writes=[dst_trk] if m == 0 else ())
                dst_trk.write_at(pe)
                ptrk.read_at(pe)
                return pe
            return f

        p_eq = proj8(eqw, t5[0], pw5[0], x5, x5_trk,
                     evac=evac_act(eq_t, eq_trk, "L0_ffn_me_eq_b"))
        p_v = proj8(mwv, t5[1], pw5[1], eq_t, eq_trk, kdeps=p_eq,
                    evac=evac_act(v_t, v_trk, "L0_ffn_me_bv"))
        p_sym = proj8(mwo, t5[2], pw5[2], v_t, v_trk, kdeps=p_v,
                      evac=evac_act(sym_t, sym_trk, "L0_ffn_me_bo"))
        h1_pts = [None] * 16
        for mc in range(16):
            pb, ptrk = next_psM()
            pt_h = mm_acc(lambda k, mc=mc: wslice(c1w, 2048, k, mc),
                          lambda k: chunk(sym_t, k), KC, pb, ptrk,
                          deps=[pw5[3]], kdeps=p_sym, rtrks_rec=[sym_trk],
                          rtrks=[t5[3]])
            pe = sca(lambda e, pb=pb, mc=mc: e.activation(
                chunk(h1_t, mc), pb[0:128, 0:NT], AF.Gelu,
                bias=bias_col("L0_ffn_me_c1b", mc)),
                deps=[pt_h], writes=[h1_trk] if mc == 0 else ())
            h1_trk.write_at(pe)
            ptrk.read_at(pe)
            h1_pts[mc] = pe
        for m in range(8):
            pb, ptrk = next_psM()
            pt_o = mm_acc(lambda k, m=m: wslice(c2w, 1024, k, m),
                          lambda k: chunk(h1_t, k), 16, pb, ptrk,
                          deps=[pw5[4]], kdeps=h1_pts, rtrks_rec=[h1_trk],
                          rtrks=[t5[4]])
            acc_evac_rmw(pb, ptrk, m, j, "L0_ffn_me_c2b", pt_wb, 1,
                         acc5_t, acc5_trk, pt_o=pt_o)
        dma("scalar", dram_tile(acc_v[0], j), sb_tile3(acc5_t),
            rtrks=(acc5_trk,), wtrks=(dtrk["acc0_sp"][j],))

    retire((eqw, t5[0]), (mwv, t5[1]), (mwo, t5[2]), (c1w, t5[3]),
           (c2w, t5[4]), (x5, x5_trk), (eq_t, eq_trk), (v_t, v_trk),
           (sym_t, sym_trk), (h1_t, h1_trk), (acc5_t, acc5_trk))
    ph5.close()

    ph6 = ES()
    sb6 = lambda nm, sh, dt=F32: ph6.enter_context(nc.sbuf_tensor(nm, sh, dt))
    synw = sb6("synw", [128, KC * 1024], BF16)
    cwv = sb6("ce_wv", [128, KC * 1024], BF16)
    cwo = sb6("ce_wo", [128, KC * 1024], BF16)
    pr.new_phase()
    t6 = [fresh(w_, pr.w_sem()) for w_ in (synw, cwv, cwo)]
    pw6 = [load_weight("L0_ffn_ce_syn_w", synw, t6[0], 1024),
           load_weight("L0_ffn_ce_wv", cwv, t6[1], 1024),
           load_weight("L0_ffn_ce_wo", cwo, t6[2], 1024)]
    x6_t = [sb6(f"x6_{i}", [128, KC * NT], BF16) for i in range(2)]
    x6_trk = [fresh(x6_t[0], pr.buf_sem()), fresh(x6_t[1], pr.buf_sem())]
    syn_t = sb6("syn_t", [128, KC * NT])
    syn_trk = fresh(syn_t)
    synb_t = sb6("synb_t", [128, KC * NT], BF16)
    synb_trk = fresh(synb_t)
    v6_t = sb6("v6_t", [128, KC * NT], BF16)
    v6_trk = fresh(v6_t)
    s1_t = [sb6(f"s1_t6{i}", [128, KC * NT]) for i in range(2)]
    s1t_trk = [fresh(s1_t[0]), fresh(s1_t[1])]
    s1b_t = [sb6(f"s1b_t{i}", [128, KC * NT], BF16) for i in range(2)]
    s1b_trk = [fresh(s1b_t[0]), fresh(s1b_t[1])]
    h2_t = sb6("h2_t", [128, KC * NT], BF16)
    h2t_trk = fresh(h2_t, pr.buf_sem())
    rows6 = sb6("rows6", [1, 2 * NT])
    rows6_trk = fresh(rows6)

    def ph6_head(j, gen=None):
        par = j % 2
        x6, x6t = x6_t[par], x6_trk[par]
        s1, s1t = s1_t[par], s1t_trk[par]
        dma("sync", sb_tile3(x6), dram_tile(x1n_v, j), wtrks=(x6t,),
            rtrks=(dtrk["x1n_sp"][j],))

        def evac_syn(m, pb, ptrk, pt_o):
            pe = sca(lambda e, pb=pb, m=m: e.add(
                chunk(syn_t, m), pb[0:128, 0:NT], bias_col("L0_ffn_ce_syn_b", m)),
                deps=[pt_o], writes=[syn_trk] if m == 0 else ())
            syn_trk.write_at(pe)
            ptrk.read_at(pe)
            pc = sca(lambda e, m=m: e.copy(chunk(synb_t, m), chunk(syn_t, m)),
                     deps=[pe], writes=[synb_trk] if m == 0 else ())
            synb_trk.write_at(pc)
            return pc

        p_syn = proj8(synw, t6[0], pw6[0], x6, x6t, evac=evac_syn, gen=gen)

        def evac_v(m, pb, ptrk, pt_o):
            pe = sca(lambda e, pb=pb, m=m: e.activation(
                chunk(v6_t, m), pb[0:128, 0:NT], AF.Identity,
                bias=bias_col("L0_ffn_ce_bv", m)),
                deps=[pt_o], writes=[v6_trk] if m == 0 else ())
            v6_trk.write_at(pe)
            ptrk.read_at(pe)
            return pe

        p_v6 = proj8(cwv, t6[1], pw6[1], synb_t, synb_trk, kdeps=p_syn,
                     evac=evac_v)

        def evac_s1(m, pb, ptrk, pt_o):
            pt_s = vec(lambda e, pb=pb, m=m, s1=s1: e.scalar_tensor_tensor(
                chunk(s1, m), pb[0:128, 0:NT], bias_col("L0_ffn_ce_bo", m),
                chunk(syn_t, m), ALU.add, ALU.add),
                deps=[pt_o], reads=[syn_trk] if m == 0 else (),
                writes=[s1t] if m == 0 else ())
            s1t.write_at(pt_s)
            syn_trk.read_at(pt_s)
            ptrk.read_at(pt_s)
            return pt_s

        proj8(cwo, t6[2], pw6[2], v6_t, v6_trk, kdeps=p_v6, evac=evac_s1)

    def ph6_tail(j):
        par = j % 2
        s1, s1t = s1_t[par], s1t_trk[par]
        s1b, s1bt = s1b_t[par], s1b_trk[par]
        cast_pt = None
        for c in range(KC):
            cast_pt = sca(lambda e, c=c, s1=s1, s1b=s1b: e.copy(
                chunk(s1b, c), chunk(s1, c)),
                reads=[s1t] if c == 0 else (),
                writes=[s1bt] if c == 0 else ())
        s1bt.write_at(cast_pt)
        s1t.read_at(cast_pt)
        yield
        do_b = ln_stats1(s1b, s1bt, rows6[:, 0:NT],
                         rows6[:, NT:2 * NT],
                         rows6_trk, 0, 1, deps=[cast_pt],
                         src_f32=s1, src_trk=s1t, defer=True)
        yield
        pt_mub, pt_rsb = do_b()
        pt_xc = None
        for c in range(KC):
            pt_xc = vec(lambda e, c=c, s1=s1: e.tensor_sub(
                chunk(s1, c), chunk(s1, c), psB[0][0:128, 0:NT]),
                deps=[pt_mub] if c == 0 else (),
                writes=[s1t] if c == 0 else ())
        s1t.write_at(pt_xc)
        psB_trk[0].read_at(pt_xc)
        yield
        pt_m = None
        for c in range(KC):
            pt_m = vec(lambda e, c=c, s1=s1: e.tensor_mul(
                chunk(s1, c), chunk(s1, c), psB[1][0:128, 0:NT]),
                deps=[pt_rsb, pt_xc] if c == 0 else ())
        s1t.write_at(pt_m)
        psB_trk[1].read_at(pt_m)
        yield
        h2_pt = None
        for c in range(KC):
            h2_pt = sca(lambda e, c=c, s1=s1: e.activation(
                chunk(h2_t, c), chunk(s1, c), AF.Identity,
                scale=bias_col("L0_ffn_ce_n1g", c),
                bias=bias_col("L0_ffn_ce_n1b", c)),
                deps=[pt_m] if c == 0 else (),
                writes=[h2t_trk] if c == 0 else ())
        h2t_trk.write_at(h2_pt)
        s1t.read_at(h2_pt)
        dma("scalar", dram_tile(h2_v, j), sb_tile3(h2_t), rtrks=(h2t_trk,),
            wtrks=(dtrk["h2_sp"][j],))
        yield

    for j in range(NTIL):
        g6 = ph6_tail(j - 1) if j > 0 else None
        ph6_head(j, g6)
        if g6 is not None:
            for _ in g6:
                pass
    for _ in ph6_tail(NTIL - 1):
        pass

    retire((synw, t6[0]), (cwv, t6[1]), (cwo, t6[2]),
           (x6_t[0], x6_trk[0]), (x6_t[1], x6_trk[1]), (syn_t, syn_trk),
           (synb_t, synb_trk), (v6_t, v6_trk), (s1_t[0], s1t_trk[0]),
           (s1_t[1], s1t_trk[1]), (s1b_t[0], s1b_trk[0]),
           (s1b_t[1], s1b_trk[1]), (h2_t, h2t_trk), (rows6, rows6_trk))
    ph6.close()

    ph7 = ES()
    sb7 = lambda nm, sh, dt=F32: ph7.enter_context(nc.sbuf_tensor(nm, sh, dt))
    f1w = sb7("f1w", [128, KC * 2048], BF16)
    f2w = sb7("f2w", [128, 16 * 1024], BF16)
    genw = sb7("genw", [128, KC * 1024], BF16)
    pr.new_phase()
    t7 = [fresh(w_, pr.w_sem()) for w_ in (f1w, f2w, genw)]
    pw7 = [load_weight("L0_ffn_ce_f1w", f1w, t7[0], 2048),
           load_weight("L0_ffn_ce_f2w", f2w, t7[1], 1024),
           load_weight("L0_ffn_ce_gen_w", genw, t7[2], 1024)]
    h2b_t = [sb7(f"h2b{i}", [128, KC * NT], BF16) for i in range(2)]
    h2b_trk = [fresh(h2b_t[0], pr.buf_sem()), fresh(h2b_t[1], pr.buf_sem())]
    fh_t = sb7("fh_t", [128, 16 * NT], BF16)
    fh_trk = fresh(fh_t)
    s2_t = [sb7(f"s2_t{i}", [128, KC * NT]) for i in range(2)]
    s2_trk = [fresh(s2_t[0]), fresh(s2_t[1])]
    s2b_t = [sb7(f"s2b_t{i}", [128, KC * NT], BF16) for i in range(2)]
    s2b_trk = [fresh(s2b_t[0]), fresh(s2b_t[1])]
    h2n_t = sb7("h2n_t", [128, KC * NT], BF16)
    h2n_trk = fresh(h2n_t)
    acc7_t = sb7("acc7_t", [128, KC * NT])
    acc7_trk = fresh(acc7_t, pr.buf_sem())
    rows7 = sb7("rows7", [1, 2 * NT])
    rows7_trk = fresh(rows7)

    def ph7_head(j, gen=None):
        par = j % 2
        h2b, h2bt = h2b_t[par], h2b_trk[par]
        s2, s2t = s2_t[par], s2_trk[par]
        dma("sync", sb_tile3(h2b), dram_tile(h2_v, j), wtrks=(h2bt,),
            rtrks=(dtrk["h2_sp"][j],))
        fh_pts = [None] * 16
        for mc in range(16):
            pb, ptrk = next_psM()
            pt_h = mm_acc(lambda k, mc=mc: wslice(f1w, 2048, k, mc),
                          lambda k, h2b=h2b: chunk(h2b, k), KC, pb, ptrk,
                          deps=[pw7[0]], rtrks=[h2bt, t7[0]])
            pe = sca(lambda e, pb=pb, mc=mc: e.activation(
                chunk(fh_t, mc), pb[0:128, 0:NT], AF.Relu,
                bias=bias_col("L0_ffn_ce_f1b", mc)),
                deps=[pt_h], writes=[fh_trk] if mc == 0 else ())
            fh_trk.write_at(pe)
            ptrk.read_at(pe)
            fh_pts[mc] = pe
            if gen is not None and mc % 2 == 1:
                next(gen, None)
        for m in range(8):
            pb, ptrk = next_psM()
            pt_o = mm_acc(lambda k, m=m: wslice(f2w, 1024, k, m),
                          lambda k: chunk(fh_t, k), 16, pb, ptrk,
                          deps=[pw7[1]], kdeps=fh_pts, rtrks_rec=[fh_trk],
                          rtrks=[t7[1]])
            pt_s = vec(lambda e, pb=pb, m=m, h2b=h2b, s2=s2:
                       e.scalar_tensor_tensor(
                chunk(s2, m), pb[0:128, 0:NT], bias_col("L0_ffn_ce_f2b", m),
                chunk(h2b, m), ALU.add, ALU.add),
                deps=[pt_o], reads=[h2bt] if m == 0 else (),
                writes=[s2t] if m == 0 else ())
            s2t.write_at(pt_s)
            h2bt.read_at(pt_s)
            ptrk.read_at(pt_s)
            if gen is not None:
                next(gen, None)

    def ph7_tail(j):
        par = j % 2
        s2, s2t = s2_t[par], s2_trk[par]
        s2b, s2bt = s2b_t[par], s2b_trk[par]
        c_sc = c_ve = None
        for c in range(KC):
            if c % 2 == 0:
                c_sc = sca(lambda e, c=c, s2=s2, s2b=s2b: e.copy(
                    chunk(s2b, c), chunk(s2, c)),
                    reads=[s2t] if c == 0 else (),
                    writes=[s2bt] if c == 0 else ())
            else:
                c_ve = vec(lambda e, c=c, s2=s2, s2b=s2b: e.tensor_copy(
                    chunk(s2b, c), chunk(s2, c)),
                    reads=[s2t] if c == 1 else (),
                    writes=[s2bt] if c == 1 else ())
        s2bt.write_at(c_sc)
        s2bt.write_at(c_ve)
        s2t.read_at(c_sc)
        s2t.read_at(c_ve)
        yield
        do_b = ln_stats1(s2b, s2bt, rows7[:, 0:NT],
                         rows7[:, NT:2 * NT],
                         rows7_trk, 0, 1, deps=[c_sc, c_ve], defer=True)
        yield
        pt_mub, pt_rsb = do_b()
        pt_xc = None
        for c in range(KC):
            pt_xc = vec(lambda e, c=c, s2=s2: e.tensor_sub(
                chunk(s2, c), chunk(s2, c), psB[0][0:128, 0:NT]),
                deps=[pt_mub] if c == 0 else (),
                writes=[s2t] if c == 0 else ())
        s2t.write_at(pt_xc)
        psB_trk[0].read_at(pt_xc)
        yield
        pt_m = None
        for c in range(KC):
            pt_m = vec(lambda e, c=c, s2=s2: e.tensor_mul(
                chunk(s2, c), chunk(s2, c), psB[1][0:128, 0:NT]),
                deps=[pt_rsb, pt_xc] if c == 0 else ())
        s2t.write_at(pt_m)
        psB_trk[1].read_at(pt_m)
        yield
        hn_pts = [None] * KC
        hn_pt = None
        for c in range(KC):
            hn_pt = sca(lambda e, c=c, s2=s2: e.activation(
                chunk(h2n_t, c), chunk(s2, c), AF.Identity,
                scale=bias_col("L0_ffn_ce_n2g", c),
                bias=bias_col("L0_ffn_ce_n2b", c)),
                deps=[pt_m] if c == 0 else (),
                writes=[h2n_trk] if c == 0 else ())
            hn_pts[c] = hn_pt
        h2n_trk.write_at(hn_pt)
        s2t.read_at(hn_pt)
        yield
        pt_wb = wrow_dma_bcast(j % 2, R_W2, j)
        for m in range(8):
            pb, ptrk = next_psM()
            pt_o = mm_acc(lambda k, m=m: wslice(genw, 1024, k, m),
                          lambda k: chunk(h2n_t, k), KC, pb, ptrk,
                          deps=[pw7[2]], kdeps=hn_pts, rtrks_rec=[h2n_trk],
                          rtrks=[t7[2]])
            acc_evac_rmw(pb, ptrk, m, j, "L0_ffn_ce_gen_b", pt_wb, 0,
                         acc7_t, acc7_trk, pt_o=pt_o)
            if m % 2 == 1:
                yield
        dma("scalar", dram_tile(acc_v[1], j), sb_tile3(acc7_t),
            rtrks=(acc7_trk,), wtrks=(dtrk["acc1_sp"][j],))
        yield

    for j in range(NTIL):
        g7 = ph7_tail(j - 1) if j > 0 else None
        ph7_head(j, g7)
        if g7 is not None:
            for _ in g7:
                pass
    for _ in ph7_tail(NTIL - 1):
        pass

    retire((f1w, t7[0]), (f2w, t7[1]), (genw, t7[2]),
           (h2b_t[0], h2b_trk[0]), (h2b_t[1], h2b_trk[1]), (fh_t, fh_trk),
           (s2_t[0], s2_trk[0]), (s2_t[1], s2_trk[1]),
           (s2b_t[0], s2b_trk[0]), (s2b_t[1], s2b_trk[1]),
           (h2n_t, h2n_trk), (acc7_t, acc7_trk),
           (rows7, rows7_trk))
    ph7.close()

    ph8 = ES()
    sb8 = lambda nm, sh, dt=F32: ph8.enter_context(nc.sbuf_tensor(nm, sh, dt))
    mw1 = sb8("ml_w1", [128, KC * 4096], BF16)
    mw2 = sb8("ml_w2", [128, 32 * 1024], BF16)
    pr.new_phase()
    t8 = [fresh(mw1, pr.w_sem()), fresh(mw2, pr.w_sem())]
    pw8 = [load_weight("L0_ffn_ml_w1", mw1, t8[0], 4096),
           load_weight("L0_ffn_ml_w2", mw2, t8[1], 1024)]
    x8_t = [sb8(f"x8_{i}", [128, KC * NT], BF16) for i in range(2)]
    x8_trk = [fresh(x8_t[0], pr.buf_sem()), fresh(x8_t[1], pr.buf_sem())]
    h3_t = sb8("h3_t", [128, 32 * NT], BF16)
    h3_trk = fresh(h3_t)
    f_t = sb8("f_t", [128, KC * NT])
    f_trk = fresh(f_t, pr.buf_sem())
    x1c_c = [sb8("x1c_c0", [128, NT])] * 2
    x1c_trk = [fresh(x1c_c[0], pr.buf_sem())] * 2
    rows8 = sb8("rows8", [1, 2 * NT])
    rows8_trk = fresh(rows8)

    def ph8_head(j, gen=None):
        par = j % 2
        x8, x8t = x8_t[par], x8_trk[par]
        dma("sync", sb_tile3(x8), dram_tile(x1n_v, j), wtrks=(x8t,),
            rtrks=(dtrk["x1n_sp"][j],))
        pt_wb = wrow_dma_bcast(j % 2, R_W3, j)
        h3_pts = [None] * 32
        for mc in range(32):
            pb, ptrk = next_psM()
            pt_h = mm_acc(lambda k, mc=mc: wslice(mw1, 4096, k, mc),
                          lambda k, x8=x8: chunk(x8, k), KC, pb, ptrk,
                          deps=[pw8[0]], rtrks=[x8t, t8[0]])
            pe = sca(lambda e, pb=pb, mc=mc: e.activation(
                chunk(h3_t, mc), pb[0:128, 0:NT], AF.Gelu,
                bias=bias_col("L0_ffn_ml_b1", mc)),
                deps=[pt_h], writes=[h3_trk] if mc == 0 else ())
            h3_trk.write_at(pe)
            ptrk.read_at(pe)
            h3_pts[mc] = pe
            if gen is not None and mc % 3 == 2:
                next(gen, None)
        return h3_pts, pt_wb

    def ph8_headB(j, h3_pts, pt_wb):
        for m in range(8):
            pb, ptrk = next_psM()
            pt_o = mm_acc(lambda k, m=m: wslice(mw2, 1024, k, m),
                          lambda k: chunk(h3_t, k), 32, pb, ptrk,
                          deps=[pw8[1]], kdeps=h3_pts, rtrks_rec=[h3_trk],
                          rtrks=[t8[1]])
            acc_evac_rmw(pb, ptrk, m, j, "L0_ffn_ml_b2", pt_wb, 1,
                         f_t, f_trk, pt_o=pt_o)

    def ph8_tail(j):
        par = j % 2
        fb, fbt = x8_t[par], x8_trk[par]
        c_sc = c_ve = None
        for c in range(KC):
            if c % 2 == 0:
                c_sc = sca(lambda e, c=c, fb=fb: e.copy(chunk(fb, c),
                                                        chunk(f_t, c)),
                           reads=[f_trk] if c == 0 else (),
                           writes=[fbt] if c == 0 else ())
            else:
                c_ve = vec(lambda e, c=c, fb=fb: e.tensor_copy(
                    chunk(fb, c), chunk(f_t, c)),
                    reads=[f_trk] if c == 1 else (),
                    writes=[fbt] if c == 1 else ())
        fbt.write_at(c_sc)
        fbt.write_at(c_ve)
        f_trk.read_at(c_sc)
        f_trk.read_at(c_ve)
        yield
        do_b = ln_stats1(fb, fbt, rows8[:, 0:NT], rows8[:, NT:2 * NT],
                         rows8_trk, 0, 1,
                         deps=[c_sc, c_ve], src_f32=f_t,
                         src_trk=f_trk, defer=True)
        yield
        pt_mub, pt_rsb = do_b()
        pt_xc = None
        for c in range(KC):
            pt_xc = vec(lambda e, c=c: e.tensor_sub(chunk(f_t, c), chunk(f_t, c),
                                                    psB[0][0:128, 0:NT]),
                        deps=[pt_mub] if c == 0 else (),
                        writes=[f_trk] if c == 0 else ())
        f_trk.write_at(pt_xc)
        psB_trk[0].read_at(pt_xc)
        yield
        pt_o2 = None
        for c in range(KC):
            pt_m = vec(lambda e, c=c: e.tensor_mul(chunk(f_t, c), chunk(f_t, c),
                                                   psB[1][0:128, 0:NT]),
                       deps=[pt_rsb, pt_xc] if c == 0 else ())
            f_trk.write_at(pt_m)
            xc, xct = x1c_c[c % 2], x1c_trk[c % 2]
            dma("sync", xc[:], x1_v[:, c, j * NT:(j + 1) * NT],
                wtrks=(xct,), rtrks=(dtrk["x1_sp"][j],))
            pt_o2 = vec(lambda e, c=c, xc=xc: e.tensor_add(chunk(f_t, c),
                                                           chunk(f_t, c),
                                                           xc[:]),
                        deps=[pt_m], reads=[xct])
            f_trk.write_at(pt_o2)
            if c % 3 == 2:
                yield
        psB_trk[1].read_at(pt_o2)
        dma("sync", dram_tile(out_v, j), sb_tile3(f_t), rtrks=(f_trk,))
        yield

    for j in range(NTIL):
        g8 = ph8_tail(j - 1) if j > 0 else None
        h3_pts, pt_wb8 = ph8_head(j, g8)
        if g8 is not None:
            for _ in g8:
                pass
        ph8_headB(j, h3_pts, pt_wb8)
    for _ in ph8_tail(NTIL - 1):
        pass

    ph8.close()

    for s_, v_ in sorted(pr.counts.items()):
        pr.wait("sync", s_, v_)

    with nc.Block() as block:
        pr.finalize(block)
    ctx.close()
    return nc


def _pack_biases(inputs):
    out = np.zeros((128, NBIAS), np.float32)
    for name, ncol in BIAS_SPEC:
        v = np.asarray(inputs[name], np.float32)
        off = BIAS_OFF[name]
        if v.shape[0] == 4:
            out[0:4, off] = v
        else:
            out[:, off:off + ncol] = v.reshape(ncol, 128).T
    return out


def make_in_maps(inputs):
    x = np.asarray(inputs["L0_attn_x"])
    biasv = _pack_biases(inputs)
    ident = np.eye(128, dtype=np.float32)
    in_maps = []
    for c in range(8):
        bi, si = c // 4, c % 4
        xs = np.ascontiguousarray(x[bi, si * T:(si + 1) * T, :].T)
        fold = np.zeros((128, 16), np.float32)
        for jj in range(8):
            if jj // 4 == bi and (jj % 4) < si:
                fold[:, jj] = 1.0
        fold[:, 8:16] = 1.0 - fold[:, 0:8]
        m = {"x": xs, "biasv": biasv, "foldmask": fold, "identity": ident}
        for n in WEIGHT_NAMES:
            m[n] = np.asarray(inputs[n])
        in_maps.append(m)
    return in_maps


def kernel(**inputs):
    nc = build_nc()
    in_maps = make_in_maps(inputs)
    res = run_bass_kernel_spmd(nc, in_maps, core_ids=list(range(8)))
    outs = [np.asarray(res.results[c]["out"]).T for c in range(8)]
    return np.stack([np.concatenate(outs[b * 4:(b + 1) * 4], axis=0)
                     for b in range(2)], axis=0)


# version 20
# speedup vs baseline: 1.0014x; 1.0014x over previous
import numpy as np
import concourse.bass as bass
import concourse.mybir as mybir
from concourse.bass_utils import run_bass_kernel_spmd

F32 = mybir.dt.float32
BF16 = mybir.dt.bfloat16
AF = mybir.ActivationFunctionType
ALU = mybir.AluOpType
AX = mybir.AxisListType

D = 1024
KC = 8
T = 4096
NT = 512
NTIL = T // NT
EPS = 1e-5

ENGINES = ("sync", "gpsimd", "tensor", "scalar", "vector")

BIAS_SPEC = [
    ("L0_attn_norm1_w", 8), ("L0_attn_norm1_b", 8), ("L0_attn_gate_b", 16),
    ("L0_attn_state_b", 8), ("L0_attn_out_b", 8), ("L0_attn_norm2_w", 8),
    ("L0_attn_norm2_b", 8), ("L0_attn_mlp_b1", 32), ("L0_attn_mlp_b2", 8),
    ("L0_ffn_router_b", 1), ("L0_ffn_load_balancer", 1),
    ("L0_ffn_me_eq_b", 8), ("L0_ffn_me_bv", 8), ("L0_ffn_me_bo", 8),
    ("L0_ffn_me_c1b", 16), ("L0_ffn_me_c2b", 8),
    ("L0_ffn_ce_syn_b", 8), ("L0_ffn_ce_bv", 8), ("L0_ffn_ce_bo", 8),
    ("L0_ffn_ce_n1g", 8), ("L0_ffn_ce_n1b", 8), ("L0_ffn_ce_f1b", 16),
    ("L0_ffn_ce_f2b", 8), ("L0_ffn_ce_n2g", 8), ("L0_ffn_ce_n2b", 8),
    ("L0_ffn_ce_gen_b", 8), ("L0_ffn_ml_b1", 32), ("L0_ffn_ml_b2", 8),
]
BIAS_OFF = {}
_off = 0
for _n, _c in BIAS_SPEC:
    BIAS_OFF[_n] = _off
    _off += _c
NBIAS = _off

WSHAPE = {
    "L0_attn_gate_w": (D, 2048), "L0_attn_state_w": (D, D),
    "L0_attn_out_w": (D, D), "L0_attn_mlp_w1": (D, 4096),
    "L0_attn_mlp_w2": (4096, D), "L0_ffn_router_w": (D, 4),
    "L0_ffn_sw_w1": (D, 4096), "L0_ffn_sw_w3": (D, 4096),
    "L0_ffn_sw_w2": (4096, D), "L0_ffn_me_eq_w": (D, D),
    "L0_ffn_me_wv": (D, D), "L0_ffn_me_wo": (D, D),
    "L0_ffn_me_c1w": (D, 2048), "L0_ffn_me_c2w": (2048, D),
    "L0_ffn_ce_syn_w": (D, D), "L0_ffn_ce_wv": (D, D),
    "L0_ffn_ce_wo": (D, D), "L0_ffn_ce_f1w": (D, 2048),
    "L0_ffn_ce_f2w": (2048, D), "L0_ffn_ce_gen_w": (D, D),
    "L0_ffn_ml_w1": (D, 4096), "L0_ffn_ml_w2": (4096, D),
}
WEIGHT_NAMES = list(WSHAPE)

R_MU1, R_RS1, R_P, R_W0, R_W1, R_W2, R_W3 = range(7)


class Prog:

    def __init__(self, nc):
        self.nc = nc
        self.streams = {e: [] for e in ENGINES}
        self.sems = {}
        self.counts = {}
        self.lastwait = {e: {} for e in ENGINES}

    def new_sem(self, ctx, name):
        self.sems[name] = ctx.enter_context(self.nc.semaphore(name))
        self.counts[name] = 0

    def init_pools(self, ctx, npool=20, nglob=12, nw=8):
        self.npool, self.nw = npool, nw
        for i in range(npool):
            self.new_sem(ctx, f"P{i}")
        for i in range(nglob):
            self.new_sem(ctx, f"G{i}")
        for i in range(nw):
            self.new_sem(ctx, f"W{i}")
        self.phase_used = 0
        self.glob_used = 0
        self.w_used = 0

    def buf_sem(self):
        assert self.phase_used < self.npool, "per-phase DMA sem pool exhausted"
        n = f"P{self.phase_used}"
        self.phase_used += 1
        return n

    def w_sem(self):
        assert self.w_used < self.nw, "weight (SWDGE) sem pool exhausted"
        n = f"W{self.w_used}"
        self.w_used += 1
        return n

    def g_sem(self):
        n = f"G{self.glob_used}"
        self.glob_used += 1
        return n

    def new_phase(self):
        self.phase_used = 0
        self.w_used = 0

    def wait(self, eng, sem, val):
        if not val or val <= 0:
            return
        if self.lastwait[eng].get(sem, 0) >= val:
            return
        self.lastwait[eng][sem] = val
        h = self.sems[sem]
        self.streams[eng].append(lambda e, h=h, v=val: e.wait_ge(h, v))

    def waits(self, eng, pts):
        for sem, val in pts.items():
            self.wait(eng, sem, val)

    def do(self, eng, maker, inc=None, incby=1):
        if inc is None:
            self.streams[eng].append(lambda e, m=maker: m(e))
            return None
        self.counts[inc] += incby
        cnt = self.counts[inc]
        h = self.sems[inc]
        self.streams[eng].append(lambda e, m=maker, h=h, b=incby: m(e).then_inc(h, b))
        return {inc: cnt}

    def fence(self):
        snap = dict(self.counts)
        for eng in ENGINES:
            self.waits(eng, snap)

    def finalize(self, block):
        for eng in ENGINES:
            stream = self.streams[eng]
            getattr(block, eng)(lambda e, s=stream: [f(e) for f in s])


class Trk:
    def __init__(self, sem=None):
        self.readers = {}
        self.writer = {}
        self.sem = sem

    def read_at(self, pt):
        if pt:
            for s, v in pt.items():
                self.readers[s] = max(self.readers.get(s, 0), v)

    def write_at(self, pt):
        if pt:
            for s, v in pt.items():
                self.writer[s] = max(self.writer.get(s, 0), v)

    def before_write(self):
        out = dict(self.writer)
        for s, v in self.readers.items():
            out[s] = max(out.get(s, 0), v)
        return out

    def before_read(self):
        return dict(self.writer)


def merge_pts(*pts):
    out = {}
    for p in pts:
        if p:
            for s, v in p.items():
                out[s] = max(out.get(s, 0), v)
    return out


def build_nc(dbg=()):
    nc = bass.Bass("TRN2", target_bir_lowering=False)
    from contextlib import ExitStack, ExitStack as ES

    x_d = nc.dram_tensor("x", [D, T], F32, kind="ExternalInput").ap()
    biasv_d = nc.dram_tensor("biasv", [128, NBIAS], F32, kind="ExternalInput").ap()
    fold_d = nc.dram_tensor("foldmask", [128, 16], F32, kind="ExternalInput").ap()
    ident_d = nc.dram_tensor("identity", [128, 128], F32, kind="ExternalInput").ap()
    w_d = {n: nc.dram_tensor(n, list(WSHAPE[n]), F32, kind="ExternalInput").ap()
           for n in WEIGHT_NAMES}
    out_d = nc.dram_tensor("out", [D, T], F32, kind="ExternalOutput").ap()

    def scratch(name, shape, dt):
        kind = "ExternalOutput" if name in dbg else "Internal"
        return nc.dram_tensor(name, shape, dt, kind=kind).ap()

    t1_sp = scratch("t1_sp", [D, T], F32)
    yloc_sp = scratch("yloc_sp", [D, T], BF16)
    t_sp = scratch("t_sp", [D, T], F32)
    x2n_sp = scratch("x2n_sp", [D, T], BF16)
    x1_sp = scratch("x1_sp", [D, T], F32)
    x1n_sp = scratch("x1n_sp", [D, T], BF16)
    h2_sp = scratch("h2_sp", [D, T], BF16)
    acc0_sp = scratch("acc0_sp", [D, T], F32)
    acc1_sp = scratch("acc1_sp", [D, T], F32)
    rows_sp = scratch("rows_sp", [8, T], F32)
    cc_in = scratch("cc_in", [128 * 9], F32)
    cc_out = scratch("cc_out", [8, 128 * 9], F32)
    hin_dbg = scratch("hin_dbg", [128, 9], F32) if "hin_dbg" in dbg else None

    dtrk = {nm: [Trk() for _ in range(NTIL)] for nm in
            ("t1_sp", "yloc_sp", "t_sp", "x2n_sp", "x1_sp", "x1n_sp", "h2_sp",
             "acc0_sp", "acc1_sp", "rows_sp")}
    dtrk.update({nm: Trk() for nm in ("cc_in", "cc_out")})

    ctx = ExitStack()
    pr = Prog(nc)
    for s in ("TS", "VS", "SS", "CC"):
        pr.new_sem(ctx, s)
    pr.init_pools(ctx)

    sb = lambda nm, sh, dt=F32: ctx.enter_context(nc.sbuf_tensor(nm, sh, dt))
    psum = lambda nm: ctx.enter_context(nc.psum_tensor(nm, [128, NT], F32))

    biasv = sb("biasv_s", [128, NBIAS])
    ones_col = sb("ones_col", [128, 1], BF16)
    ones_colf = sb("ones_colf", [128, 1])
    ones_row = sb("ones_row", [1, 128])
    consts = sb("consts", [1, 4])
    biasrl = sb("biasrl", [4, 1])
    ph12 = ES()
    sb12 = lambda nm, sh, dt=F32: ph12.enter_context(nc.sbuf_tensor(nm, sh, dt))
    hstate = sb12("hstate", [128, KC], BF16)
    h_in = sb12("h_in", [128, KC])
    Pcarry = sb12("Pcarry", [1, 1])
    foldm = sb12("foldm", [128, 16])
    mean_bs = sb12("mean_bs", [1, 1])
    rvec = sb12("rvec", [128, KC], BF16)
    msg = sb12("msg", [128, 9])
    gath = sb12("gath", [128, 72])
    aeff = sb12("aeff", [128, 1])
    hmsk = sb12("hmsk", [128, 8])

    biasv_trk, foldm_trk = Trk(pr.g_sem()), Trk(pr.g_sem())
    hstate_trk, hin_trk, Pc_trk = Trk(), Trk(), Trk()
    msg_trk, gath_trk = Trk(pr.g_sem()), Trk(pr.g_sem())
    meanb_trk, rvec_trk, biasrl_trk = Trk(), Trk(), Trk()

    psR = psum("psR")
    psB = [psum(f"psB{i}") for i in range(4)]
    psM = [psum(f"psM{i}") for i in range(3)]
    psR_trk = Trk()
    psR2_trk = Trk()
    psB_trk = [Trk() for _ in range(4)]
    psM_trk = [Trk() for _ in range(3)]
    psm_rot = [0]

    def next_psM():
        i = psm_rot[0] % 3
        psm_rot[0] += 1
        return psM[i], psM_trk[i]

    def dma(q, out_ap, in_ap, wtrks=(), rtrks=(), war=True):
        sem = None
        for t in list(wtrks) + list(rtrks):
            if t.sem is not None:
                sem = t.sem
                break
        assert sem is not None, "dma needs a tracked buffer with a sem"
        if war:
            for t in wtrks:
                pr.waits(q, t.before_write())
        for t in rtrks:
            pr.waits(q, t.before_read())
        pt = pr.do(q, lambda e, o=out_ap, i=in_ap: e.dma_start(out=o, in_=i),
                   inc=sem, incby=16)
        for t in wtrks:
            t.write_at(pt)
        for t in rtrks:
            t.read_at(pt)
        return pt

    _memloc_cache = {}

    def addr_range(t):
        nm = t.name
        if nm not in _memloc_cache:
            for a in nc.m.functions[0].allocations:
                try:
                    ml = a.memorylocations[0]
                except Exception:
                    continue
                _memloc_cache[ml.name] = (ml.addr, ml.size())
        lo, sz = _memloc_cache[nm]
        return lo, lo + sz

    _dead = []

    def retire(*pairs):
        snap = {s: pr.counts[s] for s in ("TS", "VS", "SS")}
        for t, trk in pairs:
            lo, hi = addr_range(t)
            pts = trk.before_write() if trk is not None else dict(snap)
            _dead.append((lo, hi, pts))

    def fresh(t, sem=None):
        trk = Trk(sem)
        lo, hi = addr_range(t)
        for a, b, pts in _dead:
            if a < hi and lo < b:
                trk.write_at(pts)
        if sem is not None:
            trk.write_at({sem: pr.counts[sem]})
        return trk

    def mm_acc(lhsT_fn, rhs_fn, nk, pbank, ptrk, deps=(), rtrks=(), cols=NT,
               rows=128, kdeps=None, rtrks_rec=()):
        pr.waits("tensor", ptrk.before_write())
        for dp in deps:
            if dp:
                pr.waits("tensor", dp)
        if kdeps is None:
            rtrks = list(rtrks) + list(rtrks_rec)
            rtrks_rec = ()
        for t in rtrks:
            pr.waits("tensor", t.before_read())
        pt = None
        for k in range(nk):
            if kdeps is not None and kdeps[k]:
                pr.waits("tensor", kdeps[k])
            last = k == nk - 1
            pt = pr.do("tensor",
                       lambda e, l=lhsT_fn(k), r=rhs_fn(k), s=(k == 0), p=last,
                       pb=pbank, rw=rows, cl=cols:
                       e.matmul(pb[0:rw, 0:cl], l, r, start=s, stop=p),
                       inc="TS" if last else None)
        ptrk.write_at(pt)
        for t in rtrks:
            t.read_at(pt)
        for t in rtrks_rec:
            t.read_at(pt)
        return pt

    def bcast(row_ap, pbi, deps=(), rtrks=(), cols=NT):
        pb, ptrk = psB[pbi], psB_trk[pbi]
        pr.waits("tensor", ptrk.before_write())
        for dp in deps:
            if dp:
                pr.waits("tensor", dp)
        for t in rtrks:
            pr.waits("tensor", t.before_read())
        pt = pr.do("tensor", lambda e, r=row_ap, p=pb, c=cols:
                   e.matmul(p[0:128, 0:c], ones_row[:], r, start=True, stop=True),
                   inc="TS")
        ptrk.write_at(pt)
        for t in rtrks:
            t.read_at(pt)
        return pt

    def _eng_op(eng, sem, maker, deps=(), reads=(), writes=()):
        for dp in deps:
            if dp:
                pr.waits(eng, dp)
        for t in reads:
            pr.waits(eng, t.before_read())
        for t in writes:
            pr.waits(eng, t.before_write())
        pt = pr.do(eng, maker, inc=sem)
        for t in reads:
            t.read_at(pt)
        for t in writes:
            t.write_at(pt)
        return pt

    def vec(maker, deps=(), reads=(), writes=()):
        return _eng_op("vector", "VS", maker, deps, reads, writes)

    def sca(maker, deps=(), reads=(), writes=()):
        return _eng_op("scalar", "SS", maker, deps, reads, writes)

    def rsqrt_row(dst_ap, src_ap, deps, trk=None):
        p1 = sca(lambda e, d=dst_ap, s=src_ap:
                 e.activation(d, s, AF.Ln, bias=consts[:, 0:1]), deps=deps)
        p2 = sca(lambda e, d=dst_ap: e.activation(d, d, AF.Exp, scale=-0.5),
                 deps=[p1])
        if trk is not None:
            trk.write_at(p2)
        return p2

    def load_weight(name, dst_sb, dst_trk, dcols, src_lo=None, src_hi=None,
                    krange=None, pts=None):
        wa = w_d[name]
        din = WSHAPE[name][0]
        ks = list(krange) if krange is not None else list(range(din // 128))
        lo = 0 if src_lo is None else src_lo
        hi = WSHAPE[name][1] if src_hi is None else src_hi
        assert hi - lo == dcols
        pt = None
        for ki, k in enumerate(ks):
            pt = dma("gpsimd",
                     dst_sb[:, ki * dcols:(ki + 1) * dcols],
                     wa[k * 128:(k + 1) * 128, lo:hi],
                     wtrks=(dst_trk,), war=(ki == 0))
        if pts is not None:
            pts.extend([pt] * len(ks))
        return pt

    def wslice(wsb, dcols, k, m):
        base = k * dcols + m * 128
        return wsb[:, base: base + 128]

    def bias_col(name, c):
        off = BIAS_OFF[name]
        return biasv[:, off + c: off + c + 1]

    x_v = x_d.rearrange("(c p) t -> p c t", p=128)
    t1_v = t1_sp.rearrange("(c p) t -> p c t", p=128)
    yloc_v = yloc_sp.rearrange("(c p) t -> p c t", p=128)
    t_v = t_sp.rearrange("(c p) t -> p c t", p=128)
    x2n_v = x2n_sp.rearrange("(c p) t -> p c t", p=128)
    x1_v = x1_sp.rearrange("(c p) t -> p c t", p=128)
    x1n_v = x1n_sp.rearrange("(c p) t -> p c t", p=128)
    h2_v = h2_sp.rearrange("(c p) t -> p c t", p=128)
    acc_v = [acc0_sp.rearrange("(c p) t -> p c t", p=128),
             acc1_sp.rearrange("(c p) t -> p c t", p=128)]
    out_v = out_d.rearrange("(c p) t -> p c t", p=128)

    def dram_tile(ap3, j):
        return ap3[:, :, j * NT:(j + 1) * NT]

    def sb_tile3(t):
        return t[:].rearrange("p (c n) -> p c n", n=NT)

    def chunk(t, c, width=NT):
        return t[:, c * width:(c + 1) * width]

    def ln_stats1(xb_t, xb_trk, r_mu, r_v, rows_trk, mubi, rsbi,
                  deps=(), src_f32=None, src_trk=None, defer=False):
        pt_mean = mm_acc(lambda k: ones_col[:], lambda k: chunk(xb_t, k), KC,
                         psR, psR_trk, deps=deps, rtrks=[xb_trk], rows=1)
        sq_src = xb_t if src_f32 is None else src_f32
        sq_sc = sq_ve = None
        for c in range(KC):
            rd = [src_trk] if (src_f32 is not None and src_trk is not None
                              and c < 2) else ()
            if c % 2 == 0:
                sq_sc = sca(lambda e, c=c, s=sq_src: e.activation(
                    chunk(xb_t, c), chunk(s, c), AF.Square),
                    deps=[pt_mean] if c == 0 else (), reads=rd,
                    writes=[xb_trk] if c == 0 else ())
            else:
                sq_ve = vec(lambda e, c=c, s=sq_src: e.tensor_mul(
                    chunk(xb_t, c), chunk(s, c), chunk(s, c)),
                    deps=[pt_mean] if c == 1 else (), reads=rd,
                    writes=[xb_trk] if c == 1 else ())
        xb_trk.write_at(sq_sc)
        xb_trk.write_at(sq_ve)
        if src_trk is not None:
            src_trk.read_at(sq_sc)
            src_trk.read_at(sq_ve)
        pr.waits("tensor", psR2_trk.before_write())
        pt_ss = mm_acc2(lambda k: ones_col[:], lambda k: chunk(xb_t, k), KC,
                        deps=[sq_sc, sq_ve], rtrks=[xb_trk])
        pt_mu = vec(lambda e, r=r_mu: e.tensor_scalar_mul(r, psR[0:1, 0:NT],
                                                          1.0 / 1024.0),
                    deps=[pt_mean], writes=[rows_trk])
        psR_trk.read_at(pt_mu)
        pt_m2 = vec(lambda e, r=r_v: e.tensor_scalar_mul(r, psR[32:33, 0:NT],
                                                         1.0 / 1024.0),
                    deps=[pt_ss], writes=[rows_trk])
        psR2_trk.read_at(pt_m2)
        pt_q = vec(lambda e, b=r_mu: e.tensor_mul(psR[0:1, 0:NT], b, b),
                   deps=[pt_mu])
        psR_trk.write_at(pt_q)
        pt_var = vec(lambda e, a=r_v: e.tensor_sub(a, a, psR[0:1, 0:NT]),
                     deps=[pt_m2, pt_q], writes=[rows_trk])
        psR_trk.read_at(pt_var)
        pt_rs = rsqrt_row(r_v, r_v, deps=[pt_var], trk=rows_trk)

        def do_b():
            pt_mub = bcast(r_mu, mubi, deps=[pt_mu], rtrks=[rows_trk])
            pt_rsb = bcast(r_v, rsbi, deps=[pt_rs], rtrks=[rows_trk])
            return pt_mub, pt_rsb
        if defer:
            return do_b
        return do_b()

    def mm_acc2(lhsT_fn, rhs_fn, nk, deps=(), rtrks=()):
        for dp in deps:
            if dp:
                pr.waits("tensor", dp)
        for t in rtrks:
            pr.waits("tensor", t.before_read())
        pt = None
        for k in range(nk):
            last = k == nk - 1
            pt = pr.do("tensor",
                       lambda e, l=lhsT_fn(k), r=rhs_fn(k), s=(k == 0), p=last:
                       e.matmul(psR[32:33, 0:NT], l, r, start=s, stop=p),
                       inc="TS" if last else None)
        psR2_trk.write_at(pt)
        for t in rtrks:
            t.read_at(pt)
        return pt

    dma("sync", biasv[:], biasv_d, wtrks=(biasv_trk,))
    dma("sync", foldm[:], fold_d, wtrks=(foldm_trk,))
    vec(lambda e: e.memset(ones_col[:], 1.0))
    vec(lambda e: e.memset(ones_colf[:], 1.0))
    vec(lambda e: e.memset(ones_row[:], 1.0))
    vec(lambda e: e.memset(consts[:, 0:1], EPS))
    vec(lambda e: e.memset(consts[:, 1:2], 1.0))
    vec(lambda e: e.memset(Pcarry[:], 1.0), writes=[Pc_trk])
    vec(lambda e: e.memset(hstate[:], 0.0), writes=[hstate_trk])
    pt_hin0 = vec(lambda e: e.memset(h_in[:], 0.0), writes=[hin_trk])
    off_rb = BIAS_OFF["L0_ffn_router_b"]
    off_lb = BIAS_OFF["L0_ffn_load_balancer"]
    vec(lambda e: e.tensor_add(biasrl[:], biasv[0:4, off_rb:off_rb + 1],
                               biasv[0:4, off_lb:off_lb + 1]),
        reads=[biasv_trk], writes=[biasrl_trk])
    pr.fence()

    ph1 = ES()
    sb1 = lambda nm, sh, dt=F32: ph1.enter_context(nc.sbuf_tensor(nm, sh, dt))
    gate_w = sb1("gate_w", [128, KC * 2048], BF16)
    state_w = sb1("state_w", [128, KC * 1024], BF16)
    pr.new_phase()
    gate_w_trk, state_w_trk = fresh(gate_w, pr.w_sem()), fresh(state_w, pr.w_sem())
    pw_gate = load_weight("L0_attn_gate_w", gate_w, gate_w_trk, 2048)
    pw_state = load_weight("L0_attn_state_w", state_w, state_w_trk, 1024)

    x_t = [sb1(f"x_t{i}", [128, KC * NT]) for i in range(2)]
    x_t_trk = [fresh(x_t[0], pr.buf_sem()), fresh(x_t[1], pr.buf_sem())]
    x_bt = [sb1(f"x_bt{i}", [128, KC * NT], BF16) for i in range(2)]
    x_bt_trk = [fresh(x_bt[0], pr.w_sem()), fresh(x_bt[1], pr.w_sem())]
    xc_t = sb1("xc_t", [128, KC * NT])
    xc_trk = fresh(xc_t)
    t1_t = sb1("t1_t", [128, KC * NT])
    t1_trk = fresh(t1_t, pr.buf_sem())
    xn_t = sb1("xn_t", [128, KC * NT], BF16)
    xn_trk = fresh(xn_t)
    gate_s = sb1("gate_s", [128, KC * NT], BF16)
    gg_s = sb1("gg_s", [128, KC * NT], BF16)
    gate_s_trk, gg_trk = fresh(gate_s), fresh(gg_s)
    s1_c = sb1("s1_c", [128, NT])
    s1_trk = fresh(s1_c)
    dbu_c = sb1("dbu_c", [128, NT])
    dbu_trk = fresh(dbu_c)
    y_t = sb1("y_t", [128, KC * NT], BF16)
    y_trk = fresh(y_t, pr.buf_sem())
    rvec_f = sb1("rvec_f", [128, KC])
    rvec_f_trk = fresh(rvec_f)
    zrow = sb1("zrow", [1, NT])
    rowst = sb1("rowst", [1, 10 * NT])
    rows_trk = fresh(rowst, pr.buf_sem())

    rowsd = sb1("rowsd", [1, 4 * NT])
    rowsd_trk = fresh(rowsd, pr.buf_sem())
    vec(lambda e: e.memset(zrow[:], 0.0))

    def row(i):
        return rowst[:, i * NT:(i + 1) * NT]

    def rowd(i):
        return rowsd[:, i * NT:(i + 1) * NT]

    pv = None
    for k in range(KC):
        pv = vec(lambda e, k=k: e.reduce_sum(rvec_f[:, k:k + 1],
                                             state_w[:, k * 1024:(k + 1) * 1024],
                                             axis=AX.X),
                 deps=[pw_state] if k == 0 else (), writes=[rvec_f_trk])
    state_w_trk.write_at(pw_state)
    state_w_trk.read_at(pv)
    pt_rvec = sca(lambda e: e.activation(rvec[:], rvec_f[:], AF.Identity,
                                         scale=1.0 / 1024.0),
                  reads=[rvec_f_trk], writes=[rvec_trk])
    off_sb_ = BIAS_OFF["L0_attn_state_b"]
    pr.waits("tensor", psR_trk.before_write())
    pr.waits("tensor", biasv_trk.before_read())
    pt_mb = pr.do("tensor", lambda e: e.matmul(
        psR[0:1, 0:8], ones_colf[:], biasv[0:128, off_sb_:off_sb_ + 8],
        start=True, stop=True), inc="TS")
    psR_trk.write_at(pt_mb)
    biasv_trk.read_at(pt_mb)
    p1 = vec(lambda e: e.reduce_sum(mean_bs[:], psR[0:1, 0:8], axis=AX.X),
             deps=[pt_mb], writes=[meanb_trk])
    psR_trk.read_at(p1)
    pt_meanb = vec(lambda e: e.tensor_scalar_mul(mean_bs[:], mean_bs[:],
                                                 1.0 / 1024.0),
                   deps=[p1], reads=[meanb_trk], writes=[meanb_trk])

    def st_stats(j):
        par = j % 2
        xt, xt_trk = x_t[par], x_t_trk[par]
        xbt, xbt_trk = x_bt[par], x_bt_trk[par]
        dma("sync", sb_tile3(xt), dram_tile(x_v, j), wtrks=(xt_trk,))
        dma("gpsimd", sb_tile3(xbt), dram_tile(x_v, j), wtrks=(xbt_trk,))
        pt_mean = mm_acc(lambda k: ones_col[:], lambda k, b=xbt: chunk(b, k), KC,
                         psR, psR_trk, rtrks=[xbt_trk], rows=1)
        sq_pt = None
        for c in range(KC):
            sq_pt = sca(lambda e, c=c, b=xbt: e.activation(
                chunk(b, c), chunk(b, c), AF.Square),
                deps=[pt_mean] if c == 0 else (),
                writes=[xbt_trk] if c == 0 else ())
        xbt_trk.write_at(sq_pt)
        pt_ss = mm_acc2(lambda k: ones_col[:], lambda k, b=xbt: chunk(b, k), KC,
                        deps=[sq_pt], rtrks=[xbt_trk])
        pt_mu = vec(lambda e: e.tensor_scalar_mul(row(0), psR[0:1, 0:NT],
                                                  1.0 / 1024.0),
                    deps=[pt_mean], writes=[rows_trk])
        psR_trk.read_at(pt_mu)
        dma("scalar", rows_sp[R_MU1:R_MU1 + 1, j * NT:(j + 1) * NT], row(0),
            rtrks=(rows_trk,), wtrks=(dtrk["rows_sp"][j],))
        pt_m2 = vec(lambda e: e.tensor_scalar_mul(row(1), psR[32:33, 0:NT],
                                                  1.0 / 1024.0),
                    deps=[pt_ss], writes=[rows_trk])
        psR2_trk.read_at(pt_m2)
        pt_q = vec(lambda e: e.tensor_mul(row(8), row(0), row(0)), deps=[pt_mu])
        pt_var = vec(lambda e: e.tensor_sub(row(1), row(1), row(8)),
                     deps=[pt_m2, pt_q], writes=[rows_trk])
        pt_rs1 = rsqrt_row(row(2), row(1), deps=[pt_var], trk=rows_trk)
        dma("scalar", rows_sp[R_RS1:R_RS1 + 1, j * NT:(j + 1) * NT], row(2),
            rtrks=(rows_trk,), wtrks=(dtrk["rows_sp"][j],))
        pt_v = vec(lambda e: e.tensor_scalar_add(row(8), row(1), consts[:, 0:1]),
                   deps=[pt_var])
        pt_r = vec(lambda e: e.reciprocal(row(9), row(8)), deps=[pt_v])
        pt_v2 = vec(lambda e: e.tensor_mul(row(3), row(1), row(9)), deps=[pt_r])
        pt_rs2 = rsqrt_row(row(3), row(3), deps=[pt_v2], trk=rows_trk)
        pt_srow = vec(lambda e: e.tensor_mul(row(3), row(2), row(3)),
                      deps=[pt_rs1, pt_rs2], writes=[rows_trk])
        return pt_mu, pt_rs1, pt_srow

    def mk_apply(j, sstats):
        pt_mu, pt_rs1, pt_srow = sstats
        par = j % 2
        xt, xt_trk = x_t[par], x_t_trk[par]
        state = {"xn_pts": [None] * KC, "xc_pts": [None] * KC, "done": False}

        def step(m):
            if m == 0:
                state["mub"] = bcast(row(0), 0, deps=[pt_mu], rtrks=[rows_trk])
            if m < 4:
                for c in (2 * m, 2 * m + 1):
                    pt_xc = vec(lambda e, c=c, xt=xt: e.tensor_sub(
                        chunk(xc_t, c), chunk(xt, c), psB[0][0:128, 0:NT]),
                        deps=[state["mub"]] if c == 0 else (),
                        reads=[xt_trk] if c == 0 else (),
                        writes=[xc_trk] if c == 0 else ())
                    state["xc_pts"][c] = pt_xc
                    if c == KC - 1:
                        xc_trk.write_at(pt_xc)
                        xt_trk.read_at(pt_xc)
                        psB_trk[0].read_at(pt_xc)
            elif m == 4:
                state["rs1b"] = bcast(row(2), 0, deps=[pt_rs1, state["xc_pts"][7]],
                                      rtrks=[rows_trk])
                state["srb"] = bcast(row(3), 1, deps=[pt_srow], rtrks=[rows_trk])
            if m >= 4:
                for c in (2 * (m - 4), 2 * (m - 4) + 1):
                    pt_t1 = vec(lambda e, c=c: e.tensor_mul(
                        chunk(t1_t, c), chunk(xc_t, c), psB[0][0:128, 0:NT]),
                        deps=([state["rs1b"], state["xc_pts"][c]] if c < 2
                              else [state["xc_pts"][c]]),
                        writes=[t1_trk] if c == 0 else ())
                    pt_rr = vec(lambda e, c=c: e.tensor_mul(
                        chunk(xc_t, c), chunk(xc_t, c), psB[1][0:128, 0:NT]),
                        deps=[state["srb"], pt_t1] if c == 0 else [pt_t1])
                    xn_pt = sca(lambda e, c=c: e.activation(
                        chunk(xn_t, c), chunk(xc_t, c), AF.Identity,
                        scale=bias_col("L0_attn_norm1_w", c),
                        bias=bias_col("L0_attn_norm1_b", c)),
                        deps=[pt_rr],
                        writes=[xn_trk] if c == 0 else ())
                    state["xn_pts"][c] = xn_pt
                    if c == KC - 1:
                        t1_trk.write_at(pt_t1)
                        psB_trk[0].read_at(pt_t1)
                        xc_trk.write_at(pt_rr)
                        psB_trk[1].read_at(pt_rr)
                        xn_trk.write_at(xn_pt)
                        xc_trk.read_at(xn_pt)
                        dma("sync", dram_tile(t1_v, j), sb_tile3(t1_t),
                            rtrks=(t1_trk,), wtrks=(dtrk["t1_sp"][j],))
                        state["done"] = True
        return step, state

    def st_delta(j, apl):
        pt_dm = mm_acc(lambda k: rvec[:, k:k + 1], lambda k: chunk(xn_t, k), KC,
                       psR, psR_trk, kdeps=apl["xn_pts"],
                       rtrks_rec=[xn_trk], rtrks=[rvec_trk], rows=1)
        pt_z = vec(lambda e: e.tensor_scalar_add(rowd(0), psR[0:1, 0:NT],
                                                 mean_bs[:]),
                   deps=[pt_dm, pt_meanb], writes=[rowsd_trk])
        psR_trk.read_at(pt_z)
        pt_az = sca(lambda e: e.activation(rowd(2), rowd(0), AF.Abs),
                    deps=[pt_z])
        pt_ez = sca(lambda e: e.activation(rowd(2), rowd(2), AF.Exp, scale=-1.0),
                    deps=[pt_az])
        pt_lz = sca(lambda e: e.activation(rowd(2), rowd(2), AF.Ln,
                                           bias=consts[:, 1:2]), deps=[pt_ez])
        pt_rz = sca(lambda e: e.activation(rowd(3), rowd(0), AF.Relu),
                    deps=[pt_z])
        pt_delta = vec(lambda e: e.tensor_add(rowd(0), rowd(2), rowd(3)),
                       deps=[pt_lz, pt_rz], writes=[rowsd_trk])
        pt_dA = sca(lambda e: e.activation(rowd(1), rowd(0), AF.Exp, scale=-1.0),
                    deps=[pt_delta])
        pt_P = vec(lambda e: e.tensor_tensor_scan(rowd(3), rowd(1), zrow[:],
                                                  Pcarry[:], ALU.mult, ALU.add),
                   deps=[pt_dA], reads=[Pc_trk], writes=[rowsd_trk])
        pt_Pc = vec(lambda e: e.tensor_copy(Pcarry[:], rowd(3)[:, NT - 1:NT]),
                    deps=[pt_P], writes=[Pc_trk])
        dma("scalar", rows_sp[R_P:R_P + 1, j * NT:(j + 1) * NT], rowd(3),
            rtrks=(rowsd_trk,), wtrks=(dtrk["rows_sp"][j],))
        return pt_delta, pt_dA

    def st_gate(j, apl):
        uin_pts = [None] * KC
        gg_pts = [None] * KC
        for m in range(16):
            pb, ptrk = next_psM()
            pt_g = mm_acc(lambda k, m=m: wslice(gate_w, 2048, k, m),
                          lambda k: chunk(xn_t, k), KC, pb, ptrk,
                          kdeps=apl["xn_pts"], rtrks_rec=[xn_trk],
                          rtrks=[gate_w_trk])
            dstt, dtrk_ = (gate_s, gate_s_trk) if m < 8 else (gg_s, gg_trk)
            pe = sca(lambda e, pb=pb, c=m % 8, dstt=dstt, m=m: e.activation(
                chunk(dstt, c), pb[0:128, 0:NT], AF.Silu,
                bias=bias_col("L0_attn_gate_b", m)),
                deps=[pt_g], writes=[dtrk_])
            ptrk.read_at(pe)
            if m < 8:
                pt_ui = vec(lambda e, c=m: e.tensor_mul(
                    chunk(gate_s, c), chunk(xn_t, c), chunk(gate_s, c)),
                    deps=[pe])
                uin_pts[m] = pt_ui
                if m == 7:
                    gate_s_trk.write_at(pt_ui)
                    xn_trk.read_at(pt_ui)
            else:
                gg_pts[m - 8] = pe
        return uin_pts, gg_pts

    def st_state(j, uin_pts, gg_pts, dpts, apply_step):
        pt_delta, pt_dA = dpts
        pt_db = bcast(rowd(0), 2, deps=[pt_delta], rtrks=[rowsd_trk])
        pt_dab = bcast(rowd(1), 3, deps=[pt_dA], rtrks=[rowsd_trk])
        for m in range(8):
            pbu, ptu = next_psM()
            pt_u = mm_acc(lambda k, m=m: wslice(state_w, 1024, k, m),
                          lambda k: chunk(gate_s, k), KC, pbu, ptu,
                          kdeps=uin_pts, rtrks_rec=[gate_s_trk],
                          rtrks=[state_w_trk])
            pbb, ptb = next_psM()
            pt_bm = mm_acc(lambda k, m=m: wslice(state_w, 1024, k, m),
                           lambda k: chunk(gg_s, k), KC, pbb, ptb,
                           kdeps=gg_pts, rtrks_rec=[gg_trk],
                           rtrks=[state_w_trk])
            pt_s1 = sca(lambda e, pbu=pbu, m=m: e.add(
                s1_c[:], pbu[0:128, 0:NT], bias_col("L0_attn_state_b", m)),
                deps=[pt_u], writes=[s1_trk])
            ptu.read_at(pt_s1)
            pt_s2 = vec(lambda e, pbb=pbb, m=m: e.scalar_tensor_tensor(
                dbu_c[:], pbb[0:128, 0:NT], bias_col("L0_attn_state_b", m),
                s1_c[:], ALU.add, ALU.mult),
                deps=[pt_bm, pt_s1], reads=[s1_trk], writes=[dbu_trk])
            ptb.read_at(pt_s2)
            pt_dbu = vec(lambda e: e.tensor_mul(dbu_c[:], dbu_c[:],
                                                psB[2][0:128, 0:NT]),
                         deps=[pt_db, pt_s2], writes=[dbu_trk])
            psB_trk[2].read_at(pt_dbu)
            pt_sc = vec(lambda e, m=m: e.tensor_tensor_scan(
                chunk(y_t, m), psB[3][0:128, 0:NT], dbu_c[:],
                hstate[:, m:m + 1], ALU.mult, ALU.add),
                deps=[pt_dab, pt_dbu], reads=[hstate_trk, dbu_trk],
                writes=[y_trk])
            psB_trk[3].read_at(pt_sc)
            vec(lambda e, m=m: e.tensor_copy(hstate[:, m:m + 1],
                                             chunk(y_t, m)[:, NT - 1:NT]),
                deps=[pt_sc], reads=[y_trk], writes=[hstate_trk])
            if apply_step is not None:
                apply_step(m)
        dma("scalar", dram_tile(yloc_v, j), sb_tile3(y_t),
            rtrks=(y_trk,), wtrks=(dtrk["yloc_sp"][j],))

    sstats = st_stats(0)
    apply_step, apl = mk_apply(0, sstats)
    for m_ in range(8):
        apply_step(m_)
    dpts = st_delta(0, apl)
    for j in range(NTIL):
        up, gp = st_gate(j, apl)
        if j + 1 < NTIL:
            sstats = st_stats(j + 1)
            apply_next, apl_next = mk_apply(j + 1, sstats)
        else:
            apply_next, apl_next = None, None
        st_state(j, up, gp, dpts, apply_next)
        if apl_next is not None:
            apl = apl_next
            dpts = st_delta(j + 1, apl)

    def emit_collective():
        pr.waits("tensor", psB_trk[0].before_write())
        pr.waits("tensor", Pc_trk.before_read())
        pt_ar = pr.do("tensor", lambda e: e.matmul(
            psB[0][0:128, 0:1], ones_row[:], Pcarry[:], start=True, stop=True),
            inc="TS")
        psB_trk[0].write_at(pt_ar)
        Pc_trk.read_at(pt_ar)
        pm1 = vec(lambda e: e.tensor_copy(msg[:, 0:1], psB[0][0:128, 0:1]),
                  deps=[pt_ar], writes=[msg_trk])
        psB_trk[0].read_at(pm1)
        pm2 = vec(lambda e: e.tensor_copy(msg[:, 1:9], hstate[:]),
                  reads=[hstate_trk], writes=[msg_trk])
        dma("sync", cc_in.rearrange("(p c) -> p c", c=9), msg[:],
            rtrks=(msg_trk,), wtrks=(dtrk["cc_in"],))
        pr.waits("gpsimd", dtrk["cc_in"].before_read())
        pt_cc = pr.do("gpsimd", lambda e: e.collective_compute(
            "AllGather", ALU.bypass, replica_groups=[list(range(8))],
            ins=[cc_in], outs=[cc_out]), inc="CC")
        dtrk["cc_in"].read_at(pt_cc)
        dtrk["cc_out"].write_at(pt_cc)
        dma("scalar", gath[:].rearrange("p (r c) -> p r c", c=9),
            cc_out.rearrange("r (p c) -> p r c", c=9),
            wtrks=(gath_trk,), rtrks=(dtrk["cc_out"],))
        pt_f = pt_hin0
        for jj in range(8):
            pa = vec(lambda e, jj=jj: e.tensor_scalar(
                aeff[:], gath[:, jj * 9:jj * 9 + 1], foldm[:, jj:jj + 1],
                foldm[:, 8 + jj:9 + jj], ALU.mult, ALU.add),
                deps=[pt_f],
                reads=[gath_trk, foldm_trk] if jj == 0 else ())
            ph_ = vec(lambda e, jj=jj: e.tensor_scalar_mul(
                hmsk[:], gath[:, jj * 9 + 1:jj * 9 + 9], foldm[:, jj:jj + 1]),
                deps=[pt_f])
            pt_f = vec(lambda e: e.scalar_tensor_tensor(
                h_in[:], h_in[:], aeff[:], hmsk[:], ALU.mult, ALU.add),
                deps=[pa, ph_, pt_f])
        hin_trk.write_at(pt_f)
        gath_trk.read_at(pt_f)
        if hin_dbg is not None:
            pdbg = vec(lambda e: e.tensor_copy(msg[:, 0:8], h_in[:]), deps=[pt_f],
                       writes=[msg_trk])
            dma("sync", hin_dbg, msg[:].rearrange("p c -> p c") if False else msg[:],
                rtrks=(msg_trk,))

    retire((gate_w, gate_w_trk), (state_w, state_w_trk),
           (x_t[0], x_t_trk[0]), (x_t[1], x_t_trk[1]),
           (x_bt[0], x_bt_trk[0]), (x_bt[1], x_bt_trk[1]),
           (xc_t, xc_trk), (t1_t, t1_trk), (xn_t, xn_trk),
           (gate_s, gate_s_trk), (gg_s, gg_trk),
           (s1_c, s1_trk), (dbu_c, dbu_trk), (y_t, y_trk),
           (rvec_f, rvec_f_trk), (zrow, None),
           (rowst, rows_trk), (rowsd, rowsd_trk))
    ph1.close()

    ph2 = ES()
    sb2 = lambda nm, sh, dt=F32: ph2.enter_context(nc.sbuf_tensor(nm, sh, dt))
    t1b_t = [sb2(f"t1b{i}", [128, KC * NT]) for i in range(2)]
    yb_t = [sb2(f"yb{i}", [128, KC * NT], BF16) for i in range(2)]
    out_w = sb2("out_w", [128, KC * 1024], BF16)
    pr.new_phase()
    t1b_trk = [fresh(t1b_t[0], pr.buf_sem()), fresh(t1b_t[1], pr.buf_sem())]
    yb_trk = [fresh(yb_t[0], pr.buf_sem()), fresh(yb_t[1], pr.buf_sem())]
    out_w_trk = fresh(out_w, pr.w_sem())
    ow_pts = []
    load_weight("L0_attn_out_w", out_w, out_w_trk, 1024, pts=ow_pts)

    def ow_sl(k, m):
        return wslice(out_w, 1024, k, m)
    tt_t = [sb2(f"tt_t{i}", [128, KC * NT]) for i in range(2)]
    tt_trk = [fresh(tt_t[0], pr.buf_sem()), fresh(tt_t[1], pr.buf_sem())]
    tb_t = [sb2(f"tb_t{i}", [128, KC * NT], BF16) for i in range(2)]
    tb_trk = [fresh(tb_t[0]), fresh(tb_t[1])]
    xc2_t = sb2("xc2_t", [128, KC * NT])
    xc2_trk = fresh(xc2_t)
    x2n_t = sb2("x2n_t", [128, KC * NT], BF16)
    x2n_trk = fresh(x2n_t, pr.buf_sem())
    pbc = sb2("pbc", [128, NT])
    pbc_trk = fresh(pbc, pr.buf_sem())
    rows2 = sb2("rows2", [1, 2 * NT])
    rows2_trk = fresh(rows2)

    hin_b = sb2("hin_b", [128, KC], BF16)
    hin_b_trk = fresh(hin_b)
    w_h = sb2("w_h", [128, KC])
    w_h_trk = fresh(w_h)
    def emit_wh():
        pt_hb = vec(lambda e: e.tensor_copy(hin_b[:], h_in[:]),
                    reads=[hin_trk], writes=[hin_b_trk])
        for mc_ in range(KC):
            pb_, ptrk_ = next_psM()
            pt_wh = mm_acc(lambda k, mc_=mc_: ow_sl(k, mc_),
                           lambda k: hin_b[:, k:k + 1], KC, pb_, ptrk_,
                           deps=[pt_hb], kdeps=ow_pts,
                           rtrks_rec=[out_w_trk],
                           rtrks=[hin_b_trk], cols=1)
            pe_ = vec(lambda e, pb_=pb_, mc_=mc_: e.tensor_copy(
                w_h[:, mc_:mc_ + 1], pb_[0:128, 0:1]),
                deps=[pt_wh], writes=[w_h_trk] if mc_ == 0 else ())
            w_h_trk.write_at(pe_)
            ptrk_.read_at(pe_)

    def ph2_head(j, gen=None):
        par = j % 2
        t1t, t1trk = t1b_t[par], t1b_trk[par]
        yt, ytrk = yb_t[par], yb_trk[par]
        tt, tttrk = tt_t[par], tt_trk[par]
        dma("sync", sb_tile3(t1t), dram_tile(t1_v, j), wtrks=(t1trk,),
            rtrks=(dtrk["t1_sp"][j],))
        dma("sync", sb_tile3(yt), dram_tile(yloc_v, j), wtrks=(ytrk,),
            rtrks=(dtrk["yloc_sp"][j],))
        for m in range(8):
            pb, ptrk = next_psM()
            pt_s = mm_acc(lambda k, m=m: ow_sl(k, m),
                          lambda k, yt=yt: chunk(yt, k), KC, pb, ptrk,
                          kdeps=ow_pts,
                          rtrks_rec=[out_w_trk],
                          rtrks=[ytrk])
            pt_t = vec(lambda e, pb=pb, m=m, t1t=t1t, tt=tt:
                       e.scalar_tensor_tensor(
                chunk(tt, m), pb[0:128, 0:NT], bias_col("L0_attn_out_b", m),
                chunk(t1t, m), ALU.add, ALU.add),
                deps=[pt_s], reads=[t1trk] if m == 0 else (),
                writes=[tttrk] if m == 0 else ())
            ptrk.read_at(pt_t)
            tttrk.write_at(pt_t)
            t1trk.read_at(pt_t)
            if gen is not None:
                next(gen, None)

    def ph2_tail(j):
        par = j % 2
        tt, tttrk = tt_t[par], tt_trk[par]
        tb, tbtrk = tb_t[par], tb_trk[par]
        psrc = bass.AP(rows_sp.tensor, R_P * T + j * NT, [[0, 128], [1, NT]])
        pt_pb = dma("sync", pbc[:], psrc, wtrks=(pbc_trk,),
                    rtrks=(dtrk["rows_sp"][j],))
        pt_t2 = None
        for m in range(8):
            pt_t2 = vec(lambda e, m=m, tt=tt: e.scalar_tensor_tensor(
                chunk(tt, m), pbc[0:128, 0:NT], w_h[:, m:m + 1],
                chunk(tt, m), ALU.mult, ALU.add),
                deps=[pt_pb] if m == 0 else (),
                reads=([w_h_trk, tttrk, pbc_trk] if m == 0 else ()),
                writes=[tttrk] if m == 0 else ())
            tttrk.write_at(pt_t2)
        pbc_trk.read_at(pt_t2)
        yield
        dma("sync", dram_tile(t_v, j), sb_tile3(tt), rtrks=(tttrk,),
            wtrks=(dtrk["t_sp"][j],))
        c_sc = c_ve = None
        for m in range(KC):
            if m % 2 == 0:
                c_sc = sca(lambda e, m=m, tb=tb, tt=tt:
                           e.copy(chunk(tb, m), chunk(tt, m)),
                           reads=[tttrk] if m == 0 else (),
                           writes=[tbtrk] if m == 0 else ())
            else:
                c_ve = vec(lambda e, m=m, tb=tb, tt=tt:
                           e.tensor_copy(chunk(tb, m), chunk(tt, m)),
                           reads=[tttrk] if m == 1 else (),
                           writes=[tbtrk] if m == 1 else ())
        tbtrk.write_at(c_sc)
        tbtrk.write_at(c_ve)
        tttrk.read_at(c_sc)
        tttrk.read_at(c_ve)
        yield
        do_b = ln_stats1(tb, tbtrk, rows2[:, 0:NT],
                         rows2[:, NT:2 * NT], rows2_trk, 1, 2,
                         deps=[c_sc, c_ve], defer=True)
        yield
        pt_mub, pt_rsb = do_b()
        pt_xc = None
        for c in range(KC):
            pt_xc = vec(lambda e, c=c, tt=tt: e.tensor_sub(
                chunk(xc2_t, c), chunk(tt, c), psB[1][0:128, 0:NT]),
                deps=[pt_mub] if c == 0 else (),
                reads=[tttrk] if c == 0 else (),
                writes=[xc2_trk] if c == 0 else ())
        xc2_trk.write_at(pt_xc)
        tttrk.read_at(pt_xc)
        psB_trk[1].read_at(pt_xc)
        yield
        pt_r2 = None
        for c in range(KC):
            pt_r2 = vec(lambda e, c=c: e.tensor_mul(chunk(xc2_t, c),
                                                    chunk(xc2_t, c),
                                                    psB[2][0:128, 0:NT]),
                        deps=[pt_rsb, pt_xc] if c == 0 else ())
        xc2_trk.write_at(pt_r2)
        psB_trk[2].read_at(pt_r2)
        yield
        xn2_pt = None
        for c in range(KC):
            xn2_pt = sca(lambda e, c=c: e.activation(
                chunk(x2n_t, c), chunk(xc2_t, c), AF.Identity,
                scale=bias_col("L0_attn_norm2_w", c),
                bias=bias_col("L0_attn_norm2_b", c)),
                deps=[pt_r2] if c == 0 else (),
                writes=[x2n_trk] if c == 0 else ())
        x2n_trk.write_at(xn2_pt)
        xc2_trk.read_at(xn2_pt)
        dma("scalar", dram_tile(x2n_v, j), sb_tile3(x2n_t), rtrks=(x2n_trk,),
            wtrks=(dtrk["x2n_sp"][j],))
        yield

    def chain2(*gens):
        for g in gens:
            for x in g:
                yield x

    for j in range(NTIL):
        g2 = ph2_tail(j - 1) if j >= 2 else None
        ph2_head(j, g2)
        if j == 0:
            emit_collective()
        elif j == 1:
            emit_wh()
            for _ in ph2_tail(0):
                pass
        if g2 is not None:
            for _ in g2:
                pass
    for _ in ph2_tail(NTIL - 1):
        pass

    retire((out_w, out_w_trk), (t1b_t[0], t1b_trk[0]),
           (t1b_t[1], t1b_trk[1]), (yb_t[0], yb_trk[0]), (yb_t[1], yb_trk[1]),
           (tt_t[0], tt_trk[0]), (tt_t[1], tt_trk[1]), (tb_t[0], tb_trk[0]),
           (tb_t[1], tb_trk[1]), (xc2_t, xc2_trk), (x2n_t, x2n_trk),
           (pbc, pbc_trk), (rows2, rows2_trk), (hin_b, hin_b_trk),
           (w_h, w_h_trk))
    ph2.close()
    retire((hstate, hstate_trk), (h_in, hin_trk), (Pcarry, Pc_trk),
           (foldm, foldm_trk), (mean_bs, meanb_trk), (rvec, rvec_trk),
           (gath, gath_trk), (msg, msg_trk), (aeff, None), (hmsk, None))
    ph12.close()

    ph3 = ES()
    sb3 = lambda nm, sh, dt=F32: ph3.enter_context(nc.sbuf_tensor(nm, sh, dt))
    x2nb_t = [sb3(f"x2nb{i}", [128, KC * NT], BF16) for i in range(2)]
    w1 = sb3("mlp_w1", [128, KC * 4096], BF16)
    w2 = sb3("mlp_w2", [128, 32 * 1024], BF16)
    pr.new_phase()
    x2nb_trk = [fresh(x2nb_t[0], pr.buf_sem()), fresh(x2nb_t[1], pr.buf_sem())]
    w1_trk, w2_trk = fresh(w1, pr.w_sem()), fresh(w2, pr.w_sem())
    w1_pts = []
    load_weight("L0_attn_mlp_w1", w1, w1_trk, 4096, pts=w1_pts)
    pw2 = load_weight("L0_attn_mlp_w2", w2, w2_trk, 1024)

    def w1_sl(k, mc):
        return wslice(w1, 4096, k, mc)
    h_t = sb3("h_t", [128, 32 * NT], BF16)
    h_trk = fresh(h_t)
    sc_c = [sb3(f"sc_c{i}", [128, NT]) for i in range(2)]
    sc_trk = [fresh(sc_c[0], pr.buf_sem()), fresh(sc_c[1], pr.buf_sem())]
    ao_t = sb3("ao_t", [128, KC * NT])
    ao_trk = fresh(ao_t, pr.buf_sem())
    rows3 = sb3("rows3", [1, 2 * NT])
    rows3_trk = fresh(rows3)

    def row3(i):
        return rows3[:, i * NT:(i + 1) * NT]

    def ph3_headA(j, gen=None):
        par = j % 2
        xb, xbtrk = x2nb_t[par], x2nb_trk[par]
        dma("sync", sb_tile3(xb), dram_tile(x2n_v, j), wtrks=(xbtrk,),
            rtrks=(dtrk["x2n_sp"][j],))
        h_pts = [None] * 32
        for mc in range(32):
            pb, ptrk = next_psM()
            pt_h = mm_acc(lambda k, mc=mc: w1_sl(k, mc),
                          lambda k, xb=xb: chunk(xb, k), KC, pb, ptrk,
                          kdeps=w1_pts, rtrks_rec=[w1_trk],
                          rtrks=[xbtrk])
            pe = sca(lambda e, pb=pb, mc=mc: e.activation(
                chunk(h_t, mc), pb[0:128, 0:NT], AF.Gelu,
                bias=bias_col("L0_attn_mlp_b1", mc)),
                deps=[pt_h], writes=[h_trk] if mc == 0 else ())
            h_trk.write_at(pe)
            ptrk.read_at(pe)
            h_pts[mc] = pe
            if gen is not None and mc % 3 == 2:
                next(gen, None)
        return h_pts

    def ph3_headB(j, h_pts):
        for m in range(8):
            pb, ptrk = next_psM()
            pt_o = mm_acc(lambda k, m=m: wslice(w2, 1024, k, m),
                          lambda k: chunk(h_t, k), 32, pb, ptrk,
                          deps=[pw2], kdeps=h_pts, rtrks_rec=[h_trk],
                          rtrks=[w2_trk])
            sc, sct = sc_c[m % 2], sc_trk[m % 2]
            dma("sync", sc[:], t_v[:, m, j * NT:(j + 1) * NT],
                wtrks=(sct,), rtrks=(dtrk["t_sp"][j],))
            pt_a = vec(lambda e, pb=pb, m=m, sc=sc: e.scalar_tensor_tensor(
                chunk(ao_t, m), pb[0:128, 0:NT], bias_col("L0_attn_mlp_b2", m),
                sc[:], ALU.add, ALU.add),
                deps=[pt_o], reads=[sct],
                writes=[ao_trk] if m == 0 else ())
            ptrk.read_at(pt_a)
            ao_trk.write_at(pt_a)

    def ph3_tailprod(j):
        par = j % 2
        stg, stgtrk = x2nb_t[par], x2nb_trk[par]
        c_sc = c_ve = None
        for c in range(KC):
            if c % 2 == 0:
                c_sc = sca(lambda e, c=c, stg=stg: e.copy(chunk(stg, c),
                                                          chunk(ao_t, c)),
                           reads=[ao_trk] if c == 0 else (),
                           writes=[stgtrk] if c == 0 else ())
            else:
                c_ve = vec(lambda e, c=c, stg=stg: e.tensor_copy(
                    chunk(stg, c), chunk(ao_t, c)),
                    reads=[ao_trk] if c == 1 else (),
                    writes=[stgtrk] if c == 1 else ())
        stgtrk.write_at(c_sc)
        stgtrk.write_at(c_ve)
        ao_trk.read_at(c_sc)
        ao_trk.read_at(c_ve)
        yield
        do_b = ln_stats1(stg, stgtrk, row3(0), row3(1),
                         rows3_trk, 0, 1, deps=[c_sc, c_ve], defer=True)
        yield
        pt_mub, pt_rsb = do_b()
        pt_xc = None
        for c in range(KC):
            pt_xc = vec(lambda e, c=c: e.tensor_sub(chunk(ao_t, c),
                                                    chunk(ao_t, c),
                                                    psB[0][0:128, 0:NT]),
                        deps=[pt_mub] if c == 0 else (),
                        writes=[ao_trk] if c == 0 else ())
        ao_trk.write_at(pt_xc)
        psB_trk[0].read_at(pt_xc)
        yield
        pt_x1 = None
        for c in range(KC):
            pt_m = vec(lambda e, c=c: e.tensor_mul(chunk(ao_t, c),
                                                   chunk(ao_t, c),
                                                   psB[1][0:128, 0:NT]),
                       deps=[pt_rsb, pt_xc] if c == 0 else ())
            sc, sct = sc_c[c % 2], sc_trk[c % 2]
            dma("sync", sc[:], x_v[:, c, j * NT:(j + 1) * NT],
                wtrks=(sct,))
            pt_x1 = vec(lambda e, c=c, sc=sc: e.tensor_add(chunk(ao_t, c),
                                                           chunk(ao_t, c),
                                                           sc[:]),
                        deps=[pt_m], reads=[sct])
        ao_trk.write_at(pt_x1)
        psB_trk[1].read_at(pt_x1)
        dma("sync", dram_tile(x1_v, j), sb_tile3(ao_t), rtrks=(ao_trk,),
            wtrks=(dtrk["x1_sp"][j],))
        yield
        c_sc = c_ve = None
        for c in range(KC):
            if c % 2 == 0:
                c_sc = sca(lambda e, c=c, stg=stg: e.copy(chunk(stg, c),
                                                          chunk(ao_t, c)),
                           reads=[ao_trk] if c == 0 else (),
                           writes=[stgtrk] if c == 0 else ())
            else:
                c_ve = vec(lambda e, c=c, stg=stg: e.tensor_copy(
                    chunk(stg, c), chunk(ao_t, c)),
                    reads=[ao_trk] if c == 1 else (),
                    writes=[stgtrk] if c == 1 else ())
        stgtrk.write_at(c_sc)
        stgtrk.write_at(c_ve)
        ao_trk.read_at(c_sc)
        ao_trk.read_at(c_ve)
        yield
        do_b = ln_stats1(stg, stgtrk, row3(0), row3(1),
                         rows3_trk, 0, 1, deps=[c_sc, c_ve], defer=True)
        yield
        pt_mub, pt_rsb = do_b()
        pt_xc = None
        for c in range(KC):
            pt_xc = vec(lambda e, c=c: e.tensor_sub(chunk(ao_t, c),
                                                    chunk(ao_t, c),
                                                    psB[0][0:128, 0:NT]),
                        deps=[pt_mub] if c == 0 else (),
                        writes=[ao_trk] if c == 0 else ())
        ao_trk.write_at(pt_xc)
        psB_trk[0].read_at(pt_xc)
        yield
        pt_n = None
        for c in range(KC):
            pt_n = vec(lambda e, c=c, stg=stg: e.tensor_mul(
                chunk(stg, c), chunk(ao_t, c), psB[1][0:128, 0:NT]),
                deps=[pt_rsb, pt_xc] if c == 0 else (),
                writes=[stgtrk] if c == 0 else ())
        stgtrk.write_at(pt_n)
        ao_trk.read_at(pt_n)
        psB_trk[1].read_at(pt_n)
        dma("scalar", dram_tile(x1n_v, j), sb_tile3(stg), rtrks=(stgtrk,),
            wtrks=(dtrk["x1n_sp"][j],))
        yield

    gen3 = None
    for j in range(NTIL):
        gen3 = ph3_tailprod(j - 1) if j > 0 else None
        h_pts = ph3_headA(j, gen3)
        if gen3 is not None:
            for _ in gen3:
                pass
        ph3_headB(j, h_pts)
    for _ in ph3_tailprod(NTIL - 1):
        pass

    retire((w1, w1_trk), (w2, w2_trk), (x2nb_t[0], x2nb_trk[0]),
           (x2nb_t[1], x2nb_trk[1]), (h_t, h_trk), (sc_c[0], sc_trk[0]), (sc_c[1], sc_trk[1]),
           (ao_t, ao_trk), (rows3, rows3_trk))
    ph3.close()

    aold_c = [sb(f"aold_c{i}", [128, NT]) for i in range(2)]
    aold_trk = [fresh(aold_c[0], pr.buf_sem()), fresh(aold_c[1], pr.buf_sem())]
    wbc = [sb(f"wbc{i}", [128, NT]) for i in range(2)]
    wbct = [fresh(wbc[0], pr.buf_sem()), fresh(wbc[1], pr.buf_sem())]

    def wrow_dma_bcast(par, ridx, j, q="sync"):
        src = bass.AP(rows_sp.tensor, ridx * T + j * NT, [[0, 128], [1, NT]])
        return dma(q, wbc[par][:], src, wtrks=(wbct[par],),
                   rtrks=(dtrk["rows_sp"][j],))

    def acc_evac_rmw(pb, ptrk, m, j, bias_name, pt_wb, src_acc,
                     dst_t, dst_trk, pt_o=None):
        ac, at = aold_c[m % 2], aold_trk[m % 2]
        wb, wbt = wbc[j % 2], wbct[j % 2]
        dma("sync", ac[:], acc_v[src_acc][:, m, j * NT:(j + 1) * NT],
            wtrks=(at,), rtrks=(dtrk[f"acc{src_acc}_sp"][j],))
        pt_t = vec(lambda e, pb=pb, m=m, wb=wb: e.scalar_tensor_tensor(
            chunk(dst_t, m), pb[0:128, 0:NT], bias_col(bias_name, m),
            wb[0:128, 0:NT], ALU.add, ALU.mult),
            deps=[pt_o, pt_wb], reads=[wbt],
            writes=[dst_trk] if m == 0 else ())
        dst_trk.write_at(pt_t)
        ptrk.read_at(pt_t)
        pt_a = vec(lambda e, m=m, ac=ac: e.tensor_add(
            chunk(dst_t, m), chunk(dst_t, m), ac[:]),
            deps=[pt_t], reads=[at])
        dst_trk.write_at(pt_a)
        return pt_a

    def ph_e0(half):
        ph = ES()
        sbp = lambda nm, sh, dt=F32: ph.enter_context(
            nc.sbuf_tensor(f"{nm}_{half}", sh, dt))
        lo, hi = (0, 2048) if half == 0 else (2048, 4096)
        xn8_t = [sbp(f"e0x{i}", [128, KC * NT], BF16) for i in range(2)]
        if half == 0:
            rw = sbp("router_w", [128, KC * 4], BF16)
            ident4 = sbp("ident4", [128, 128])
            lsb = sbp("lsb", [4, NT])
            lt = sbp("lt", [128, 16])
            gbuf = sbp("gbuf", [128, 64])
            wtok = sbp("wtok", [128, 12])
            wrows = sbp("wrows", [4, NT])
        w1h = sbp("sw1h", [128, KC * 2048], BF16)
        w3h = sbp("sw3h", [128, KC * 2048], BF16)
        w2h = sbp("sw2h", [128, 16 * 1024], BF16)
        pr.new_phase()
        xn8_trk = [fresh(xn8_t[0], pr.buf_sem()),
                   fresh(xn8_t[1], pr.buf_sem())]
        if half == 0:
            rwt = fresh(rw, pr.w_sem())
            pwr = load_weight("L0_ffn_router_w", rw, rwt, 4)
            ident4_trk = fresh(ident4, pr.buf_sem())
            dma("sync", ident4[:], ident_d, wtrks=(ident4_trk,))
            lsb_trk = fresh(lsb)
            lt_trk = fresh(lt)
            gbuf_trk = fresh(gbuf)
            wtok_trk = fresh(wtok)
            wrows_trk = fresh(wrows, pr.buf_sem())
        w1t, w3t, w2t = (fresh(w1h, pr.w_sem()), fresh(w3h, pr.w_sem()),
                         fresh(w2h, pr.w_sem()))
        pww1 = load_weight("L0_ffn_sw_w1", w1h, w1t, 2048, src_lo=lo, src_hi=hi)
        pww3 = load_weight("L0_ffn_sw_w3", w3h, w3t, 2048, src_lo=lo, src_hi=hi)
        pww2 = load_weight("L0_ffn_sw_w2", w2h, w2t, 1024,
                           krange=range(lo // 128, hi // 128))
        h0s = sbp("h0s", [128, 16 * NT], BF16)
        h0s_trk = fresh(h0s)
        acc_t = sbp("e0acc", [128, KC * NT])
        acc_trk = fresh(acc_t, pr.buf_sem())

        def e0_router(j, xn8, xn8t, cell):
            pt_l = mm_acc(lambda k: rw[:, k * 4:(k + 1) * 4],
                          lambda k, xn8=xn8: chunk(xn8, k), KC, psR, psR_trk,
                          deps=[pwr], rtrks_rec=[],
                          rtrks=[rwt, xn8t], rows=4)
            pt_le = sca(lambda e: e.activation(lsb[:], psR[0:4, 0:NT],
                                               AF.Identity, bias=biasrl[:]),
                        deps=[pt_l], reads=[biasrl_trk],
                        writes=[lsb_trk])
            psR_trk.read_at(pt_le)
            yield
            pr.waits("tensor", psB_trk[2].before_write())
            pr.waits("tensor", ident4_trk.before_read())
            pr.waits("tensor", lsb_trk.before_read())
            pt_tr = None
            for g in range(4):
                pt_tr = pr.do("tensor", lambda e, g=g: e.transpose(
                    psB[2][0:128, g * 4:(g + 1) * 4],
                    lsb[0:4, g * 128:(g + 1) * 128], ident4[0:4, 0:4]),
                    inc="TS" if g == 3 else None)
            psB_trk[2].write_at(pt_tr)
            lsb_trk.read_at(pt_tr)
            ident4_trk.read_at(pt_tr)
            pt_lt = vec(lambda e: e.tensor_copy(lt[:], psB[2][0:128, 0:16]),
                        deps=[pt_tr], writes=[lt_trk])
            psB_trk[2].read_at(pt_lt)
            yield
            lt3 = lt[:].rearrange("p (g e) -> p g e", e=4)
            pairs = [(0, 1), (0, 2), (0, 3), (1, 2), (1, 3), (2, 3)]
            cmp_pts = {}
            for pi, (i, jd) in enumerate(pairs):
                vec(lambda e, pi=pi, i=i, jd=jd: e.tensor_tensor(
                    gbuf[:, pi * 4:(pi + 1) * 4], lt3[:, :, i], lt3[:, :, jd],
                    ALU.is_ge),
                    deps=[pt_lt],
                    writes=[gbuf_trk] if pi == 0 else ())
                cmp_pts[pi] = vec(lambda e, pi=pi, i=i, jd=jd: e.tensor_tensor(
                    gbuf[:, 24 + pi * 4:24 + (pi + 1) * 4],
                    lt3[:, :, i], lt3[:, :, jd], ALU.is_lt),
                    deps=[pt_lt])
            gbuf_trk.write_at(cmp_pts[5])
            seen = set()
            pt_rank = cmp_pts[5]
            for pi, (i, jd) in enumerate(pairs):
                for tgt, src_col in ((jd, pi * 4), (i, 24 + pi * 4)):
                    col = 48 + tgt * 4
                    if tgt not in seen:
                        seen.add(tgt)
                        pt_rank = vec(lambda e, col=col, sc=src_col:
                                      e.tensor_copy(
                            gbuf[:, col:col + 4], gbuf[:, sc:sc + 4]),
                            deps=[cmp_pts[pi], pt_rank])
                    else:
                        pt_rank = vec(lambda e, col=col, sc=src_col:
                                      e.tensor_add(
                            gbuf[:, col:col + 4], gbuf[:, col:col + 4],
                            gbuf[:, sc:sc + 4]),
                            deps=[cmp_pts[pi], pt_rank])
            gbuf_trk.write_at(pt_rank)
            pt_msk = vec(lambda e: e.tensor_single_scalar(
                gbuf[:, 48:64], gbuf[:, 48:64], 1.5, ALU.is_le),
                deps=[pt_rank], writes=[gbuf_trk])
            yield
            pt_mx = vec(lambda e: e.reduce_max(wtok[:, 0:4], lt3, axis=AX.X),
                        deps=[pt_lt], writes=[wtok_trk])
            mx_b = bass.AP(wtok, 0, [[12, 128], [1, 4], [0, 4]])
            pt_sub = vec(lambda e: e.tensor_sub(lt3, lt3, mx_b),
                         deps=[pt_mx], reads=[wtok_trk], writes=[lt_trk])
            pt_ex = sca(lambda e: e.activation(lt[:], lt[:], AF.Exp),
                        deps=[pt_sub], writes=[lt_trk])
            msk_b = bass.AP(gbuf, 48, [[64, 128], [1, 4], [4, 4]])
            pt_em = vec(lambda e: e.tensor_mul(lt3, lt3, msk_b),
                        deps=[pt_ex, pt_msk], writes=[lt_trk])
            gbuf_trk.read_at(pt_em)
            pt_s = vec(lambda e: e.reduce_sum(wtok[:, 4:8], lt3, axis=AX.X),
                       deps=[pt_em], writes=[wtok_trk])
            pt_rc = vec(lambda e: e.reciprocal(wtok[:, 8:12], wtok[:, 4:8]),
                        deps=[pt_s], writes=[wtok_trk])
            rc_b = bass.AP(wtok, 8, [[12, 128], [1, 4], [0, 4]])
            pt_w = vec(lambda e: e.tensor_mul(lt3, lt3, rc_b),
                       deps=[pt_rc], writes=[lt_trk])
            wtok_trk.read_at(pt_w)
            yield
            pr.waits("tensor", psB_trk[3].before_write())
            pr.waits("tensor", lt_trk.before_read())
            pt_tb = None
            for g in range(4):
                pt_tb = pr.do("tensor", lambda e, g=g: e.transpose(
                    psB[3][0:4, g * 128:(g + 1) * 128],
                    lt[:, g * 4:(g + 1) * 4], ident4[:]),
                    inc="TS" if g == 3 else None)
            psB_trk[3].write_at(pt_tb)
            lt_trk.read_at(pt_tb)
            ident4_trk.read_at(pt_tb)
            pt_wr = vec(lambda e: e.tensor_copy(wrows[:], psB[3][0:4, 0:NT]),
                        deps=[pt_tb], writes=[wrows_trk])
            psB_trk[3].read_at(pt_wr)
            for ei in range(4):
                dma("scalar", rows_sp[R_W0 + ei:R_W0 + ei + 1,
                                      j * NT:(j + 1) * NT],
                    wrows[ei:ei + 1, :], rtrks=(wrows_trk,),
                    wtrks=(dtrk["rows_sp"][j],))
            cell["pt"] = wrow_dma_bcast(j % 2, R_W0, j)
            yield

        def e0_gemms(j, pt_w0b, xn8, xn8t, xn_pts=None, rgen=None,
                     cell=None):
            h0_pts = [None] * 16
            for mc in range(16):
                pb1, pt1 = next_psM()
                pm1 = mm_acc(lambda k, mc=mc: wslice(w1h, 2048, k, mc),
                             lambda k, xn8=xn8: chunk(xn8, k), KC, pb1, pt1,
                             deps=[pww1], kdeps=xn_pts, rtrks_rec=[xn8t],
                             rtrks=[w1t])
                pb2, pt2 = next_psM()
                pm2 = mm_acc(lambda k, mc=mc: wslice(w3h, 2048, k, mc),
                             lambda k, xn8=xn8: chunk(xn8, k), KC, pb2, pt2,
                             deps=[pww3], kdeps=xn_pts, rtrks_rec=[xn8t],
                             rtrks=[w3t])
                ps_ = sca(lambda e, pb1=pb1, mc=mc: e.activation(
                    chunk(h0s, mc), pb1[0:128, 0:NT], AF.Silu),
                    deps=[pm1], writes=[h0s_trk] if mc == 0 else ())
                h0s_trk.write_at(ps_)
                pt1.read_at(ps_)
                pv_ = vec(lambda e, pb2=pb2, mc=mc: e.tensor_mul(
                    chunk(h0s, mc), chunk(h0s, mc), pb2[0:128, 0:NT]),
                    deps=[pm2, ps_], writes=[h0s_trk])
                pt2.read_at(pv_)
                h0_pts[mc] = pv_
                if rgen is not None and mc % 3 == 2:
                    next(rgen, None)
            if rgen is not None:
                for _ in rgen:
                    pass
                pt_w0b = cell["pt"]
            wb, wbt = wbc[j % 2], wbct[j % 2]
            for m in range(8):
                pb, ptrk = next_psM()
                pt_o = mm_acc(lambda k, m=m: wslice(w2h, 1024, k, m),
                              lambda k: chunk(h0s, k), 16, pb, ptrk,
                              deps=[pww2], kdeps=h0_pts, rtrks_rec=[h0s_trk],
                              rtrks=[w2t])
                if half == 0:
                    pt_a = vec(lambda e, m=m, pb=pb, wb=wb: e.tensor_mul(
                        chunk(acc_t, m), pb[0:128, 0:NT], wb[0:128, 0:NT]),
                        deps=[pt_o, pt_w0b], reads=[wbt],
                        writes=[acc_trk] if m == 0 else ())
                    acc_trk.write_at(pt_a)
                    ptrk.read_at(pt_a)
                else:
                    ac, at = aold_c[m % 2], aold_trk[m % 2]
                    dma("sync", ac[:], acc_v[0][:, m, j * NT:(j + 1) * NT],
                        wtrks=(at,), rtrks=(dtrk["acc0_sp"][j],))
                    pt_t = vec(lambda e, m=m, pb=pb, wb=wb: e.tensor_mul(
                        chunk(acc_t, m), pb[0:128, 0:NT], wb[0:128, 0:NT]),
                        deps=[pt_o, pt_w0b], reads=[wbt],
                        writes=[acc_trk] if m == 0 else ())
                    acc_trk.write_at(pt_t)
                    ptrk.read_at(pt_t)
                    pt_a = vec(lambda e, m=m, ac=ac: e.tensor_add(
                        chunk(acc_t, m), chunk(acc_t, m), ac[:]),
                        deps=[pt_t], reads=[at])
                    acc_trk.write_at(pt_a)
            dma("scalar", dram_tile(acc_v[0 if half == 0 else 1], j),
                sb_tile3(acc_t), rtrks=(acc_trk,),
                wtrks=(dtrk[f"acc{0 if half == 0 else 1}_sp"][j],))

        for j in range(NTIL):
            par = j % 2
            xn8, xn8t = xn8_t[par], xn8_trk[par]
            dma("sync", sb_tile3(xn8), dram_tile(x1n_v, j), wtrks=(xn8t,),
                rtrks=(dtrk["x1n_sp"][j],))
            if half == 0:
                cell = {}
                rgen = e0_router(j, xn8, xn8t, cell)
                e0_gemms(j, None, xn8, xn8t, rgen=rgen, cell=cell)
            else:
                pt_w0b = wrow_dma_bcast(j % 2, R_W0, j)
                e0_gemms(j, pt_w0b, xn8, xn8t)
        pairs = [(w1h, w1t), (w3h, w3t), (w2h, w2t), (h0s, h0s_trk),
                 (acc_t, acc_trk),
                 (xn8_t[0], xn8_trk[0]), (xn8_t[1], xn8_trk[1])]
        if half == 0:
            pairs += [(rw, rwt), (ident4, ident4_trk), (lsb, lsb_trk),
                      (lt, lt_trk), (gbuf, gbuf_trk), (wtok, wtok_trk),
                      (wrows, wrows_trk)]
        retire(*pairs)
        ph.close()

    ph_e0(0)
    ph_e0(1)

    def proj8(wsb, wtrk, pw, src_t, src_trk, dcols=1024, nk=KC, evac=None,
              kdeps=None, gen=None):
        pts = []
        for m in range(8):
            pb, ptrk = next_psM()
            if kdeps is None:
                pt_o = mm_acc(lambda k, m=m: wslice(wsb, dcols, k, m),
                              lambda k: chunk(src_t, k), nk, pb, ptrk,
                              deps=[pw], rtrks=[src_trk, wtrk])
            else:
                pt_o = mm_acc(lambda k, m=m: wslice(wsb, dcols, k, m),
                              lambda k: chunk(src_t, k), nk, pb, ptrk,
                              deps=[pw], kdeps=kdeps, rtrks_rec=[src_trk],
                              rtrks=[wtrk])
            pts.append(evac(m, pb, ptrk, pt_o))
            if gen is not None:
                next(gen, None)
        return pts

    ph5 = ES()
    sb5 = lambda nm, sh, dt=F32: ph5.enter_context(nc.sbuf_tensor(nm, sh, dt))
    eqw = sb5("eqw", [128, KC * 1024], BF16)
    mwv = sb5("me_wv", [128, KC * 1024], BF16)
    mwo = sb5("me_wo", [128, KC * 1024], BF16)
    c1w = sb5("c1w", [128, KC * 2048], BF16)
    c2w = sb5("c2w", [128, 16 * 1024], BF16)
    pr.new_phase()
    t5 = [fresh(w_, pr.w_sem()) for w_ in (eqw, mwv, mwo, c1w, c2w)]
    pw5 = [load_weight("L0_ffn_me_eq_w", eqw, t5[0], 1024),
           load_weight("L0_ffn_me_wv", mwv, t5[1], 1024),
           load_weight("L0_ffn_me_wo", mwo, t5[2], 1024),
           load_weight("L0_ffn_me_c1w", c1w, t5[3], 2048),
           load_weight("L0_ffn_me_c2w", c2w, t5[4], 1024)]
    x5 = sb5("x5", [128, KC * NT], BF16)
    x5_trk = fresh(x5, pr.buf_sem())
    eq_t = sb5("eq_t", [128, KC * NT], BF16)
    eq_trk = fresh(eq_t)
    v_t = sb5("v_t", [128, KC * NT], BF16)
    v_trk = fresh(v_t)
    sym_t = sb5("sym_t", [128, KC * NT], BF16)
    sym_trk = fresh(sym_t)
    h1_t = sb5("h1_t", [128, 16 * NT], BF16)
    h1_trk = fresh(h1_t)
    acc5_t = sb5("acc5_t", [128, KC * NT])
    acc5_trk = fresh(acc5_t, pr.buf_sem())
    for j in range(NTIL):
        dma("sync", sb_tile3(x5), dram_tile(x1n_v, j), wtrks=(x5_trk,),
            rtrks=(dtrk["x1n_sp"][j],))
        pt_wb = wrow_dma_bcast(j % 2, R_W1, j)

        def evac_act(dst_t, dst_trk, bias_name, func=AF.Identity):
            def f(m, pb, ptrk, pt_o):
                pe = sca(lambda e, pb=pb, m=m: e.activation(
                    chunk(dst_t, m), pb[0:128, 0:NT], func,
                    bias=bias_col(bias_name, m)),
                    deps=[pt_o], writes=[dst_trk] if m == 0 else ())
                dst_trk.write_at(pe)
                ptrk.read_at(pe)
                return pe
            return f

        p_eq = proj8(eqw, t5[0], pw5[0], x5, x5_trk,
                     evac=evac_act(eq_t, eq_trk, "L0_ffn_me_eq_b"))
        p_v = proj8(mwv, t5[1], pw5[1], eq_t, eq_trk, kdeps=p_eq,
                    evac=evac_act(v_t, v_trk, "L0_ffn_me_bv"))
        p_sym = proj8(mwo, t5[2], pw5[2], v_t, v_trk, kdeps=p_v,
                      evac=evac_act(sym_t, sym_trk, "L0_ffn_me_bo"))
        h1_pts = [None] * 16
        for mc in range(16):
            pb, ptrk = next_psM()
            pt_h = mm_acc(lambda k, mc=mc: wslice(c1w, 2048, k, mc),
                          lambda k: chunk(sym_t, k), KC, pb, ptrk,
                          deps=[pw5[3]], kdeps=p_sym, rtrks_rec=[sym_trk],
                          rtrks=[t5[3]])
            pe = sca(lambda e, pb=pb, mc=mc: e.activation(
                chunk(h1_t, mc), pb[0:128, 0:NT], AF.Gelu,
                bias=bias_col("L0_ffn_me_c1b", mc)),
                deps=[pt_h], writes=[h1_trk] if mc == 0 else ())
            h1_trk.write_at(pe)
            ptrk.read_at(pe)
            h1_pts[mc] = pe
        for m in range(8):
            pb, ptrk = next_psM()
            pt_o = mm_acc(lambda k, m=m: wslice(c2w, 1024, k, m),
                          lambda k: chunk(h1_t, k), 16, pb, ptrk,
                          deps=[pw5[4]], kdeps=h1_pts, rtrks_rec=[h1_trk],
                          rtrks=[t5[4]])
            acc_evac_rmw(pb, ptrk, m, j, "L0_ffn_me_c2b", pt_wb, 1,
                         acc5_t, acc5_trk, pt_o=pt_o)
        dma("scalar", dram_tile(acc_v[0], j), sb_tile3(acc5_t),
            rtrks=(acc5_trk,), wtrks=(dtrk["acc0_sp"][j],))

    retire((eqw, t5[0]), (mwv, t5[1]), (mwo, t5[2]), (c1w, t5[3]),
           (c2w, t5[4]), (x5, x5_trk), (eq_t, eq_trk), (v_t, v_trk),
           (sym_t, sym_trk), (h1_t, h1_trk), (acc5_t, acc5_trk))
    ph5.close()

    ph6 = ES()
    sb6 = lambda nm, sh, dt=F32: ph6.enter_context(nc.sbuf_tensor(nm, sh, dt))
    synw = sb6("synw", [128, KC * 1024], BF16)
    cwv = sb6("ce_wv", [128, KC * 1024], BF16)
    cwo = sb6("ce_wo", [128, KC * 1024], BF16)
    pr.new_phase()
    t6 = [fresh(w_, pr.w_sem()) for w_ in (synw, cwv, cwo)]
    pw6 = [load_weight("L0_ffn_ce_syn_w", synw, t6[0], 1024),
           load_weight("L0_ffn_ce_wv", cwv, t6[1], 1024),
           load_weight("L0_ffn_ce_wo", cwo, t6[2], 1024)]
    x6_t = [sb6(f"x6_{i}", [128, KC * NT], BF16) for i in range(2)]
    x6_trk = [fresh(x6_t[0], pr.buf_sem()), fresh(x6_t[1], pr.buf_sem())]
    syn_t = sb6("syn_t", [128, KC * NT])
    syn_trk = fresh(syn_t)
    synb_t = sb6("synb_t", [128, KC * NT], BF16)
    synb_trk = fresh(synb_t)
    v6_t = sb6("v6_t", [128, KC * NT], BF16)
    v6_trk = fresh(v6_t)
    s1_t = [sb6(f"s1_t6{i}", [128, KC * NT]) for i in range(2)]
    s1t_trk = [fresh(s1_t[0]), fresh(s1_t[1])]
    s1b_t = [sb6(f"s1b_t{i}", [128, KC * NT], BF16) for i in range(2)]
    s1b_trk = [fresh(s1b_t[0]), fresh(s1b_t[1])]
    h2_t = sb6("h2_t", [128, KC * NT], BF16)
    h2t_trk = fresh(h2_t, pr.buf_sem())
    rows6 = sb6("rows6", [1, 2 * NT])
    rows6_trk = fresh(rows6)

    def ph6_head(j, gen=None):
        par = j % 2
        x6, x6t = x6_t[par], x6_trk[par]
        s1, s1t = s1_t[par], s1t_trk[par]
        dma("sync", sb_tile3(x6), dram_tile(x1n_v, j), wtrks=(x6t,),
            rtrks=(dtrk["x1n_sp"][j],))

        def evac_syn(m, pb, ptrk, pt_o):
            pe = sca(lambda e, pb=pb, m=m: e.add(
                chunk(syn_t, m), pb[0:128, 0:NT], bias_col("L0_ffn_ce_syn_b", m)),
                deps=[pt_o], writes=[syn_trk] if m == 0 else ())
            syn_trk.write_at(pe)
            ptrk.read_at(pe)
            pc = sca(lambda e, m=m: e.copy(chunk(synb_t, m), chunk(syn_t, m)),
                     deps=[pe], writes=[synb_trk] if m == 0 else ())
            synb_trk.write_at(pc)
            return pc

        p_syn = proj8(synw, t6[0], pw6[0], x6, x6t, evac=evac_syn, gen=gen)

        def evac_v(m, pb, ptrk, pt_o):
            pe = sca(lambda e, pb=pb, m=m: e.activation(
                chunk(v6_t, m), pb[0:128, 0:NT], AF.Identity,
                bias=bias_col("L0_ffn_ce_bv", m)),
                deps=[pt_o], writes=[v6_trk] if m == 0 else ())
            v6_trk.write_at(pe)
            ptrk.read_at(pe)
            return pe

        p_v6 = proj8(cwv, t6[1], pw6[1], synb_t, synb_trk, kdeps=p_syn,
                     evac=evac_v)

        def evac_s1(m, pb, ptrk, pt_o):
            pt_s = vec(lambda e, pb=pb, m=m, s1=s1: e.scalar_tensor_tensor(
                chunk(s1, m), pb[0:128, 0:NT], bias_col("L0_ffn_ce_bo", m),
                chunk(syn_t, m), ALU.add, ALU.add),
                deps=[pt_o], reads=[syn_trk] if m == 0 else (),
                writes=[s1t] if m == 0 else ())
            s1t.write_at(pt_s)
            syn_trk.read_at(pt_s)
            ptrk.read_at(pt_s)
            return pt_s

        proj8(cwo, t6[2], pw6[2], v6_t, v6_trk, kdeps=p_v6, evac=evac_s1)

    def ph6_tail(j):
        par = j % 2
        s1, s1t = s1_t[par], s1t_trk[par]
        s1b, s1bt = s1b_t[par], s1b_trk[par]
        cast_pt = None
        for c in range(KC):
            cast_pt = sca(lambda e, c=c, s1=s1, s1b=s1b: e.copy(
                chunk(s1b, c), chunk(s1, c)),
                reads=[s1t] if c == 0 else (),
                writes=[s1bt] if c == 0 else ())
        s1bt.write_at(cast_pt)
        s1t.read_at(cast_pt)
        yield
        do_b = ln_stats1(s1b, s1bt, rows6[:, 0:NT],
                         rows6[:, NT:2 * NT],
                         rows6_trk, 0, 1, deps=[cast_pt],
                         src_f32=s1, src_trk=s1t, defer=True)
        yield
        pt_mub, pt_rsb = do_b()
        pt_xc = None
        for c in range(KC):
            pt_xc = vec(lambda e, c=c, s1=s1: e.tensor_sub(
                chunk(s1, c), chunk(s1, c), psB[0][0:128, 0:NT]),
                deps=[pt_mub] if c == 0 else (),
                writes=[s1t] if c == 0 else ())
        s1t.write_at(pt_xc)
        psB_trk[0].read_at(pt_xc)
        yield
        pt_m = None
        for c in range(KC):
            pt_m = vec(lambda e, c=c, s1=s1: e.tensor_mul(
                chunk(s1, c), chunk(s1, c), psB[1][0:128, 0:NT]),
                deps=[pt_rsb, pt_xc] if c == 0 else ())
        s1t.write_at(pt_m)
        psB_trk[1].read_at(pt_m)
        yield
        h2_pt = None
        for c in range(KC):
            h2_pt = sca(lambda e, c=c, s1=s1: e.activation(
                chunk(h2_t, c), chunk(s1, c), AF.Identity,
                scale=bias_col("L0_ffn_ce_n1g", c),
                bias=bias_col("L0_ffn_ce_n1b", c)),
                deps=[pt_m] if c == 0 else (),
                writes=[h2t_trk] if c == 0 else ())
        h2t_trk.write_at(h2_pt)
        s1t.read_at(h2_pt)
        dma("scalar", dram_tile(h2_v, j), sb_tile3(h2_t), rtrks=(h2t_trk,),
            wtrks=(dtrk["h2_sp"][j],))
        yield

    for j in range(NTIL):
        g6 = ph6_tail(j - 1) if j > 0 else None
        ph6_head(j, g6)
        if g6 is not None:
            for _ in g6:
                pass
    for _ in ph6_tail(NTIL - 1):
        pass

    retire((synw, t6[0]), (cwv, t6[1]), (cwo, t6[2]),
           (x6_t[0], x6_trk[0]), (x6_t[1], x6_trk[1]), (syn_t, syn_trk),
           (synb_t, synb_trk), (v6_t, v6_trk), (s1_t[0], s1t_trk[0]),
           (s1_t[1], s1t_trk[1]), (s1b_t[0], s1b_trk[0]),
           (s1b_t[1], s1b_trk[1]), (h2_t, h2t_trk), (rows6, rows6_trk))
    ph6.close()

    ph7 = ES()
    sb7 = lambda nm, sh, dt=F32: ph7.enter_context(nc.sbuf_tensor(nm, sh, dt))
    f1w = sb7("f1w", [128, KC * 2048], BF16)
    f2w = sb7("f2w", [128, 16 * 1024], BF16)
    genw = sb7("genw", [128, KC * 1024], BF16)
    pr.new_phase()
    t7 = [fresh(w_, pr.w_sem()) for w_ in (f1w, f2w, genw)]
    pw7 = [load_weight("L0_ffn_ce_f1w", f1w, t7[0], 2048),
           load_weight("L0_ffn_ce_f2w", f2w, t7[1], 1024),
           load_weight("L0_ffn_ce_gen_w", genw, t7[2], 1024)]
    h2b_t = [sb7(f"h2b{i}", [128, KC * NT], BF16) for i in range(2)]
    h2b_trk = [fresh(h2b_t[0], pr.buf_sem()), fresh(h2b_t[1], pr.buf_sem())]
    fh_t = sb7("fh_t", [128, 16 * NT], BF16)
    fh_trk = fresh(fh_t)
    s2_t = [sb7(f"s2_t{i}", [128, KC * NT]) for i in range(2)]
    s2_trk = [fresh(s2_t[0]), fresh(s2_t[1])]
    s2b_t = [sb7(f"s2b_t{i}", [128, KC * NT], BF16) for i in range(2)]
    s2b_trk = [fresh(s2b_t[0]), fresh(s2b_t[1])]
    h2n_t = sb7("h2n_t", [128, KC * NT], BF16)
    h2n_trk = fresh(h2n_t)
    acc7_t = sb7("acc7_t", [128, KC * NT])
    acc7_trk = fresh(acc7_t, pr.buf_sem())
    rows7 = sb7("rows7", [1, 2 * NT])
    rows7_trk = fresh(rows7)

    def ph7_head(j, gen=None):
        par = j % 2
        h2b, h2bt = h2b_t[par], h2b_trk[par]
        s2, s2t = s2_t[par], s2_trk[par]
        dma("sync", sb_tile3(h2b), dram_tile(h2_v, j), wtrks=(h2bt,),
            rtrks=(dtrk["h2_sp"][j],))
        fh_pts = [None] * 16
        for mc in range(16):
            pb, ptrk = next_psM()
            pt_h = mm_acc(lambda k, mc=mc: wslice(f1w, 2048, k, mc),
                          lambda k, h2b=h2b: chunk(h2b, k), KC, pb, ptrk,
                          deps=[pw7[0]], rtrks=[h2bt, t7[0]])
            pe = sca(lambda e, pb=pb, mc=mc: e.activation(
                chunk(fh_t, mc), pb[0:128, 0:NT], AF.Relu,
                bias=bias_col("L0_ffn_ce_f1b", mc)),
                deps=[pt_h], writes=[fh_trk] if mc == 0 else ())
            fh_trk.write_at(pe)
            ptrk.read_at(pe)
            fh_pts[mc] = pe
            if gen is not None and mc % 2 == 1:
                next(gen, None)
        for m in range(8):
            pb, ptrk = next_psM()
            pt_o = mm_acc(lambda k, m=m: wslice(f2w, 1024, k, m),
                          lambda k: chunk(fh_t, k), 16, pb, ptrk,
                          deps=[pw7[1]], kdeps=fh_pts, rtrks_rec=[fh_trk],
                          rtrks=[t7[1]])
            pt_s = vec(lambda e, pb=pb, m=m, h2b=h2b, s2=s2:
                       e.scalar_tensor_tensor(
                chunk(s2, m), pb[0:128, 0:NT], bias_col("L0_ffn_ce_f2b", m),
                chunk(h2b, m), ALU.add, ALU.add),
                deps=[pt_o], reads=[h2bt] if m == 0 else (),
                writes=[s2t] if m == 0 else ())
            s2t.write_at(pt_s)
            h2bt.read_at(pt_s)
            ptrk.read_at(pt_s)
            if gen is not None:
                next(gen, None)

    def ph7_tail(j):
        par = j % 2
        s2, s2t = s2_t[par], s2_trk[par]
        s2b, s2bt = s2b_t[par], s2b_trk[par]
        c_sc = c_ve = None
        for c in range(KC):
            if c % 2 == 0:
                c_sc = sca(lambda e, c=c, s2=s2, s2b=s2b: e.copy(
                    chunk(s2b, c), chunk(s2, c)),
                    reads=[s2t] if c == 0 else (),
                    writes=[s2bt] if c == 0 else ())
            else:
                c_ve = vec(lambda e, c=c, s2=s2, s2b=s2b: e.tensor_copy(
                    chunk(s2b, c), chunk(s2, c)),
                    reads=[s2t] if c == 1 else (),
                    writes=[s2bt] if c == 1 else ())
        s2bt.write_at(c_sc)
        s2bt.write_at(c_ve)
        s2t.read_at(c_sc)
        s2t.read_at(c_ve)
        yield
        do_b = ln_stats1(s2b, s2bt, rows7[:, 0:NT],
                         rows7[:, NT:2 * NT],
                         rows7_trk, 0, 1, deps=[c_sc, c_ve], defer=True)
        yield
        pt_mub, pt_rsb = do_b()
        pt_xc = None
        for c in range(KC):
            pt_xc = vec(lambda e, c=c, s2=s2: e.tensor_sub(
                chunk(s2, c), chunk(s2, c), psB[0][0:128, 0:NT]),
                deps=[pt_mub] if c == 0 else (),
                writes=[s2t] if c == 0 else ())
        s2t.write_at(pt_xc)
        psB_trk[0].read_at(pt_xc)
        yield
        pt_m = None
        for c in range(KC):
            pt_m = vec(lambda e, c=c, s2=s2: e.tensor_mul(
                chunk(s2, c), chunk(s2, c), psB[1][0:128, 0:NT]),
                deps=[pt_rsb, pt_xc] if c == 0 else ())
        s2t.write_at(pt_m)
        psB_trk[1].read_at(pt_m)
        yield
        hn_pts = [None] * KC
        hn_pt = None
        for c in range(KC):
            hn_pt = sca(lambda e, c=c, s2=s2: e.activation(
                chunk(h2n_t, c), chunk(s2, c), AF.Identity,
                scale=bias_col("L0_ffn_ce_n2g", c),
                bias=bias_col("L0_ffn_ce_n2b", c)),
                deps=[pt_m] if c == 0 else (),
                writes=[h2n_trk] if c == 0 else ())
            hn_pts[c] = hn_pt
        h2n_trk.write_at(hn_pt)
        s2t.read_at(hn_pt)
        yield
        pt_wb = wrow_dma_bcast(j % 2, R_W2, j)
        for m in range(8):
            pb, ptrk = next_psM()
            pt_o = mm_acc(lambda k, m=m: wslice(genw, 1024, k, m),
                          lambda k: chunk(h2n_t, k), KC, pb, ptrk,
                          deps=[pw7[2]], kdeps=hn_pts, rtrks_rec=[h2n_trk],
                          rtrks=[t7[2]])
            acc_evac_rmw(pb, ptrk, m, j, "L0_ffn_ce_gen_b", pt_wb, 0,
                         acc7_t, acc7_trk, pt_o=pt_o)
            if m % 2 == 1:
                yield
        dma("scalar", dram_tile(acc_v[1], j), sb_tile3(acc7_t),
            rtrks=(acc7_trk,), wtrks=(dtrk["acc1_sp"][j],))
        yield

    for j in range(NTIL):
        g7 = ph7_tail(j - 1) if j > 0 else None
        ph7_head(j, g7)
        if g7 is not None:
            for _ in g7:
                pass
    for _ in ph7_tail(NTIL - 1):
        pass

    retire((f1w, t7[0]), (f2w, t7[1]), (genw, t7[2]),
           (h2b_t[0], h2b_trk[0]), (h2b_t[1], h2b_trk[1]), (fh_t, fh_trk),
           (s2_t[0], s2_trk[0]), (s2_t[1], s2_trk[1]),
           (s2b_t[0], s2b_trk[0]), (s2b_t[1], s2b_trk[1]),
           (h2n_t, h2n_trk), (acc7_t, acc7_trk),
           (rows7, rows7_trk))
    ph7.close()

    ph8 = ES()
    sb8 = lambda nm, sh, dt=F32: ph8.enter_context(nc.sbuf_tensor(nm, sh, dt))
    mw1 = sb8("ml_w1", [128, KC * 4096], BF16)
    mw2 = sb8("ml_w2", [128, 32 * 1024], BF16)
    pr.new_phase()
    t8 = [fresh(mw1, pr.w_sem()), fresh(mw2, pr.w_sem())]
    pw8 = [load_weight("L0_ffn_ml_w1", mw1, t8[0], 4096),
           load_weight("L0_ffn_ml_w2", mw2, t8[1], 1024)]
    x8_t = [sb8(f"x8_{i}", [128, KC * NT], BF16) for i in range(2)]
    x8_trk = [fresh(x8_t[0], pr.buf_sem()), fresh(x8_t[1], pr.buf_sem())]
    h3_t = sb8("h3_t", [128, 32 * NT], BF16)
    h3_trk = fresh(h3_t)
    f_t = sb8("f_t", [128, KC * NT])
    f_trk = fresh(f_t, pr.buf_sem())
    x1c_c = [sb8("x1c_c0", [128, NT])] * 2
    x1c_trk = [fresh(x1c_c[0], pr.buf_sem())] * 2
    rows8 = sb8("rows8", [1, 2 * NT])
    rows8_trk = fresh(rows8)

    def ph8_head(j, gen=None):
        par = j % 2
        x8, x8t = x8_t[par], x8_trk[par]
        dma("sync", sb_tile3(x8), dram_tile(x1n_v, j), wtrks=(x8t,),
            rtrks=(dtrk["x1n_sp"][j],))
        pt_wb = wrow_dma_bcast(j % 2, R_W3, j)
        h3_pts = [None] * 32
        for mc in range(32):
            pb, ptrk = next_psM()
            pt_h = mm_acc(lambda k, mc=mc: wslice(mw1, 4096, k, mc),
                          lambda k, x8=x8: chunk(x8, k), KC, pb, ptrk,
                          deps=[pw8[0]], rtrks=[x8t, t8[0]])
            pe = sca(lambda e, pb=pb, mc=mc: e.activation(
                chunk(h3_t, mc), pb[0:128, 0:NT], AF.Gelu,
                bias=bias_col("L0_ffn_ml_b1", mc)),
                deps=[pt_h], writes=[h3_trk] if mc == 0 else ())
            h3_trk.write_at(pe)
            ptrk.read_at(pe)
            h3_pts[mc] = pe
            if gen is not None and mc % 3 == 2:
                next(gen, None)
        return h3_pts, pt_wb

    def ph8_headB(j, h3_pts, pt_wb):
        for m in range(8):
            pb, ptrk = next_psM()
            pt_o = mm_acc(lambda k, m=m: wslice(mw2, 1024, k, m),
                          lambda k: chunk(h3_t, k), 32, pb, ptrk,
                          deps=[pw8[1]], kdeps=h3_pts, rtrks_rec=[h3_trk],
                          rtrks=[t8[1]])
            acc_evac_rmw(pb, ptrk, m, j, "L0_ffn_ml_b2", pt_wb, 1,
                         f_t, f_trk, pt_o=pt_o)

    def ph8_tail(j):
        par = j % 2
        fb, fbt = x8_t[par], x8_trk[par]
        c_sc = c_ve = None
        for c in range(KC):
            if c % 2 == 0:
                c_sc = sca(lambda e, c=c, fb=fb: e.copy(chunk(fb, c),
                                                        chunk(f_t, c)),
                           reads=[f_trk] if c == 0 else (),
                           writes=[fbt] if c == 0 else ())
            else:
                c_ve = vec(lambda e, c=c, fb=fb: e.tensor_copy(
                    chunk(fb, c), chunk(f_t, c)),
                    reads=[f_trk] if c == 1 else (),
                    writes=[fbt] if c == 1 else ())
        fbt.write_at(c_sc)
        fbt.write_at(c_ve)
        f_trk.read_at(c_sc)
        f_trk.read_at(c_ve)
        yield
        do_b = ln_stats1(fb, fbt, rows8[:, 0:NT], rows8[:, NT:2 * NT],
                         rows8_trk, 0, 1,
                         deps=[c_sc, c_ve], src_f32=f_t,
                         src_trk=f_trk, defer=True)
        yield
        pt_mub, pt_rsb = do_b()
        pt_xc = None
        for c in range(KC):
            pt_xc = vec(lambda e, c=c: e.tensor_sub(chunk(f_t, c), chunk(f_t, c),
                                                    psB[0][0:128, 0:NT]),
                        deps=[pt_mub] if c == 0 else (),
                        writes=[f_trk] if c == 0 else ())
        f_trk.write_at(pt_xc)
        psB_trk[0].read_at(pt_xc)
        yield
        pt_o2 = None
        for c in range(KC):
            pt_m = vec(lambda e, c=c: e.tensor_mul(chunk(f_t, c), chunk(f_t, c),
                                                   psB[1][0:128, 0:NT]),
                       deps=[pt_rsb, pt_xc] if c == 0 else ())
            f_trk.write_at(pt_m)
            xc, xct = x1c_c[c % 2], x1c_trk[c % 2]
            dma("sync", xc[:], x1_v[:, c, j * NT:(j + 1) * NT],
                wtrks=(xct,), rtrks=(dtrk["x1_sp"][j],))
            pt_o2 = vec(lambda e, c=c, xc=xc: e.tensor_add(chunk(f_t, c),
                                                           chunk(f_t, c),
                                                           xc[:]),
                        deps=[pt_m], reads=[xct])
            f_trk.write_at(pt_o2)
            if c % 3 == 2:
                yield
        psB_trk[1].read_at(pt_o2)
        dma("sync", dram_tile(out_v, j), sb_tile3(f_t), rtrks=(f_trk,))
        yield

    for j in range(NTIL):
        g8 = ph8_tail(j - 1) if j > 0 else None
        h3_pts, pt_wb8 = ph8_head(j, g8)
        if g8 is not None:
            for _ in g8:
                pass
        ph8_headB(j, h3_pts, pt_wb8)
    for _ in ph8_tail(NTIL - 1):
        pass

    ph8.close()

    for s_, v_ in sorted(pr.counts.items()):
        pr.wait("sync", s_, v_)

    with nc.Block() as block:
        pr.finalize(block)
    ctx.close()
    return nc


def _pack_biases(inputs):
    out = np.zeros((128, NBIAS), np.float32)
    for name, ncol in BIAS_SPEC:
        v = np.asarray(inputs[name], np.float32)
        off = BIAS_OFF[name]
        if v.shape[0] == 4:
            out[0:4, off] = v
        else:
            out[:, off:off + ncol] = v.reshape(ncol, 128).T
    return out


def make_in_maps(inputs):
    x = np.asarray(inputs["L0_attn_x"])
    biasv = _pack_biases(inputs)
    ident = np.eye(128, dtype=np.float32)
    in_maps = []
    for c in range(8):
        bi, si = c // 4, c % 4
        xs = np.ascontiguousarray(x[bi, si * T:(si + 1) * T, :].T)
        fold = np.zeros((128, 16), np.float32)
        for jj in range(8):
            if jj // 4 == bi and (jj % 4) < si:
                fold[:, jj] = 1.0
        fold[:, 8:16] = 1.0 - fold[:, 0:8]
        m = {"x": xs, "biasv": biasv, "foldmask": fold, "identity": ident}
        for n in WEIGHT_NAMES:
            m[n] = np.asarray(inputs[n])
        in_maps.append(m)
    return in_maps


def kernel(**inputs):
    nc = build_nc()
    in_maps = make_in_maps(inputs)
    res = run_bass_kernel_spmd(nc, in_maps, core_ids=list(range(8)))
    outs = [np.asarray(res.results[c]["out"]).T for c in range(8)]
    return np.stack([np.concatenate(outs[b * 4:(b + 1) * 4], axis=0)
                     for b in range(2)], axis=0)


# version 21 (final)
# speedup vs baseline: 1.0081x; 1.0067x over previous
import numpy as np
import concourse.bass as bass
import concourse.mybir as mybir
from concourse.bass_utils import run_bass_kernel_spmd

F32 = mybir.dt.float32
BF16 = mybir.dt.bfloat16
AF = mybir.ActivationFunctionType
ALU = mybir.AluOpType
AX = mybir.AxisListType

D = 1024
KC = 8
T = 4096
NT = 512
NTIL = T // NT
EPS = 1e-5

ENGINES = ("sync", "gpsimd", "tensor", "scalar", "vector")

BIAS_SPEC = [
    ("L0_attn_norm1_w", 8), ("L0_attn_norm1_b", 8), ("L0_attn_gate_b", 16),
    ("L0_attn_state_b", 8), ("L0_attn_out_b", 8), ("L0_attn_norm2_w", 8),
    ("L0_attn_norm2_b", 8), ("L0_attn_mlp_b1", 32), ("L0_attn_mlp_b2", 8),
    ("L0_ffn_router_b", 1), ("L0_ffn_load_balancer", 1),
    ("L0_ffn_me_eq_b", 8), ("L0_ffn_me_bv", 8), ("L0_ffn_me_bo", 8),
    ("L0_ffn_me_c1b", 16), ("L0_ffn_me_c2b", 8),
    ("L0_ffn_ce_syn_b", 8), ("L0_ffn_ce_bv", 8), ("L0_ffn_ce_bo", 8),
    ("L0_ffn_ce_n1g", 8), ("L0_ffn_ce_n1b", 8), ("L0_ffn_ce_f1b", 16),
    ("L0_ffn_ce_f2b", 8), ("L0_ffn_ce_n2g", 8), ("L0_ffn_ce_n2b", 8),
    ("L0_ffn_ce_gen_b", 8), ("L0_ffn_ml_b1", 32), ("L0_ffn_ml_b2", 8),
]
BIAS_OFF = {}
_off = 0
for _n, _c in BIAS_SPEC:
    BIAS_OFF[_n] = _off
    _off += _c
NBIAS = _off

WSHAPE = {
    "L0_attn_gate_w": (D, 2048), "L0_attn_state_w": (D, D),
    "L0_attn_out_w": (D, D), "L0_attn_mlp_w1": (D, 4096),
    "L0_attn_mlp_w2": (4096, D), "L0_ffn_router_w": (D, 4),
    "L0_ffn_sw_w1": (D, 4096), "L0_ffn_sw_w3": (D, 4096),
    "L0_ffn_sw_w2": (4096, D), "L0_ffn_me_eq_w": (D, D),
    "L0_ffn_me_wv": (D, D), "L0_ffn_me_wo": (D, D),
    "L0_ffn_me_c1w": (D, 2048), "L0_ffn_me_c2w": (2048, D),
    "L0_ffn_ce_syn_w": (D, D), "L0_ffn_ce_wv": (D, D),
    "L0_ffn_ce_wo": (D, D), "L0_ffn_ce_f1w": (D, 2048),
    "L0_ffn_ce_f2w": (2048, D), "L0_ffn_ce_gen_w": (D, D),
    "L0_ffn_ml_w1": (D, 4096), "L0_ffn_ml_w2": (4096, D),
}
WEIGHT_NAMES = list(WSHAPE)

R_MU1, R_RS1, R_P, R_W0, R_W1, R_W2, R_W3 = range(7)


class Prog:

    def __init__(self, nc):
        self.nc = nc
        self.streams = {e: [] for e in ENGINES}
        self.sems = {}
        self.counts = {}
        self.lastwait = {e: {} for e in ENGINES}

    def new_sem(self, ctx, name):
        self.sems[name] = ctx.enter_context(self.nc.semaphore(name))
        self.counts[name] = 0

    def init_pools(self, ctx, npool=20, nglob=12, nw=8):
        self.npool, self.nw = npool, nw
        for i in range(npool):
            self.new_sem(ctx, f"P{i}")
        for i in range(nglob):
            self.new_sem(ctx, f"G{i}")
        for i in range(nw):
            self.new_sem(ctx, f"W{i}")
        self.phase_used = 0
        self.glob_used = 0
        self.w_used = 0

    def buf_sem(self):
        assert self.phase_used < self.npool, "per-phase DMA sem pool exhausted"
        n = f"P{self.phase_used}"
        self.phase_used += 1
        return n

    def w_sem(self):
        assert self.w_used < self.nw, "weight (SWDGE) sem pool exhausted"
        n = f"W{self.w_used}"
        self.w_used += 1
        return n

    def g_sem(self):
        n = f"G{self.glob_used}"
        self.glob_used += 1
        return n

    def new_phase(self):
        self.phase_used = 0
        self.w_used = 0

    def wait(self, eng, sem, val):
        if not val or val <= 0:
            return
        if self.lastwait[eng].get(sem, 0) >= val:
            return
        self.lastwait[eng][sem] = val
        h = self.sems[sem]
        self.streams[eng].append(lambda e, h=h, v=val: e.wait_ge(h, v))

    def waits(self, eng, pts):
        for sem, val in pts.items():
            self.wait(eng, sem, val)

    def do(self, eng, maker, inc=None, incby=1):
        if inc is None:
            self.streams[eng].append(lambda e, m=maker: m(e))
            return None
        self.counts[inc] += incby
        cnt = self.counts[inc]
        h = self.sems[inc]
        self.streams[eng].append(lambda e, m=maker, h=h, b=incby: m(e).then_inc(h, b))
        return {inc: cnt}

    def fence(self):
        snap = dict(self.counts)
        for eng in ENGINES:
            self.waits(eng, snap)

    def finalize(self, block):
        for eng in ENGINES:
            stream = self.streams[eng]
            getattr(block, eng)(lambda e, s=stream: [f(e) for f in s])


class Trk:
    def __init__(self, sem=None):
        self.readers = {}
        self.writer = {}
        self.sem = sem

    def read_at(self, pt):
        if pt:
            for s, v in pt.items():
                self.readers[s] = max(self.readers.get(s, 0), v)

    def write_at(self, pt):
        if pt:
            for s, v in pt.items():
                self.writer[s] = max(self.writer.get(s, 0), v)

    def before_write(self):
        out = dict(self.writer)
        for s, v in self.readers.items():
            out[s] = max(out.get(s, 0), v)
        return out

    def before_read(self):
        return dict(self.writer)


def merge_pts(*pts):
    out = {}
    for p in pts:
        if p:
            for s, v in p.items():
                out[s] = max(out.get(s, 0), v)
    return out


def build_nc(dbg=()):
    nc = bass.Bass("TRN2", target_bir_lowering=False)
    from contextlib import ExitStack, ExitStack as ES

    x_d = nc.dram_tensor("x", [D, T], F32, kind="ExternalInput").ap()
    biasv_d = nc.dram_tensor("biasv", [128, NBIAS], F32, kind="ExternalInput").ap()
    fold_d = nc.dram_tensor("foldmask", [128, 16], F32, kind="ExternalInput").ap()
    ident_d = nc.dram_tensor("identity", [128, 128], F32, kind="ExternalInput").ap()
    w_d = {n: nc.dram_tensor(n, list(WSHAPE[n]), F32, kind="ExternalInput").ap()
           for n in WEIGHT_NAMES}
    out_d = nc.dram_tensor("out", [D, T], F32, kind="ExternalOutput").ap()

    def scratch(name, shape, dt):
        kind = "ExternalOutput" if name in dbg else "Internal"
        return nc.dram_tensor(name, shape, dt, kind=kind).ap()

    t1_sp = scratch("t1_sp", [D, T], F32)
    yloc_sp = scratch("yloc_sp", [D, T], BF16)
    t_sp = scratch("t_sp", [D, T], F32)
    x2n_sp = scratch("x2n_sp", [D, T], BF16)
    x1_sp = scratch("x1_sp", [D, T], F32)
    x1n_sp = scratch("x1n_sp", [D, T], BF16)
    h2_sp = scratch("h2_sp", [D, T], BF16)
    acc0_sp = scratch("acc0_sp", [D, T], F32)
    acc1_sp = scratch("acc1_sp", [D, T], F32)
    rows_sp = scratch("rows_sp", [8, T], F32)
    cc_in = scratch("cc_in", [128 * 9], F32)
    cc_out = scratch("cc_out", [8, 128 * 9], F32)
    hin_dbg = scratch("hin_dbg", [128, 9], F32) if "hin_dbg" in dbg else None

    dtrk = {nm: [Trk() for _ in range(NTIL)] for nm in
            ("t1_sp", "yloc_sp", "t_sp", "x2n_sp", "x1_sp", "x1n_sp", "h2_sp",
             "acc0_sp", "acc1_sp", "rows_sp")}
    dtrk.update({nm: Trk() for nm in ("cc_in", "cc_out")})

    ctx = ExitStack()
    pr = Prog(nc)
    for s in ("TS", "VS", "SS", "CC"):
        pr.new_sem(ctx, s)
    pr.init_pools(ctx)

    sb = lambda nm, sh, dt=F32: ctx.enter_context(nc.sbuf_tensor(nm, sh, dt))
    psum = lambda nm: ctx.enter_context(nc.psum_tensor(nm, [128, NT], F32))

    biasv = sb("biasv_s", [128, NBIAS])
    ones_col = sb("ones_col", [128, 1], BF16)
    ones_colf = sb("ones_colf", [128, 1])
    ones_row = sb("ones_row", [1, 128])
    consts = sb("consts", [1, 4])
    biasrl = sb("biasrl", [4, 1])
    ph12 = ES()
    sb12 = lambda nm, sh, dt=F32: ph12.enter_context(nc.sbuf_tensor(nm, sh, dt))
    hstate = sb12("hstate", [128, KC], BF16)
    h_in = sb12("h_in", [128, KC])
    Pcarry = sb12("Pcarry", [1, 1])
    foldm = sb12("foldm", [128, 16])
    mean_bs = sb12("mean_bs", [1, 1])
    rvec = sb12("rvec", [128, KC], BF16)
    msg = sb12("msg", [128, 9])
    gath = sb12("gath", [128, 72])
    aeff = sb12("aeff", [128, 1])
    hmsk = sb12("hmsk", [128, 8])

    biasv_trk, foldm_trk = Trk(pr.g_sem()), Trk(pr.g_sem())
    hstate_trk, hin_trk, Pc_trk = Trk(), Trk(), Trk()
    msg_trk, gath_trk = Trk(pr.g_sem()), Trk(pr.g_sem())
    meanb_trk, rvec_trk, biasrl_trk = Trk(), Trk(), Trk()

    psR = psum("psR")
    psB = [psum(f"psB{i}") for i in range(4)]
    psM = [psum(f"psM{i}") for i in range(3)]
    psR_trk = Trk()
    psR2_trk = Trk()
    psB_trk = [Trk() for _ in range(4)]
    psM_trk = [Trk() for _ in range(3)]
    psm_rot = [0]

    def next_psM():
        i = psm_rot[0] % 3
        psm_rot[0] += 1
        return psM[i], psM_trk[i]

    def dma(q, out_ap, in_ap, wtrks=(), rtrks=(), war=True):
        sem = None
        for t in list(wtrks) + list(rtrks):
            if t.sem is not None:
                sem = t.sem
                break
        assert sem is not None, "dma needs a tracked buffer with a sem"
        if war:
            for t in wtrks:
                pr.waits(q, t.before_write())
        for t in rtrks:
            pr.waits(q, t.before_read())
        pt = pr.do(q, lambda e, o=out_ap, i=in_ap: e.dma_start(out=o, in_=i),
                   inc=sem, incby=16)
        for t in wtrks:
            t.write_at(pt)
        for t in rtrks:
            t.read_at(pt)
        return pt

    _memloc_cache = {}

    def addr_range(t):
        nm = t.name
        if nm not in _memloc_cache:
            for a in nc.m.functions[0].allocations:
                try:
                    ml = a.memorylocations[0]
                except Exception:
                    continue
                _memloc_cache[ml.name] = (ml.addr, ml.size())
        lo, sz = _memloc_cache[nm]
        return lo, lo + sz

    _dead = []

    def retire(*pairs):
        snap = {s: pr.counts[s] for s in ("TS", "VS", "SS")}
        for t, trk in pairs:
            lo, hi = addr_range(t)
            pts = trk.before_write() if trk is not None else dict(snap)
            _dead.append((lo, hi, pts))

    def fresh(t, sem=None):
        trk = Trk(sem)
        lo, hi = addr_range(t)
        for a, b, pts in _dead:
            if a < hi and lo < b:
                trk.write_at(pts)
        if sem is not None:
            trk.write_at({sem: pr.counts[sem]})
        return trk

    def mm_acc(lhsT_fn, rhs_fn, nk, pbank, ptrk, deps=(), rtrks=(), cols=NT,
               rows=128, kdeps=None, rtrks_rec=()):
        pr.waits("tensor", ptrk.before_write())
        for dp in deps:
            if dp:
                pr.waits("tensor", dp)
        if kdeps is None:
            rtrks = list(rtrks) + list(rtrks_rec)
            rtrks_rec = ()
        for t in rtrks:
            pr.waits("tensor", t.before_read())
        pt = None
        for k in range(nk):
            if kdeps is not None and kdeps[k]:
                pr.waits("tensor", kdeps[k])
            last = k == nk - 1
            pt = pr.do("tensor",
                       lambda e, l=lhsT_fn(k), r=rhs_fn(k), s=(k == 0), p=last,
                       pb=pbank, rw=rows, cl=cols:
                       e.matmul(pb[0:rw, 0:cl], l, r, start=s, stop=p),
                       inc="TS" if last else None)
        ptrk.write_at(pt)
        for t in rtrks:
            t.read_at(pt)
        for t in rtrks_rec:
            t.read_at(pt)
        return pt

    def bcast(row_ap, pbi, deps=(), rtrks=(), cols=NT):
        pb, ptrk = psB[pbi], psB_trk[pbi]
        pr.waits("tensor", ptrk.before_write())
        for dp in deps:
            if dp:
                pr.waits("tensor", dp)
        for t in rtrks:
            pr.waits("tensor", t.before_read())
        pt = pr.do("tensor", lambda e, r=row_ap, p=pb, c=cols:
                   e.matmul(p[0:128, 0:c], ones_row[:], r, start=True, stop=True),
                   inc="TS")
        ptrk.write_at(pt)
        for t in rtrks:
            t.read_at(pt)
        return pt

    def _eng_op(eng, sem, maker, deps=(), reads=(), writes=()):
        for dp in deps:
            if dp:
                pr.waits(eng, dp)
        for t in reads:
            pr.waits(eng, t.before_read())
        for t in writes:
            pr.waits(eng, t.before_write())
        pt = pr.do(eng, maker, inc=sem)
        for t in reads:
            t.read_at(pt)
        for t in writes:
            t.write_at(pt)
        return pt

    def vec(maker, deps=(), reads=(), writes=()):
        return _eng_op("vector", "VS", maker, deps, reads, writes)

    def sca(maker, deps=(), reads=(), writes=()):
        return _eng_op("scalar", "SS", maker, deps, reads, writes)

    def rsqrt_row(dst_ap, src_ap, deps, trk=None):
        p1 = sca(lambda e, d=dst_ap, s=src_ap:
                 e.activation(d, s, AF.Ln, bias=consts[:, 0:1]), deps=deps)
        p2 = sca(lambda e, d=dst_ap: e.activation(d, d, AF.Exp, scale=-0.5),
                 deps=[p1])
        if trk is not None:
            trk.write_at(p2)
        return p2

    def load_weight(name, dst_sb, dst_trk, dcols, src_lo=None, src_hi=None,
                    krange=None, pts=None):
        wa = w_d[name]
        din = WSHAPE[name][0]
        ks = list(krange) if krange is not None else list(range(din // 128))
        lo = 0 if src_lo is None else src_lo
        hi = WSHAPE[name][1] if src_hi is None else src_hi
        assert hi - lo == dcols
        pt = None
        for ki, k in enumerate(ks):
            pt = dma("gpsimd",
                     dst_sb[:, ki * dcols:(ki + 1) * dcols],
                     wa[k * 128:(k + 1) * 128, lo:hi],
                     wtrks=(dst_trk,), war=(ki == 0))
        if pts is not None:
            pts.extend([pt] * len(ks))
        return pt

    def wslice(wsb, dcols, k, m):
        base = k * dcols + m * 128
        return wsb[:, base: base + 128]

    def bias_col(name, c):
        off = BIAS_OFF[name]
        return biasv[:, off + c: off + c + 1]

    x_v = x_d.rearrange("(c p) t -> p c t", p=128)
    t1_v = t1_sp.rearrange("(c p) t -> p c t", p=128)
    yloc_v = yloc_sp.rearrange("(c p) t -> p c t", p=128)
    t_v = t_sp.rearrange("(c p) t -> p c t", p=128)
    x2n_v = x2n_sp.rearrange("(c p) t -> p c t", p=128)
    x1_v = x1_sp.rearrange("(c p) t -> p c t", p=128)
    x1n_v = x1n_sp.rearrange("(c p) t -> p c t", p=128)
    h2_v = h2_sp.rearrange("(c p) t -> p c t", p=128)
    acc_v = [acc0_sp.rearrange("(c p) t -> p c t", p=128),
             acc1_sp.rearrange("(c p) t -> p c t", p=128)]
    out_v = out_d.rearrange("(c p) t -> p c t", p=128)

    def dram_tile(ap3, j):
        return ap3[:, :, j * NT:(j + 1) * NT]

    def sb_tile3(t):
        return t[:].rearrange("p (c n) -> p c n", n=NT)

    def chunk(t, c, width=NT):
        return t[:, c * width:(c + 1) * width]

    def ln_stats1(xb_t, xb_trk, r_mu, r_v, rows_trk, mubi, rsbi,
                  deps=(), src_f32=None, src_trk=None, defer=False):
        pt_mean = mm_acc(lambda k: ones_col[:], lambda k: chunk(xb_t, k), KC,
                         psR, psR_trk, deps=deps, rtrks=[xb_trk], rows=1)
        sq_src = xb_t if src_f32 is None else src_f32
        sq_sc = sq_ve = None
        for c in range(KC):
            rd = [src_trk] if (src_f32 is not None and src_trk is not None
                              and c < 2) else ()
            if c % 2 == 0:
                sq_sc = sca(lambda e, c=c, s=sq_src: e.activation(
                    chunk(xb_t, c), chunk(s, c), AF.Square),
                    deps=[pt_mean] if c == 0 else (), reads=rd,
                    writes=[xb_trk] if c == 0 else ())
            else:
                sq_ve = vec(lambda e, c=c, s=sq_src: e.tensor_mul(
                    chunk(xb_t, c), chunk(s, c), chunk(s, c)),
                    deps=[pt_mean] if c == 1 else (), reads=rd,
                    writes=[xb_trk] if c == 1 else ())
        xb_trk.write_at(sq_sc)
        xb_trk.write_at(sq_ve)
        if src_trk is not None:
            src_trk.read_at(sq_sc)
            src_trk.read_at(sq_ve)
        pr.waits("tensor", psR2_trk.before_write())
        pt_ss = mm_acc2(lambda k: ones_col[:], lambda k: chunk(xb_t, k), KC,
                        deps=[sq_sc, sq_ve], rtrks=[xb_trk])
        pt_mu = vec(lambda e, r=r_mu: e.tensor_scalar_mul(r, psR[0:1, 0:NT],
                                                          1.0 / 1024.0),
                    deps=[pt_mean], writes=[rows_trk])
        psR_trk.read_at(pt_mu)
        pt_m2 = vec(lambda e, r=r_v: e.tensor_scalar_mul(r, psR[32:33, 0:NT],
                                                         1.0 / 1024.0),
                    deps=[pt_ss], writes=[rows_trk])
        psR2_trk.read_at(pt_m2)
        pt_q = vec(lambda e, b=r_mu: e.tensor_mul(psR[0:1, 0:NT], b, b),
                   deps=[pt_mu])
        psR_trk.write_at(pt_q)
        pt_var = vec(lambda e, a=r_v: e.tensor_sub(a, a, psR[0:1, 0:NT]),
                     deps=[pt_m2, pt_q], writes=[rows_trk])
        psR_trk.read_at(pt_var)
        pt_rs = rsqrt_row(r_v, r_v, deps=[pt_var], trk=rows_trk)

        def do_b():
            pt_mub = bcast(r_mu, mubi, deps=[pt_mu], rtrks=[rows_trk])
            pt_rsb = bcast(r_v, rsbi, deps=[pt_rs], rtrks=[rows_trk])
            return pt_mub, pt_rsb
        if defer:
            return do_b
        return do_b()

    def mm_acc2(lhsT_fn, rhs_fn, nk, deps=(), rtrks=()):
        for dp in deps:
            if dp:
                pr.waits("tensor", dp)
        for t in rtrks:
            pr.waits("tensor", t.before_read())
        pt = None
        for k in range(nk):
            last = k == nk - 1
            pt = pr.do("tensor",
                       lambda e, l=lhsT_fn(k), r=rhs_fn(k), s=(k == 0), p=last:
                       e.matmul(psR[32:33, 0:NT], l, r, start=s, stop=p),
                       inc="TS" if last else None)
        psR2_trk.write_at(pt)
        for t in rtrks:
            t.read_at(pt)
        return pt

    dma("sync", biasv[:], biasv_d, wtrks=(biasv_trk,))
    dma("sync", foldm[:], fold_d, wtrks=(foldm_trk,))
    vec(lambda e: e.memset(ones_col[:], 1.0))
    vec(lambda e: e.memset(ones_colf[:], 1.0))
    vec(lambda e: e.memset(ones_row[:], 1.0))
    vec(lambda e: e.memset(consts[:, 0:1], EPS))
    vec(lambda e: e.memset(consts[:, 1:2], 1.0))
    vec(lambda e: e.memset(Pcarry[:], 1.0), writes=[Pc_trk])
    vec(lambda e: e.memset(hstate[:], 0.0), writes=[hstate_trk])
    pt_hin0 = vec(lambda e: e.memset(h_in[:], 0.0), writes=[hin_trk])
    off_rb = BIAS_OFF["L0_ffn_router_b"]
    off_lb = BIAS_OFF["L0_ffn_load_balancer"]
    vec(lambda e: e.tensor_add(biasrl[:], biasv[0:4, off_rb:off_rb + 1],
                               biasv[0:4, off_lb:off_lb + 1]),
        reads=[biasv_trk], writes=[biasrl_trk])
    pr.fence()

    ph1 = ES()
    sb1 = lambda nm, sh, dt=F32: ph1.enter_context(nc.sbuf_tensor(nm, sh, dt))
    gate_w = sb1("gate_w", [128, KC * 2048], BF16)
    state_w = sb1("state_w", [128, KC * 1024], BF16)
    pr.new_phase()
    gate_w_trk, state_w_trk = fresh(gate_w, pr.w_sem()), fresh(state_w, pr.w_sem())
    pw_gate = load_weight("L0_attn_gate_w", gate_w, gate_w_trk, 2048)
    pw_state = load_weight("L0_attn_state_w", state_w, state_w_trk, 1024)

    x_t = [sb1(f"x_t{i}", [128, KC * NT]) for i in range(2)]
    x_t_trk = [fresh(x_t[0], pr.buf_sem()), fresh(x_t[1], pr.buf_sem())]
    x_bt = [sb1(f"x_bt{i}", [128, KC * NT], BF16) for i in range(2)]
    x_bt_trk = [fresh(x_bt[0], pr.w_sem()), fresh(x_bt[1], pr.w_sem())]
    xc_t = sb1("xc_t", [128, KC * NT])
    xc_trk = fresh(xc_t)
    t1_t = sb1("t1_t", [128, KC * NT])
    t1_trk = fresh(t1_t, pr.buf_sem())
    xn_t = sb1("xn_t", [128, KC * NT], BF16)
    xn_trk = fresh(xn_t)
    gate_s = sb1("gate_s", [128, KC * NT], BF16)
    gg_s = sb1("gg_s", [128, KC * NT], BF16)
    gate_s_trk, gg_trk = fresh(gate_s), fresh(gg_s)
    s1_c = sb1("s1_c", [128, NT])
    s1_trk = fresh(s1_c)
    dbu_c = sb1("dbu_c", [128, NT])
    dbu_trk = fresh(dbu_c)
    y_t = sb1("y_t", [128, KC * NT], BF16)
    y_trk = fresh(y_t, pr.buf_sem())
    rvec_f = sb1("rvec_f", [128, KC])
    rvec_f_trk = fresh(rvec_f)
    zrow = sb1("zrow", [1, NT])
    rowst = sb1("rowst", [1, 10 * NT])
    rows_trk = fresh(rowst, pr.buf_sem())

    rowsd = sb1("rowsd", [1, 4 * NT])
    rowsd_trk = fresh(rowsd, pr.buf_sem())
    vec(lambda e: e.memset(zrow[:], 0.0))

    def row(i):
        return rowst[:, i * NT:(i + 1) * NT]

    def rowd(i):
        return rowsd[:, i * NT:(i + 1) * NT]

    pv = None
    for k in range(KC):
        pv = vec(lambda e, k=k: e.reduce_sum(rvec_f[:, k:k + 1],
                                             state_w[:, k * 1024:(k + 1) * 1024],
                                             axis=AX.X),
                 deps=[pw_state] if k == 0 else (), writes=[rvec_f_trk])
    state_w_trk.write_at(pw_state)
    state_w_trk.read_at(pv)
    pt_rvec = sca(lambda e: e.activation(rvec[:], rvec_f[:], AF.Identity,
                                         scale=1.0 / 1024.0),
                  reads=[rvec_f_trk], writes=[rvec_trk])
    off_sb_ = BIAS_OFF["L0_attn_state_b"]
    pr.waits("tensor", psR_trk.before_write())
    pr.waits("tensor", biasv_trk.before_read())
    pt_mb = pr.do("tensor", lambda e: e.matmul(
        psR[0:1, 0:8], ones_colf[:], biasv[0:128, off_sb_:off_sb_ + 8],
        start=True, stop=True), inc="TS")
    psR_trk.write_at(pt_mb)
    biasv_trk.read_at(pt_mb)
    p1 = vec(lambda e: e.reduce_sum(mean_bs[:], psR[0:1, 0:8], axis=AX.X),
             deps=[pt_mb], writes=[meanb_trk])
    psR_trk.read_at(p1)
    pt_meanb = vec(lambda e: e.tensor_scalar_mul(mean_bs[:], mean_bs[:],
                                                 1.0 / 1024.0),
                   deps=[p1], reads=[meanb_trk], writes=[meanb_trk])

    def st_stats(j):
        par = j % 2
        xt, xt_trk = x_t[par], x_t_trk[par]
        xbt, xbt_trk = x_bt[par], x_bt_trk[par]
        dma("sync", sb_tile3(xt), dram_tile(x_v, j), wtrks=(xt_trk,))
        dma("gpsimd", sb_tile3(xbt), dram_tile(x_v, j), wtrks=(xbt_trk,))
        pt_mean = mm_acc(lambda k: ones_col[:], lambda k, b=xbt: chunk(b, k), KC,
                         psR, psR_trk, rtrks=[xbt_trk], rows=1)
        sq_sc = sq_ve = None
        for c in range(KC):
            if c % 2 == 0:
                sq_sc = sca(lambda e, c=c, b=xbt: e.activation(
                    chunk(b, c), chunk(b, c), AF.Square),
                    deps=[pt_mean] if c == 0 else (),
                    writes=[xbt_trk] if c == 0 else ())
            else:
                sq_ve = vec(lambda e, c=c, b=xbt: e.tensor_mul(
                    chunk(b, c), chunk(b, c), chunk(b, c)),
                    deps=[pt_mean] if c == 1 else (),
                    writes=[xbt_trk] if c == 1 else ())
        xbt_trk.write_at(sq_sc)
        xbt_trk.write_at(sq_ve)
        pt_ss = mm_acc2(lambda k: ones_col[:], lambda k, b=xbt: chunk(b, k), KC,
                        deps=[sq_sc, sq_ve], rtrks=[xbt_trk])
        pt_mu = vec(lambda e: e.tensor_scalar_mul(row(0), psR[0:1, 0:NT],
                                                  1.0 / 1024.0),
                    deps=[pt_mean], writes=[rows_trk])
        psR_trk.read_at(pt_mu)
        dma("scalar", rows_sp[R_MU1:R_MU1 + 1, j * NT:(j + 1) * NT], row(0),
            rtrks=(rows_trk,), wtrks=(dtrk["rows_sp"][j],))
        pt_m2 = vec(lambda e: e.tensor_scalar_mul(row(1), psR[32:33, 0:NT],
                                                  1.0 / 1024.0),
                    deps=[pt_ss], writes=[rows_trk])
        psR2_trk.read_at(pt_m2)
        pt_q = vec(lambda e: e.tensor_mul(row(8), row(0), row(0)), deps=[pt_mu])
        pt_var = vec(lambda e: e.tensor_sub(row(1), row(1), row(8)),
                     deps=[pt_m2, pt_q], writes=[rows_trk])
        pt_rs1 = rsqrt_row(row(2), row(1), deps=[pt_var], trk=rows_trk)
        dma("scalar", rows_sp[R_RS1:R_RS1 + 1, j * NT:(j + 1) * NT], row(2),
            rtrks=(rows_trk,), wtrks=(dtrk["rows_sp"][j],))
        pt_v = vec(lambda e: e.tensor_scalar_add(row(8), row(1), consts[:, 0:1]),
                   deps=[pt_var])
        pt_r = vec(lambda e: e.reciprocal(row(9), row(8)), deps=[pt_v])
        pt_v2 = vec(lambda e: e.tensor_mul(row(3), row(1), row(9)), deps=[pt_r])
        pt_rs2 = rsqrt_row(row(3), row(3), deps=[pt_v2], trk=rows_trk)
        pt_srow = vec(lambda e: e.tensor_mul(row(3), row(2), row(3)),
                      deps=[pt_rs1, pt_rs2], writes=[rows_trk])
        return pt_mu, pt_rs1, pt_srow

    def mk_apply(j, sstats):
        pt_mu, pt_rs1, pt_srow = sstats
        par = j % 2
        xt, xt_trk = x_t[par], x_t_trk[par]
        state = {"xn_pts": [None] * KC, "xc_pts": [None] * KC, "done": False}

        def step(m):
            if m == 0:
                state["mub"] = bcast(row(0), 0, deps=[pt_mu], rtrks=[rows_trk])
            if m < 4:
                for c in (2 * m, 2 * m + 1):
                    pt_xc = vec(lambda e, c=c, xt=xt: e.tensor_sub(
                        chunk(xc_t, c), chunk(xt, c), psB[0][0:128, 0:NT]),
                        deps=[state["mub"]] if c == 0 else (),
                        reads=[xt_trk] if c == 0 else (),
                        writes=[xc_trk] if c == 0 else ())
                    state["xc_pts"][c] = pt_xc
                    if c == KC - 1:
                        xc_trk.write_at(pt_xc)
                        xt_trk.read_at(pt_xc)
                        psB_trk[0].read_at(pt_xc)
            elif m == 4:
                state["rs1b"] = bcast(row(2), 0, deps=[pt_rs1, state["xc_pts"][7]],
                                      rtrks=[rows_trk])
                state["srb"] = bcast(row(3), 1, deps=[pt_srow], rtrks=[rows_trk])
            if m >= 4:
                for c in (2 * (m - 4), 2 * (m - 4) + 1):
                    pt_t1 = vec(lambda e, c=c: e.tensor_mul(
                        chunk(t1_t, c), chunk(xc_t, c), psB[0][0:128, 0:NT]),
                        deps=([state["rs1b"], state["xc_pts"][c]] if c < 2
                              else [state["xc_pts"][c]]),
                        writes=[t1_trk] if c == 0 else ())
                    pt_rr = vec(lambda e, c=c: e.tensor_mul(
                        chunk(xc_t, c), chunk(xc_t, c), psB[1][0:128, 0:NT]),
                        deps=[state["srb"], pt_t1] if c == 0 else [pt_t1])
                    xn_pt = sca(lambda e, c=c: e.activation(
                        chunk(xn_t, c), chunk(xc_t, c), AF.Identity,
                        scale=bias_col("L0_attn_norm1_w", c),
                        bias=bias_col("L0_attn_norm1_b", c)),
                        deps=[pt_rr],
                        writes=[xn_trk] if c == 0 else ())
                    state["xn_pts"][c] = xn_pt
                    if c == KC - 1:
                        t1_trk.write_at(pt_t1)
                        psB_trk[0].read_at(pt_t1)
                        xc_trk.write_at(pt_rr)
                        psB_trk[1].read_at(pt_rr)
                        xn_trk.write_at(xn_pt)
                        xc_trk.read_at(xn_pt)
                        dma("sync", dram_tile(t1_v, j), sb_tile3(t1_t),
                            rtrks=(t1_trk,), wtrks=(dtrk["t1_sp"][j],))
                        state["done"] = True
        return step, state

    def st_delta(j, apl):
        pt_dm = mm_acc(lambda k: rvec[:, k:k + 1], lambda k: chunk(xn_t, k), KC,
                       psR, psR_trk, kdeps=apl["xn_pts"],
                       rtrks_rec=[xn_trk], rtrks=[rvec_trk], rows=1)
        pt_z = vec(lambda e: e.tensor_scalar_add(rowd(0), psR[0:1, 0:NT],
                                                 mean_bs[:]),
                   deps=[pt_dm, pt_meanb], writes=[rowsd_trk])
        psR_trk.read_at(pt_z)
        pt_az = sca(lambda e: e.activation(rowd(2), rowd(0), AF.Abs),
                    deps=[pt_z])
        pt_ez = sca(lambda e: e.activation(rowd(2), rowd(2), AF.Exp, scale=-1.0),
                    deps=[pt_az])
        pt_lz = sca(lambda e: e.activation(rowd(2), rowd(2), AF.Ln,
                                           bias=consts[:, 1:2]), deps=[pt_ez])
        pt_rz = sca(lambda e: e.activation(rowd(3), rowd(0), AF.Relu),
                    deps=[pt_z])
        pt_delta = vec(lambda e: e.tensor_add(rowd(0), rowd(2), rowd(3)),
                       deps=[pt_lz, pt_rz], writes=[rowsd_trk])
        pt_dA = sca(lambda e: e.activation(rowd(1), rowd(0), AF.Exp, scale=-1.0),
                    deps=[pt_delta])
        pt_P = vec(lambda e: e.tensor_tensor_scan(rowd(3), rowd(1), zrow[:],
                                                  Pcarry[:], ALU.mult, ALU.add),
                   deps=[pt_dA], reads=[Pc_trk], writes=[rowsd_trk])
        pt_Pc = vec(lambda e: e.tensor_copy(Pcarry[:], rowd(3)[:, NT - 1:NT]),
                    deps=[pt_P], writes=[Pc_trk])
        dma("scalar", rows_sp[R_P:R_P + 1, j * NT:(j + 1) * NT], rowd(3),
            rtrks=(rowsd_trk,), wtrks=(dtrk["rows_sp"][j],))
        return pt_delta, pt_dA

    def st_gate(j, apl):
        uin_pts = [None] * KC
        gg_pts = [None] * KC
        for m in range(16):
            pb, ptrk = next_psM()
            pt_g = mm_acc(lambda k, m=m: wslice(gate_w, 2048, k, m),
                          lambda k: chunk(xn_t, k), KC, pb, ptrk,
                          kdeps=apl["xn_pts"], rtrks_rec=[xn_trk],
                          rtrks=[gate_w_trk])
            dstt, dtrk_ = (gate_s, gate_s_trk) if m < 8 else (gg_s, gg_trk)
            pe = sca(lambda e, pb=pb, c=m % 8, dstt=dstt, m=m: e.activation(
                chunk(dstt, c), pb[0:128, 0:NT], AF.Silu,
                bias=bias_col("L0_attn_gate_b", m)),
                deps=[pt_g], writes=[dtrk_])
            ptrk.read_at(pe)
            if m < 8:
                pt_ui = vec(lambda e, c=m: e.tensor_mul(
                    chunk(gate_s, c), chunk(xn_t, c), chunk(gate_s, c)),
                    deps=[pe])
                uin_pts[m] = pt_ui
                if m == 7:
                    gate_s_trk.write_at(pt_ui)
                    xn_trk.read_at(pt_ui)
            else:
                gg_pts[m - 8] = pe
        return uin_pts, gg_pts

    def st_state(j, uin_pts, gg_pts, dpts, apply_step):
        pt_delta, pt_dA = dpts
        pt_db = bcast(rowd(0), 2, deps=[pt_delta], rtrks=[rowsd_trk])
        pt_dab = bcast(rowd(1), 3, deps=[pt_dA], rtrks=[rowsd_trk])
        for m in range(8):
            pbu, ptu = next_psM()
            pt_u = mm_acc(lambda k, m=m: wslice(state_w, 1024, k, m),
                          lambda k: chunk(gate_s, k), KC, pbu, ptu,
                          kdeps=uin_pts, rtrks_rec=[gate_s_trk],
                          rtrks=[state_w_trk])
            pbb, ptb = next_psM()
            pt_bm = mm_acc(lambda k, m=m: wslice(state_w, 1024, k, m),
                           lambda k: chunk(gg_s, k), KC, pbb, ptb,
                           kdeps=gg_pts, rtrks_rec=[gg_trk],
                           rtrks=[state_w_trk])
            pt_s1 = sca(lambda e, pbu=pbu, m=m: e.add(
                s1_c[:], pbu[0:128, 0:NT], bias_col("L0_attn_state_b", m)),
                deps=[pt_u], writes=[s1_trk])
            ptu.read_at(pt_s1)
            pt_s2 = vec(lambda e, pbb=pbb, m=m: e.scalar_tensor_tensor(
                dbu_c[:], pbb[0:128, 0:NT], bias_col("L0_attn_state_b", m),
                s1_c[:], ALU.add, ALU.mult),
                deps=[pt_bm, pt_s1], reads=[s1_trk], writes=[dbu_trk])
            ptb.read_at(pt_s2)
            pt_dbu = vec(lambda e: e.tensor_mul(dbu_c[:], dbu_c[:],
                                                psB[2][0:128, 0:NT]),
                         deps=[pt_db, pt_s2], writes=[dbu_trk])
            psB_trk[2].read_at(pt_dbu)
            pt_sc = vec(lambda e, m=m: e.tensor_tensor_scan(
                chunk(y_t, m), psB[3][0:128, 0:NT], dbu_c[:],
                hstate[:, m:m + 1], ALU.mult, ALU.add),
                deps=[pt_dab, pt_dbu], reads=[hstate_trk, dbu_trk],
                writes=[y_trk])
            psB_trk[3].read_at(pt_sc)
            vec(lambda e, m=m: e.tensor_copy(hstate[:, m:m + 1],
                                             chunk(y_t, m)[:, NT - 1:NT]),
                deps=[pt_sc], reads=[y_trk], writes=[hstate_trk])
            if apply_step is not None:
                apply_step(m)
        dma("scalar", dram_tile(yloc_v, j), sb_tile3(y_t),
            rtrks=(y_trk,), wtrks=(dtrk["yloc_sp"][j],))

    sstats = st_stats(0)
    apply_step, apl = mk_apply(0, sstats)
    for m_ in range(8):
        apply_step(m_)
    dpts = st_delta(0, apl)
    for j in range(NTIL):
        up, gp = st_gate(j, apl)
        if j + 1 < NTIL:
            sstats = st_stats(j + 1)
            apply_next, apl_next = mk_apply(j + 1, sstats)
        else:
            apply_next, apl_next = None, None
        st_state(j, up, gp, dpts, apply_next)
        if apl_next is not None:
            apl = apl_next
            dpts = st_delta(j + 1, apl)

    def emit_collective():
        pr.waits("tensor", psB_trk[0].before_write())
        pr.waits("tensor", Pc_trk.before_read())
        pt_ar = pr.do("tensor", lambda e: e.matmul(
            psB[0][0:128, 0:1], ones_row[:], Pcarry[:], start=True, stop=True),
            inc="TS")
        psB_trk[0].write_at(pt_ar)
        Pc_trk.read_at(pt_ar)
        pm1 = vec(lambda e: e.tensor_copy(msg[:, 0:1], psB[0][0:128, 0:1]),
                  deps=[pt_ar], writes=[msg_trk])
        psB_trk[0].read_at(pm1)
        pm2 = vec(lambda e: e.tensor_copy(msg[:, 1:9], hstate[:]),
                  reads=[hstate_trk], writes=[msg_trk])
        dma("sync", cc_in.rearrange("(p c) -> p c", c=9), msg[:],
            rtrks=(msg_trk,), wtrks=(dtrk["cc_in"],))
        pr.waits("gpsimd", dtrk["cc_in"].before_read())
        pt_cc = pr.do("gpsimd", lambda e: e.collective_compute(
            "AllGather", ALU.bypass, replica_groups=[list(range(8))],
            ins=[cc_in], outs=[cc_out]), inc="CC")
        dtrk["cc_in"].read_at(pt_cc)
        dtrk["cc_out"].write_at(pt_cc)
        dma("scalar", gath[:].rearrange("p (r c) -> p r c", c=9),
            cc_out.rearrange("r (p c) -> p r c", c=9),
            wtrks=(gath_trk,), rtrks=(dtrk["cc_out"],))
        pt_f = pt_hin0
        for jj in range(8):
            pa = vec(lambda e, jj=jj: e.tensor_scalar(
                aeff[:], gath[:, jj * 9:jj * 9 + 1], foldm[:, jj:jj + 1],
                foldm[:, 8 + jj:9 + jj], ALU.mult, ALU.add),
                deps=[pt_f],
                reads=[gath_trk, foldm_trk] if jj == 0 else ())
            ph_ = vec(lambda e, jj=jj: e.tensor_scalar_mul(
                hmsk[:], gath[:, jj * 9 + 1:jj * 9 + 9], foldm[:, jj:jj + 1]),
                deps=[pt_f])
            pt_f = vec(lambda e: e.scalar_tensor_tensor(
                h_in[:], h_in[:], aeff[:], hmsk[:], ALU.mult, ALU.add),
                deps=[pa, ph_, pt_f])
        hin_trk.write_at(pt_f)
        gath_trk.read_at(pt_f)
        if hin_dbg is not None:
            pdbg = vec(lambda e: e.tensor_copy(msg[:, 0:8], h_in[:]), deps=[pt_f],
                       writes=[msg_trk])
            dma("sync", hin_dbg, msg[:].rearrange("p c -> p c") if False else msg[:],
                rtrks=(msg_trk,))

    retire((gate_w, gate_w_trk), (state_w, state_w_trk),
           (x_t[0], x_t_trk[0]), (x_t[1], x_t_trk[1]),
           (x_bt[0], x_bt_trk[0]), (x_bt[1], x_bt_trk[1]),
           (xc_t, xc_trk), (t1_t, t1_trk), (xn_t, xn_trk),
           (gate_s, gate_s_trk), (gg_s, gg_trk),
           (s1_c, s1_trk), (dbu_c, dbu_trk), (y_t, y_trk),
           (rvec_f, rvec_f_trk), (zrow, None),
           (rowst, rows_trk), (rowsd, rowsd_trk))
    ph1.close()

    ph2 = ES()
    sb2 = lambda nm, sh, dt=F32: ph2.enter_context(nc.sbuf_tensor(nm, sh, dt))
    t1b_t = [sb2(f"t1b{i}", [128, KC * NT]) for i in range(2)]
    yb_t = [sb2(f"yb{i}", [128, KC * NT], BF16) for i in range(2)]
    out_w = sb2("out_w", [128, KC * 1024], BF16)
    pr.new_phase()
    t1b_trk = [fresh(t1b_t[0], pr.buf_sem()), fresh(t1b_t[1], pr.buf_sem())]
    yb_trk = [fresh(yb_t[0], pr.buf_sem()), fresh(yb_t[1], pr.buf_sem())]
    out_w_trk = fresh(out_w, pr.w_sem())
    ow_pts = []
    load_weight("L0_attn_out_w", out_w, out_w_trk, 1024, pts=ow_pts)

    def ow_sl(k, m):
        return wslice(out_w, 1024, k, m)
    tt_t = [sb2(f"tt_t{i}", [128, KC * NT]) for i in range(2)]
    tt_trk = [fresh(tt_t[0], pr.buf_sem()), fresh(tt_t[1], pr.buf_sem())]
    tb_t = [sb2(f"tb_t{i}", [128, KC * NT], BF16) for i in range(2)]
    tb_trk = [fresh(tb_t[0]), fresh(tb_t[1])]
    xc2_t = sb2("xc2_t", [128, KC * NT])
    xc2_trk = fresh(xc2_t)
    x2n_t = sb2("x2n_t", [128, KC * NT], BF16)
    x2n_trk = fresh(x2n_t, pr.buf_sem())
    pbc = sb2("pbc", [128, NT])
    pbc_trk = fresh(pbc, pr.buf_sem())
    rows2 = sb2("rows2", [1, 2 * NT])
    rows2_trk = fresh(rows2)

    hin_b = sb2("hin_b", [128, KC], BF16)
    hin_b_trk = fresh(hin_b)
    w_h = sb2("w_h", [128, KC])
    w_h_trk = fresh(w_h)
    def emit_wh():
        pt_hb = vec(lambda e: e.tensor_copy(hin_b[:], h_in[:]),
                    reads=[hin_trk], writes=[hin_b_trk])
        for mc_ in range(KC):
            pb_, ptrk_ = next_psM()
            pt_wh = mm_acc(lambda k, mc_=mc_: ow_sl(k, mc_),
                           lambda k: hin_b[:, k:k + 1], KC, pb_, ptrk_,
                           deps=[pt_hb], kdeps=ow_pts,
                           rtrks_rec=[out_w_trk],
                           rtrks=[hin_b_trk], cols=1)
            pe_ = vec(lambda e, pb_=pb_, mc_=mc_: e.tensor_copy(
                w_h[:, mc_:mc_ + 1], pb_[0:128, 0:1]),
                deps=[pt_wh], writes=[w_h_trk] if mc_ == 0 else ())
            w_h_trk.write_at(pe_)
            ptrk_.read_at(pe_)

    def ph2_head(j, gen=None):
        par = j % 2
        t1t, t1trk = t1b_t[par], t1b_trk[par]
        yt, ytrk = yb_t[par], yb_trk[par]
        tt, tttrk = tt_t[par], tt_trk[par]
        dma("sync", sb_tile3(t1t), dram_tile(t1_v, j), wtrks=(t1trk,),
            rtrks=(dtrk["t1_sp"][j],))
        dma("sync", sb_tile3(yt), dram_tile(yloc_v, j), wtrks=(ytrk,),
            rtrks=(dtrk["yloc_sp"][j],))
        for m in range(8):
            pb, ptrk = next_psM()
            pt_s = mm_acc(lambda k, m=m: ow_sl(k, m),
                          lambda k, yt=yt: chunk(yt, k), KC, pb, ptrk,
                          kdeps=ow_pts,
                          rtrks_rec=[out_w_trk],
                          rtrks=[ytrk])
            pt_t = vec(lambda e, pb=pb, m=m, t1t=t1t, tt=tt:
                       e.scalar_tensor_tensor(
                chunk(tt, m), pb[0:128, 0:NT], bias_col("L0_attn_out_b", m),
                chunk(t1t, m), ALU.add, ALU.add),
                deps=[pt_s], reads=[t1trk] if m == 0 else (),
                writes=[tttrk] if m == 0 else ())
            ptrk.read_at(pt_t)
            tttrk.write_at(pt_t)
            t1trk.read_at(pt_t)
            if gen is not None:
                next(gen, None)

    def ph2_tail(j):
        par = j % 2
        tt, tttrk = tt_t[par], tt_trk[par]
        tb, tbtrk = tb_t[par], tb_trk[par]
        psrc = bass.AP(rows_sp.tensor, R_P * T + j * NT, [[0, 128], [1, NT]])
        pt_pb = dma("sync", pbc[:], psrc, wtrks=(pbc_trk,),
                    rtrks=(dtrk["rows_sp"][j],))
        pt_t2 = None
        for m in range(8):
            pt_t2 = vec(lambda e, m=m, tt=tt: e.scalar_tensor_tensor(
                chunk(tt, m), pbc[0:128, 0:NT], w_h[:, m:m + 1],
                chunk(tt, m), ALU.mult, ALU.add),
                deps=[pt_pb] if m == 0 else (),
                reads=([w_h_trk, tttrk, pbc_trk] if m == 0 else ()),
                writes=[tttrk] if m == 0 else ())
            tttrk.write_at(pt_t2)
        pbc_trk.read_at(pt_t2)
        yield
        dma("sync", dram_tile(t_v, j), sb_tile3(tt), rtrks=(tttrk,),
            wtrks=(dtrk["t_sp"][j],))
        c_sc = c_ve = None
        for m in range(KC):
            if m % 2 == 0:
                c_sc = sca(lambda e, m=m, tb=tb, tt=tt:
                           e.copy(chunk(tb, m), chunk(tt, m)),
                           reads=[tttrk] if m == 0 else (),
                           writes=[tbtrk] if m == 0 else ())
            else:
                c_ve = vec(lambda e, m=m, tb=tb, tt=tt:
                           e.tensor_copy(chunk(tb, m), chunk(tt, m)),
                           reads=[tttrk] if m == 1 else (),
                           writes=[tbtrk] if m == 1 else ())
        tbtrk.write_at(c_sc)
        tbtrk.write_at(c_ve)
        tttrk.read_at(c_sc)
        tttrk.read_at(c_ve)
        yield
        do_b = ln_stats1(tb, tbtrk, rows2[:, 0:NT],
                         rows2[:, NT:2 * NT], rows2_trk, 1, 2,
                         deps=[c_sc, c_ve], defer=True)
        yield
        pt_mub, pt_rsb = do_b()
        pt_xc = None
        for c in range(KC):
            pt_xc = vec(lambda e, c=c, tt=tt: e.tensor_sub(
                chunk(xc2_t, c), chunk(tt, c), psB[1][0:128, 0:NT]),
                deps=[pt_mub] if c == 0 else (),
                reads=[tttrk] if c == 0 else (),
                writes=[xc2_trk] if c == 0 else ())
        xc2_trk.write_at(pt_xc)
        tttrk.read_at(pt_xc)
        psB_trk[1].read_at(pt_xc)
        yield
        pt_r2 = None
        for c in range(KC):
            pt_r2 = vec(lambda e, c=c: e.tensor_mul(chunk(xc2_t, c),
                                                    chunk(xc2_t, c),
                                                    psB[2][0:128, 0:NT]),
                        deps=[pt_rsb, pt_xc] if c == 0 else ())
        xc2_trk.write_at(pt_r2)
        psB_trk[2].read_at(pt_r2)
        yield
        xn2_pt = None
        for c in range(KC):
            xn2_pt = sca(lambda e, c=c: e.activation(
                chunk(x2n_t, c), chunk(xc2_t, c), AF.Identity,
                scale=bias_col("L0_attn_norm2_w", c),
                bias=bias_col("L0_attn_norm2_b", c)),
                deps=[pt_r2] if c == 0 else (),
                writes=[x2n_trk] if c == 0 else ())
        x2n_trk.write_at(xn2_pt)
        xc2_trk.read_at(xn2_pt)
        dma("scalar", dram_tile(x2n_v, j), sb_tile3(x2n_t), rtrks=(x2n_trk,),
            wtrks=(dtrk["x2n_sp"][j],))
        yield

    def chain2(*gens):
        for g in gens:
            for x in g:
                yield x

    for j in range(NTIL):
        g2 = ph2_tail(j - 1) if j >= 2 else None
        ph2_head(j, g2)
        if j == 0:
            emit_collective()
        elif j == 1:
            emit_wh()
            for _ in ph2_tail(0):
                pass
        if g2 is not None:
            for _ in g2:
                pass
    for _ in ph2_tail(NTIL - 1):
        pass

    retire((out_w, out_w_trk), (t1b_t[0], t1b_trk[0]),
           (t1b_t[1], t1b_trk[1]), (yb_t[0], yb_trk[0]), (yb_t[1], yb_trk[1]),
           (tt_t[0], tt_trk[0]), (tt_t[1], tt_trk[1]), (tb_t[0], tb_trk[0]),
           (tb_t[1], tb_trk[1]), (xc2_t, xc2_trk), (x2n_t, x2n_trk),
           (pbc, pbc_trk), (rows2, rows2_trk), (hin_b, hin_b_trk),
           (w_h, w_h_trk))
    ph2.close()
    retire((hstate, hstate_trk), (h_in, hin_trk), (Pcarry, Pc_trk),
           (foldm, foldm_trk), (mean_bs, meanb_trk), (rvec, rvec_trk),
           (gath, gath_trk), (msg, msg_trk), (aeff, None), (hmsk, None))
    ph12.close()

    ph3 = ES()
    sb3 = lambda nm, sh, dt=F32: ph3.enter_context(nc.sbuf_tensor(nm, sh, dt))
    x2nb_t = [sb3(f"x2nb{i}", [128, KC * NT], BF16) for i in range(2)]
    w1 = sb3("mlp_w1", [128, KC * 4096], BF16)
    w2 = sb3("mlp_w2", [128, 32 * 1024], BF16)
    pr.new_phase()
    x2nb_trk = [fresh(x2nb_t[0], pr.buf_sem()), fresh(x2nb_t[1], pr.buf_sem())]
    w1_trk, w2_trk = fresh(w1, pr.w_sem()), fresh(w2, pr.w_sem())
    w1_pts = []
    load_weight("L0_attn_mlp_w1", w1, w1_trk, 4096, pts=w1_pts)
    pw2 = load_weight("L0_attn_mlp_w2", w2, w2_trk, 1024)

    def w1_sl(k, mc):
        return wslice(w1, 4096, k, mc)
    h_t = sb3("h_t", [128, 32 * NT], BF16)
    h_trk = fresh(h_t)
    sc_c = [sb3(f"sc_c{i}", [128, NT]) for i in range(2)]
    sc_trk = [fresh(sc_c[0], pr.buf_sem()), fresh(sc_c[1], pr.buf_sem())]
    ao_t = sb3("ao_t", [128, KC * NT])
    ao_trk = fresh(ao_t, pr.buf_sem())
    rows3 = sb3("rows3", [1, 2 * NT])
    rows3_trk = fresh(rows3)

    def row3(i):
        return rows3[:, i * NT:(i + 1) * NT]

    def ph3_headA(j, gen=None):
        par = j % 2
        xb, xbtrk = x2nb_t[par], x2nb_trk[par]
        dma("sync", sb_tile3(xb), dram_tile(x2n_v, j), wtrks=(xbtrk,),
            rtrks=(dtrk["x2n_sp"][j],))
        h_pts = [None] * 32
        for mc in range(32):
            pb, ptrk = next_psM()
            pt_h = mm_acc(lambda k, mc=mc: w1_sl(k, mc),
                          lambda k, xb=xb: chunk(xb, k), KC, pb, ptrk,
                          kdeps=w1_pts, rtrks_rec=[w1_trk],
                          rtrks=[xbtrk])
            pe = sca(lambda e, pb=pb, mc=mc: e.activation(
                chunk(h_t, mc), pb[0:128, 0:NT], AF.Gelu,
                bias=bias_col("L0_attn_mlp_b1", mc)),
                deps=[pt_h], writes=[h_trk] if mc == 0 else ())
            h_trk.write_at(pe)
            ptrk.read_at(pe)
            h_pts[mc] = pe
            if gen is not None and mc % 3 == 2:
                next(gen, None)
        return h_pts

    def ph3_headB(j, h_pts):
        for m in range(8):
            pb, ptrk = next_psM()
            pt_o = mm_acc(lambda k, m=m: wslice(w2, 1024, k, m),
                          lambda k: chunk(h_t, k), 32, pb, ptrk,
                          deps=[pw2], kdeps=h_pts, rtrks_rec=[h_trk],
                          rtrks=[w2_trk])
            sc, sct = sc_c[m % 2], sc_trk[m % 2]
            dma("sync", sc[:], t_v[:, m, j * NT:(j + 1) * NT],
                wtrks=(sct,), rtrks=(dtrk["t_sp"][j],))
            pt_a = vec(lambda e, pb=pb, m=m, sc=sc: e.scalar_tensor_tensor(
                chunk(ao_t, m), pb[0:128, 0:NT], bias_col("L0_attn_mlp_b2", m),
                sc[:], ALU.add, ALU.add),
                deps=[pt_o], reads=[sct],
                writes=[ao_trk] if m == 0 else ())
            ptrk.read_at(pt_a)
            ao_trk.write_at(pt_a)

    def ph3_tailprod(j):
        par = j % 2
        stg, stgtrk = x2nb_t[par], x2nb_trk[par]
        c_sc = c_ve = None
        for c in range(KC):
            if c % 2 == 0:
                c_sc = sca(lambda e, c=c, stg=stg: e.copy(chunk(stg, c),
                                                          chunk(ao_t, c)),
                           reads=[ao_trk] if c == 0 else (),
                           writes=[stgtrk] if c == 0 else ())
            else:
                c_ve = vec(lambda e, c=c, stg=stg: e.tensor_copy(
                    chunk(stg, c), chunk(ao_t, c)),
                    reads=[ao_trk] if c == 1 else (),
                    writes=[stgtrk] if c == 1 else ())
        stgtrk.write_at(c_sc)
        stgtrk.write_at(c_ve)
        ao_trk.read_at(c_sc)
        ao_trk.read_at(c_ve)
        yield
        do_b = ln_stats1(stg, stgtrk, row3(0), row3(1),
                         rows3_trk, 0, 1, deps=[c_sc, c_ve], defer=True)
        yield
        pt_mub, pt_rsb = do_b()
        pt_xc = None
        for c in range(KC):
            pt_xc = vec(lambda e, c=c: e.tensor_sub(chunk(ao_t, c),
                                                    chunk(ao_t, c),
                                                    psB[0][0:128, 0:NT]),
                        deps=[pt_mub] if c == 0 else (),
                        writes=[ao_trk] if c == 0 else ())
        ao_trk.write_at(pt_xc)
        psB_trk[0].read_at(pt_xc)
        yield
        pt_x1 = None
        for c in range(KC):
            pt_m = vec(lambda e, c=c: e.tensor_mul(chunk(ao_t, c),
                                                   chunk(ao_t, c),
                                                   psB[1][0:128, 0:NT]),
                       deps=[pt_rsb, pt_xc] if c == 0 else ())
            sc, sct = sc_c[c % 2], sc_trk[c % 2]
            dma("sync", sc[:], x_v[:, c, j * NT:(j + 1) * NT],
                wtrks=(sct,))
            pt_x1 = vec(lambda e, c=c, sc=sc: e.tensor_add(chunk(ao_t, c),
                                                           chunk(ao_t, c),
                                                           sc[:]),
                        deps=[pt_m], reads=[sct])
        ao_trk.write_at(pt_x1)
        psB_trk[1].read_at(pt_x1)
        dma("sync", dram_tile(x1_v, j), sb_tile3(ao_t), rtrks=(ao_trk,),
            wtrks=(dtrk["x1_sp"][j],))
        yield
        c_sc = c_ve = None
        for c in range(KC):
            if c % 2 == 0:
                c_sc = sca(lambda e, c=c, stg=stg: e.copy(chunk(stg, c),
                                                          chunk(ao_t, c)),
                           reads=[ao_trk] if c == 0 else (),
                           writes=[stgtrk] if c == 0 else ())
            else:
                c_ve = vec(lambda e, c=c, stg=stg: e.tensor_copy(
                    chunk(stg, c), chunk(ao_t, c)),
                    reads=[ao_trk] if c == 1 else (),
                    writes=[stgtrk] if c == 1 else ())
        stgtrk.write_at(c_sc)
        stgtrk.write_at(c_ve)
        ao_trk.read_at(c_sc)
        ao_trk.read_at(c_ve)
        yield
        do_b = ln_stats1(stg, stgtrk, row3(0), row3(1),
                         rows3_trk, 0, 1, deps=[c_sc, c_ve], defer=True)
        yield
        pt_mub, pt_rsb = do_b()
        pt_xc = None
        for c in range(KC):
            pt_xc = vec(lambda e, c=c: e.tensor_sub(chunk(ao_t, c),
                                                    chunk(ao_t, c),
                                                    psB[0][0:128, 0:NT]),
                        deps=[pt_mub] if c == 0 else (),
                        writes=[ao_trk] if c == 0 else ())
        ao_trk.write_at(pt_xc)
        psB_trk[0].read_at(pt_xc)
        yield
        pt_n = None
        for c in range(KC):
            pt_n = vec(lambda e, c=c, stg=stg: e.tensor_mul(
                chunk(stg, c), chunk(ao_t, c), psB[1][0:128, 0:NT]),
                deps=[pt_rsb, pt_xc] if c == 0 else (),
                writes=[stgtrk] if c == 0 else ())
        stgtrk.write_at(pt_n)
        ao_trk.read_at(pt_n)
        psB_trk[1].read_at(pt_n)
        dma("scalar", dram_tile(x1n_v, j), sb_tile3(stg), rtrks=(stgtrk,),
            wtrks=(dtrk["x1n_sp"][j],))
        yield

    gen3 = None
    for j in range(NTIL):
        gen3 = ph3_tailprod(j - 1) if j > 0 else None
        h_pts = ph3_headA(j, gen3)
        if gen3 is not None:
            for _ in gen3:
                pass
        ph3_headB(j, h_pts)
    for _ in ph3_tailprod(NTIL - 1):
        pass

    retire((w1, w1_trk), (w2, w2_trk), (x2nb_t[0], x2nb_trk[0]),
           (x2nb_t[1], x2nb_trk[1]), (h_t, h_trk), (sc_c[0], sc_trk[0]), (sc_c[1], sc_trk[1]),
           (ao_t, ao_trk), (rows3, rows3_trk))
    ph3.close()

    aold_c = [sb(f"aold_c{i}", [128, NT]) for i in range(2)]
    aold_trk = [fresh(aold_c[0], pr.buf_sem()), fresh(aold_c[1], pr.buf_sem())]
    wbc = [sb(f"wbc{i}", [128, NT]) for i in range(2)]
    wbct = [fresh(wbc[0], pr.buf_sem()), fresh(wbc[1], pr.buf_sem())]

    def wrow_dma_bcast(par, ridx, j, q="sync"):
        src = bass.AP(rows_sp.tensor, ridx * T + j * NT, [[0, 128], [1, NT]])
        return dma(q, wbc[par][:], src, wtrks=(wbct[par],),
                   rtrks=(dtrk["rows_sp"][j],))

    def acc_evac_rmw(pb, ptrk, m, j, bias_name, pt_wb, src_acc,
                     dst_t, dst_trk, pt_o=None):
        ac, at = aold_c[m % 2], aold_trk[m % 2]
        wb, wbt = wbc[j % 2], wbct[j % 2]
        dma("sync", ac[:], acc_v[src_acc][:, m, j * NT:(j + 1) * NT],
            wtrks=(at,), rtrks=(dtrk[f"acc{src_acc}_sp"][j],))
        pt_t = vec(lambda e, pb=pb, m=m, wb=wb: e.scalar_tensor_tensor(
            chunk(dst_t, m), pb[0:128, 0:NT], bias_col(bias_name, m),
            wb[0:128, 0:NT], ALU.add, ALU.mult),
            deps=[pt_o, pt_wb], reads=[wbt],
            writes=[dst_trk] if m == 0 else ())
        dst_trk.write_at(pt_t)
        ptrk.read_at(pt_t)
        pt_a = vec(lambda e, m=m, ac=ac: e.tensor_add(
            chunk(dst_t, m), chunk(dst_t, m), ac[:]),
            deps=[pt_t], reads=[at])
        dst_trk.write_at(pt_a)
        return pt_a

    def ph_e0(half):
        ph = ES()
        sbp = lambda nm, sh, dt=F32: ph.enter_context(
            nc.sbuf_tensor(f"{nm}_{half}", sh, dt))
        lo, hi = (0, 2048) if half == 0 else (2048, 4096)
        xn8_t = [sbp(f"e0x{i}", [128, KC * NT], BF16) for i in range(2)]
        if half == 0:
            rw = sbp("router_w", [128, KC * 4], BF16)
            ident4 = sbp("ident4", [128, 128])
            lsb = sbp("lsb", [4, NT])
            lt = sbp("lt", [128, 16])
            gbuf = sbp("gbuf", [128, 64])
            wtok = sbp("wtok", [128, 12])
            wrows = sbp("wrows", [4, NT])
        w1h = sbp("sw1h", [128, KC * 2048], BF16)
        w3h = sbp("sw3h", [128, KC * 2048], BF16)
        w2h = sbp("sw2h", [128, 16 * 1024], BF16)
        pr.new_phase()
        xn8_trk = [fresh(xn8_t[0], pr.buf_sem()),
                   fresh(xn8_t[1], pr.buf_sem())]
        if half == 0:
            rwt = fresh(rw, pr.w_sem())
            pwr = load_weight("L0_ffn_router_w", rw, rwt, 4)
            ident4_trk = fresh(ident4, pr.buf_sem())
            dma("sync", ident4[:], ident_d, wtrks=(ident4_trk,))
            lsb_trk = fresh(lsb)
            lt_trk = fresh(lt)
            gbuf_trk = fresh(gbuf)
            wtok_trk = fresh(wtok)
            wrows_trk = fresh(wrows, pr.buf_sem())
        w1t, w3t, w2t = (fresh(w1h, pr.w_sem()), fresh(w3h, pr.w_sem()),
                         fresh(w2h, pr.w_sem()))
        pww1 = load_weight("L0_ffn_sw_w1", w1h, w1t, 2048, src_lo=lo, src_hi=hi)
        pww3 = load_weight("L0_ffn_sw_w3", w3h, w3t, 2048, src_lo=lo, src_hi=hi)
        pww2 = load_weight("L0_ffn_sw_w2", w2h, w2t, 1024,
                           krange=range(lo // 128, hi // 128))
        h0s = sbp("h0s", [128, 16 * NT], BF16)
        h0s_trk = fresh(h0s)
        acc_t = sbp("e0acc", [128, KC * NT])
        acc_trk = fresh(acc_t, pr.buf_sem())

        def e0_router(j, xn8, xn8t, cell):
            pt_l = mm_acc(lambda k: rw[:, k * 4:(k + 1) * 4],
                          lambda k, xn8=xn8: chunk(xn8, k), KC, psR, psR_trk,
                          deps=[pwr], rtrks_rec=[],
                          rtrks=[rwt, xn8t], rows=4)
            pt_le = sca(lambda e: e.activation(lsb[:], psR[0:4, 0:NT],
                                               AF.Identity, bias=biasrl[:]),
                        deps=[pt_l], reads=[biasrl_trk],
                        writes=[lsb_trk])
            psR_trk.read_at(pt_le)
            yield
            pr.waits("tensor", psB_trk[2].before_write())
            pr.waits("tensor", ident4_trk.before_read())
            pr.waits("tensor", lsb_trk.before_read())
            pt_tr = None
            for g in range(4):
                pt_tr = pr.do("tensor", lambda e, g=g: e.transpose(
                    psB[2][0:128, g * 4:(g + 1) * 4],
                    lsb[0:4, g * 128:(g + 1) * 128], ident4[0:4, 0:4]),
                    inc="TS" if g == 3 else None)
            psB_trk[2].write_at(pt_tr)
            lsb_trk.read_at(pt_tr)
            ident4_trk.read_at(pt_tr)
            pt_lt = vec(lambda e: e.tensor_copy(lt[:], psB[2][0:128, 0:16]),
                        deps=[pt_tr], writes=[lt_trk])
            psB_trk[2].read_at(pt_lt)
            yield
            lt3 = lt[:].rearrange("p (g e) -> p g e", e=4)
            pairs = [(0, 1), (0, 2), (0, 3), (1, 2), (1, 3), (2, 3)]
            cmp_pts = {}
            for pi, (i, jd) in enumerate(pairs):
                vec(lambda e, pi=pi, i=i, jd=jd: e.tensor_tensor(
                    gbuf[:, pi * 4:(pi + 1) * 4], lt3[:, :, i], lt3[:, :, jd],
                    ALU.is_ge),
                    deps=[pt_lt],
                    writes=[gbuf_trk] if pi == 0 else ())
                cmp_pts[pi] = vec(lambda e, pi=pi, i=i, jd=jd: e.tensor_tensor(
                    gbuf[:, 24 + pi * 4:24 + (pi + 1) * 4],
                    lt3[:, :, i], lt3[:, :, jd], ALU.is_lt),
                    deps=[pt_lt])
            gbuf_trk.write_at(cmp_pts[5])
            seen = set()
            pt_rank = cmp_pts[5]
            for pi, (i, jd) in enumerate(pairs):
                for tgt, src_col in ((jd, pi * 4), (i, 24 + pi * 4)):
                    col = 48 + tgt * 4
                    if tgt not in seen:
                        seen.add(tgt)
                        pt_rank = vec(lambda e, col=col, sc=src_col:
                                      e.tensor_copy(
                            gbuf[:, col:col + 4], gbuf[:, sc:sc + 4]),
                            deps=[cmp_pts[pi], pt_rank])
                    else:
                        pt_rank = vec(lambda e, col=col, sc=src_col:
                                      e.tensor_add(
                            gbuf[:, col:col + 4], gbuf[:, col:col + 4],
                            gbuf[:, sc:sc + 4]),
                            deps=[cmp_pts[pi], pt_rank])
            gbuf_trk.write_at(pt_rank)
            pt_msk = vec(lambda e: e.tensor_single_scalar(
                gbuf[:, 48:64], gbuf[:, 48:64], 1.5, ALU.is_le),
                deps=[pt_rank], writes=[gbuf_trk])
            yield
            pt_mx = vec(lambda e: e.reduce_max(wtok[:, 0:4], lt3, axis=AX.X),
                        deps=[pt_lt], writes=[wtok_trk])
            mx_b = bass.AP(wtok, 0, [[12, 128], [1, 4], [0, 4]])
            pt_sub = vec(lambda e: e.tensor_sub(lt3, lt3, mx_b),
                         deps=[pt_mx], reads=[wtok_trk], writes=[lt_trk])
            pt_ex = sca(lambda e: e.activation(lt[:], lt[:], AF.Exp),
                        deps=[pt_sub], writes=[lt_trk])
            msk_b = bass.AP(gbuf, 48, [[64, 128], [1, 4], [4, 4]])
            pt_em = vec(lambda e: e.tensor_mul(lt3, lt3, msk_b),
                        deps=[pt_ex, pt_msk], writes=[lt_trk])
            gbuf_trk.read_at(pt_em)
            pt_s = vec(lambda e: e.reduce_sum(wtok[:, 4:8], lt3, axis=AX.X),
                       deps=[pt_em], writes=[wtok_trk])
            pt_rc = vec(lambda e: e.reciprocal(wtok[:, 8:12], wtok[:, 4:8]),
                        deps=[pt_s], writes=[wtok_trk])
            rc_b = bass.AP(wtok, 8, [[12, 128], [1, 4], [0, 4]])
            pt_w = vec(lambda e: e.tensor_mul(lt3, lt3, rc_b),
                       deps=[pt_rc], writes=[lt_trk])
            wtok_trk.read_at(pt_w)
            yield
            pr.waits("tensor", psB_trk[3].before_write())
            pr.waits("tensor", lt_trk.before_read())
            pt_tb = None
            for g in range(4):
                pt_tb = pr.do("tensor", lambda e, g=g: e.transpose(
                    psB[3][0:4, g * 128:(g + 1) * 128],
                    lt[:, g * 4:(g + 1) * 4], ident4[:]),
                    inc="TS" if g == 3 else None)
            psB_trk[3].write_at(pt_tb)
            lt_trk.read_at(pt_tb)
            ident4_trk.read_at(pt_tb)
            pt_wr = vec(lambda e: e.tensor_copy(wrows[:], psB[3][0:4, 0:NT]),
                        deps=[pt_tb], writes=[wrows_trk])
            psB_trk[3].read_at(pt_wr)
            for ei in range(4):
                dma("scalar", rows_sp[R_W0 + ei:R_W0 + ei + 1,
                                      j * NT:(j + 1) * NT],
                    wrows[ei:ei + 1, :], rtrks=(wrows_trk,),
                    wtrks=(dtrk["rows_sp"][j],))
            cell["pt"] = wrow_dma_bcast(j % 2, R_W0, j)
            yield

        def e0_gemms(j, pt_w0b, xn8, xn8t, xn_pts=None, rgen=None,
                     cell=None):
            h0_pts = [None] * 16
            for mc in range(16):
                pb1, pt1 = next_psM()
                pm1 = mm_acc(lambda k, mc=mc: wslice(w1h, 2048, k, mc),
                             lambda k, xn8=xn8: chunk(xn8, k), KC, pb1, pt1,
                             deps=[pww1], kdeps=xn_pts, rtrks_rec=[xn8t],
                             rtrks=[w1t])
                pb2, pt2 = next_psM()
                pm2 = mm_acc(lambda k, mc=mc: wslice(w3h, 2048, k, mc),
                             lambda k, xn8=xn8: chunk(xn8, k), KC, pb2, pt2,
                             deps=[pww3], kdeps=xn_pts, rtrks_rec=[xn8t],
                             rtrks=[w3t])
                ps_ = sca(lambda e, pb1=pb1, mc=mc: e.activation(
                    chunk(h0s, mc), pb1[0:128, 0:NT], AF.Silu),
                    deps=[pm1], writes=[h0s_trk] if mc == 0 else ())
                h0s_trk.write_at(ps_)
                pt1.read_at(ps_)
                pv_ = vec(lambda e, pb2=pb2, mc=mc: e.tensor_mul(
                    chunk(h0s, mc), chunk(h0s, mc), pb2[0:128, 0:NT]),
                    deps=[pm2, ps_], writes=[h0s_trk])
                pt2.read_at(pv_)
                h0_pts[mc] = pv_
                if rgen is not None and mc % 3 == 2:
                    next(rgen, None)
            if rgen is not None:
                for _ in rgen:
                    pass
                pt_w0b = cell["pt"]
            wb, wbt = wbc[j % 2], wbct[j % 2]
            for m in range(8):
                pb, ptrk = next_psM()
                pt_o = mm_acc(lambda k, m=m: wslice(w2h, 1024, k, m),
                              lambda k: chunk(h0s, k), 16, pb, ptrk,
                              deps=[pww2], kdeps=h0_pts, rtrks_rec=[h0s_trk],
                              rtrks=[w2t])
                if half == 0:
                    pt_a = vec(lambda e, m=m, pb=pb, wb=wb: e.tensor_mul(
                        chunk(acc_t, m), pb[0:128, 0:NT], wb[0:128, 0:NT]),
                        deps=[pt_o, pt_w0b], reads=[wbt],
                        writes=[acc_trk] if m == 0 else ())
                    acc_trk.write_at(pt_a)
                    ptrk.read_at(pt_a)
                else:
                    ac, at = aold_c[m % 2], aold_trk[m % 2]
                    dma("sync", ac[:], acc_v[0][:, m, j * NT:(j + 1) * NT],
                        wtrks=(at,), rtrks=(dtrk["acc0_sp"][j],))
                    pt_t = vec(lambda e, m=m, pb=pb, wb=wb: e.tensor_mul(
                        chunk(acc_t, m), pb[0:128, 0:NT], wb[0:128, 0:NT]),
                        deps=[pt_o, pt_w0b], reads=[wbt],
                        writes=[acc_trk] if m == 0 else ())
                    acc_trk.write_at(pt_t)
                    ptrk.read_at(pt_t)
                    pt_a = vec(lambda e, m=m, ac=ac: e.tensor_add(
                        chunk(acc_t, m), chunk(acc_t, m), ac[:]),
                        deps=[pt_t], reads=[at])
                    acc_trk.write_at(pt_a)
            dma("scalar", dram_tile(acc_v[0 if half == 0 else 1], j),
                sb_tile3(acc_t), rtrks=(acc_trk,),
                wtrks=(dtrk[f"acc{0 if half == 0 else 1}_sp"][j],))

        for j in range(NTIL):
            par = j % 2
            xn8, xn8t = xn8_t[par], xn8_trk[par]
            dma("sync", sb_tile3(xn8), dram_tile(x1n_v, j), wtrks=(xn8t,),
                rtrks=(dtrk["x1n_sp"][j],))
            if half == 0:
                cell = {}
                rgen = e0_router(j, xn8, xn8t, cell)
                e0_gemms(j, None, xn8, xn8t, rgen=rgen, cell=cell)
            else:
                pt_w0b = wrow_dma_bcast(j % 2, R_W0, j)
                e0_gemms(j, pt_w0b, xn8, xn8t)
        pairs = [(w1h, w1t), (w3h, w3t), (w2h, w2t), (h0s, h0s_trk),
                 (acc_t, acc_trk),
                 (xn8_t[0], xn8_trk[0]), (xn8_t[1], xn8_trk[1])]
        if half == 0:
            pairs += [(rw, rwt), (ident4, ident4_trk), (lsb, lsb_trk),
                      (lt, lt_trk), (gbuf, gbuf_trk), (wtok, wtok_trk),
                      (wrows, wrows_trk)]
        retire(*pairs)
        ph.close()

    ph_e0(0)
    ph_e0(1)

    def proj8(wsb, wtrk, pw, src_t, src_trk, dcols=1024, nk=KC, evac=None,
              kdeps=None, gen=None):
        pts = []
        for m in range(8):
            pb, ptrk = next_psM()
            if kdeps is None:
                pt_o = mm_acc(lambda k, m=m: wslice(wsb, dcols, k, m),
                              lambda k: chunk(src_t, k), nk, pb, ptrk,
                              deps=[pw], rtrks=[src_trk, wtrk])
            else:
                pt_o = mm_acc(lambda k, m=m: wslice(wsb, dcols, k, m),
                              lambda k: chunk(src_t, k), nk, pb, ptrk,
                              deps=[pw], kdeps=kdeps, rtrks_rec=[src_trk],
                              rtrks=[wtrk])
            pts.append(evac(m, pb, ptrk, pt_o))
            if gen is not None:
                next(gen, None)
        return pts

    ph5 = ES()
    sb5 = lambda nm, sh, dt=F32: ph5.enter_context(nc.sbuf_tensor(nm, sh, dt))
    eqw = sb5("eqw", [128, KC * 1024], BF16)
    mwv = sb5("me_wv", [128, KC * 1024], BF16)
    mwo = sb5("me_wo", [128, KC * 1024], BF16)
    c1w = sb5("c1w", [128, KC * 2048], BF16)
    c2w = sb5("c2w", [128, 16 * 1024], BF16)
    pr.new_phase()
    t5 = [fresh(w_, pr.w_sem()) for w_ in (eqw, mwv, mwo, c1w, c2w)]
    pw5 = [load_weight("L0_ffn_me_eq_w", eqw, t5[0], 1024),
           load_weight("L0_ffn_me_wv", mwv, t5[1], 1024),
           load_weight("L0_ffn_me_wo", mwo, t5[2], 1024),
           load_weight("L0_ffn_me_c1w", c1w, t5[3], 2048),
           load_weight("L0_ffn_me_c2w", c2w, t5[4], 1024)]
    x5 = sb5("x5", [128, KC * NT], BF16)
    x5_trk = fresh(x5, pr.buf_sem())
    eq_t = sb5("eq_t", [128, KC * NT], BF16)
    eq_trk = fresh(eq_t)
    v_t = sb5("v_t", [128, KC * NT], BF16)
    v_trk = fresh(v_t)
    sym_t = sb5("sym_t", [128, KC * NT], BF16)
    sym_trk = fresh(sym_t)
    h1_t = sb5("h1_t", [128, 16 * NT], BF16)
    h1_trk = fresh(h1_t)
    acc5_t = sb5("acc5_t", [128, KC * NT])
    acc5_trk = fresh(acc5_t, pr.buf_sem())
    for j in range(NTIL):
        dma("sync", sb_tile3(x5), dram_tile(x1n_v, j), wtrks=(x5_trk,),
            rtrks=(dtrk["x1n_sp"][j],))
        pt_wb = wrow_dma_bcast(j % 2, R_W1, j)

        def evac_act(dst_t, dst_trk, bias_name, func=AF.Identity):
            def f(m, pb, ptrk, pt_o):
                pe = sca(lambda e, pb=pb, m=m: e.activation(
                    chunk(dst_t, m), pb[0:128, 0:NT], func,
                    bias=bias_col(bias_name, m)),
                    deps=[pt_o], writes=[dst_trk] if m == 0 else ())
                dst_trk.write_at(pe)
                ptrk.read_at(pe)
                return pe
            return f

        p_eq = proj8(eqw, t5[0], pw5[0], x5, x5_trk,
                     evac=evac_act(eq_t, eq_trk, "L0_ffn_me_eq_b"))
        p_v = proj8(mwv, t5[1], pw5[1], eq_t, eq_trk, kdeps=p_eq,
                    evac=evac_act(v_t, v_trk, "L0_ffn_me_bv"))
        p_sym = proj8(mwo, t5[2], pw5[2], v_t, v_trk, kdeps=p_v,
                      evac=evac_act(sym_t, sym_trk, "L0_ffn_me_bo"))
        h1_pts = [None] * 16
        for mc in range(16):
            pb, ptrk = next_psM()
            pt_h = mm_acc(lambda k, mc=mc: wslice(c1w, 2048, k, mc),
                          lambda k: chunk(sym_t, k), KC, pb, ptrk,
                          deps=[pw5[3]], kdeps=p_sym, rtrks_rec=[sym_trk],
                          rtrks=[t5[3]])
            pe = sca(lambda e, pb=pb, mc=mc: e.activation(
                chunk(h1_t, mc), pb[0:128, 0:NT], AF.Gelu,
                bias=bias_col("L0_ffn_me_c1b", mc)),
                deps=[pt_h], writes=[h1_trk] if mc == 0 else ())
            h1_trk.write_at(pe)
            ptrk.read_at(pe)
            h1_pts[mc] = pe
        for m in range(8):
            pb, ptrk = next_psM()
            pt_o = mm_acc(lambda k, m=m: wslice(c2w, 1024, k, m),
                          lambda k: chunk(h1_t, k), 16, pb, ptrk,
                          deps=[pw5[4]], kdeps=h1_pts, rtrks_rec=[h1_trk],
                          rtrks=[t5[4]])
            acc_evac_rmw(pb, ptrk, m, j, "L0_ffn_me_c2b", pt_wb, 1,
                         acc5_t, acc5_trk, pt_o=pt_o)
        dma("scalar", dram_tile(acc_v[0], j), sb_tile3(acc5_t),
            rtrks=(acc5_trk,), wtrks=(dtrk["acc0_sp"][j],))

    retire((eqw, t5[0]), (mwv, t5[1]), (mwo, t5[2]), (c1w, t5[3]),
           (c2w, t5[4]), (x5, x5_trk), (eq_t, eq_trk), (v_t, v_trk),
           (sym_t, sym_trk), (h1_t, h1_trk), (acc5_t, acc5_trk))
    ph5.close()

    ph6 = ES()
    sb6 = lambda nm, sh, dt=F32: ph6.enter_context(nc.sbuf_tensor(nm, sh, dt))
    x6_t = [sb6(f"x6_{i}", [128, KC * NT], BF16) for i in range(2)]
    synw = sb6("synw", [128, KC * 1024], BF16)
    cwv = sb6("ce_wv", [128, KC * 1024], BF16)
    cwo = sb6("ce_wo", [128, KC * 1024], BF16)
    pr.new_phase()
    x6_trk = [fresh(x6_t[0], pr.buf_sem()), fresh(x6_t[1], pr.buf_sem())]
    t6 = [fresh(w_, pr.w_sem()) for w_ in (synw, cwv, cwo)]
    pw6 = [load_weight("L0_ffn_ce_syn_w", synw, t6[0], 1024),
           load_weight("L0_ffn_ce_wv", cwv, t6[1], 1024),
           load_weight("L0_ffn_ce_wo", cwo, t6[2], 1024)]
    syn_t = sb6("syn_t", [128, KC * NT])
    syn_trk = fresh(syn_t)
    synb_t = sb6("synb_t", [128, KC * NT], BF16)
    synb_trk = fresh(synb_t)
    v6_t = sb6("v6_t", [128, KC * NT], BF16)
    v6_trk = fresh(v6_t)
    s1_t = [sb6(f"s1_t6{i}", [128, KC * NT]) for i in range(2)]
    s1t_trk = [fresh(s1_t[0]), fresh(s1_t[1])]
    s1b_t = [sb6(f"s1b_t{i}", [128, KC * NT], BF16) for i in range(2)]
    s1b_trk = [fresh(s1b_t[0]), fresh(s1b_t[1])]
    h2_t = sb6("h2_t", [128, KC * NT], BF16)
    h2t_trk = fresh(h2_t, pr.buf_sem())
    rows6 = sb6("rows6", [1, 2 * NT])
    rows6_trk = fresh(rows6)

    def ph6_head(j, gen=None):
        par = j % 2
        x6, x6t = x6_t[par], x6_trk[par]
        s1, s1t = s1_t[par], s1t_trk[par]
        dma("sync", sb_tile3(x6), dram_tile(x1n_v, j), wtrks=(x6t,),
            rtrks=(dtrk["x1n_sp"][j],))

        def evac_syn(m, pb, ptrk, pt_o):
            pe = sca(lambda e, pb=pb, m=m: e.add(
                chunk(syn_t, m), pb[0:128, 0:NT], bias_col("L0_ffn_ce_syn_b", m)),
                deps=[pt_o], writes=[syn_trk] if m == 0 else ())
            syn_trk.write_at(pe)
            ptrk.read_at(pe)
            pc = vec(lambda e, m=m: e.tensor_copy(chunk(synb_t, m),
                                                  chunk(syn_t, m)),
                     deps=[pe], writes=[synb_trk] if m == 0 else ())
            synb_trk.write_at(pc)
            return pc

        p_syn = proj8(synw, t6[0], pw6[0], x6, x6t, evac=evac_syn, gen=gen)

        def evac_v(m, pb, ptrk, pt_o):
            pe = sca(lambda e, pb=pb, m=m: e.activation(
                chunk(v6_t, m), pb[0:128, 0:NT], AF.Identity,
                bias=bias_col("L0_ffn_ce_bv", m)),
                deps=[pt_o], writes=[v6_trk] if m == 0 else ())
            v6_trk.write_at(pe)
            ptrk.read_at(pe)
            return pe

        p_v6 = proj8(cwv, t6[1], pw6[1], synb_t, synb_trk, kdeps=p_syn,
                     evac=evac_v)

        def evac_s1(m, pb, ptrk, pt_o):
            pt_s = vec(lambda e, pb=pb, m=m, s1=s1: e.scalar_tensor_tensor(
                chunk(s1, m), pb[0:128, 0:NT], bias_col("L0_ffn_ce_bo", m),
                chunk(syn_t, m), ALU.add, ALU.add),
                deps=[pt_o], reads=[syn_trk] if m == 0 else (),
                writes=[s1t] if m == 0 else ())
            s1t.write_at(pt_s)
            syn_trk.read_at(pt_s)
            ptrk.read_at(pt_s)
            return pt_s

        proj8(cwo, t6[2], pw6[2], v6_t, v6_trk, kdeps=p_v6, evac=evac_s1)

    def ph6_tail(j):
        par = j % 2
        s1, s1t = s1_t[par], s1t_trk[par]
        s1b, s1bt = s1b_t[par], s1b_trk[par]
        cast_pt = None
        for c in range(KC):
            cast_pt = sca(lambda e, c=c, s1=s1, s1b=s1b: e.copy(
                chunk(s1b, c), chunk(s1, c)),
                reads=[s1t] if c == 0 else (),
                writes=[s1bt] if c == 0 else ())
        s1bt.write_at(cast_pt)
        s1t.read_at(cast_pt)
        yield
        do_b = ln_stats1(s1b, s1bt, rows6[:, 0:NT],
                         rows6[:, NT:2 * NT],
                         rows6_trk, 0, 1, deps=[cast_pt],
                         src_f32=s1, src_trk=s1t, defer=True)
        yield
        pt_mub, pt_rsb = do_b()
        pt_xc = None
        for c in range(KC):
            pt_xc = vec(lambda e, c=c, s1=s1: e.tensor_sub(
                chunk(s1, c), chunk(s1, c), psB[0][0:128, 0:NT]),
                deps=[pt_mub] if c == 0 else (),
                writes=[s1t] if c == 0 else ())
        s1t.write_at(pt_xc)
        psB_trk[0].read_at(pt_xc)
        yield
        pt_m = None
        for c in range(KC):
            pt_m = vec(lambda e, c=c, s1=s1: e.tensor_mul(
                chunk(s1, c), chunk(s1, c), psB[1][0:128, 0:NT]),
                deps=[pt_rsb, pt_xc] if c == 0 else ())
        s1t.write_at(pt_m)
        psB_trk[1].read_at(pt_m)
        yield
        h2_pt = None
        for c in range(KC):
            h2_pt = sca(lambda e, c=c, s1=s1: e.activation(
                chunk(h2_t, c), chunk(s1, c), AF.Identity,
                scale=bias_col("L0_ffn_ce_n1g", c),
                bias=bias_col("L0_ffn_ce_n1b", c)),
                deps=[pt_m] if c == 0 else (),
                writes=[h2t_trk] if c == 0 else ())
        h2t_trk.write_at(h2_pt)
        s1t.read_at(h2_pt)
        dma("scalar", dram_tile(h2_v, j), sb_tile3(h2_t), rtrks=(h2t_trk,),
            wtrks=(dtrk["h2_sp"][j],))
        yield

    for j in range(NTIL):
        g6 = ph6_tail(j - 1) if j > 0 else None
        ph6_head(j, g6)
        if g6 is not None:
            for _ in g6:
                pass
    for _ in ph6_tail(NTIL - 1):
        pass

    retire((synw, t6[0]), (cwv, t6[1]), (cwo, t6[2]),
           (x6_t[0], x6_trk[0]), (x6_t[1], x6_trk[1]), (syn_t, syn_trk),
           (synb_t, synb_trk), (v6_t, v6_trk), (s1_t[0], s1t_trk[0]),
           (s1_t[1], s1t_trk[1]), (s1b_t[0], s1b_trk[0]),
           (s1b_t[1], s1b_trk[1]), (h2_t, h2t_trk), (rows6, rows6_trk))
    ph6.close()

    ph7 = ES()
    sb7 = lambda nm, sh, dt=F32: ph7.enter_context(nc.sbuf_tensor(nm, sh, dt))
    h2b_t = [sb7(f"h2b{i}", [128, KC * NT], BF16) for i in range(2)]
    fh_t = sb7("fh_t", [128, 16 * NT], BF16)
    f1w = sb7("f1w", [128, KC * 2048], BF16)
    f2w = sb7("f2w", [128, 16 * 1024], BF16)
    genw = sb7("genw", [128, KC * 1024], BF16)
    pr.new_phase()
    h2b_trk = [fresh(h2b_t[0], pr.buf_sem()), fresh(h2b_t[1], pr.buf_sem())]
    fh_trk = fresh(fh_t)
    t7 = [fresh(w_, pr.w_sem()) for w_ in (f1w, f2w, genw)]
    pw7 = [load_weight("L0_ffn_ce_f1w", f1w, t7[0], 2048),
           load_weight("L0_ffn_ce_f2w", f2w, t7[1], 1024),
           load_weight("L0_ffn_ce_gen_w", genw, t7[2], 1024)]
    s2_t = [sb7(f"s2_t{i}", [128, KC * NT]) for i in range(2)]
    s2_trk = [fresh(s2_t[0]), fresh(s2_t[1])]
    s2b_t = [sb7(f"s2b_t{i}", [128, KC * NT], BF16) for i in range(2)]
    s2b_trk = [fresh(s2b_t[0]), fresh(s2b_t[1])]
    h2n_t = sb7("h2n_t", [128, KC * NT], BF16)
    h2n_trk = fresh(h2n_t)
    acc7_t = sb7("acc7_t", [128, KC * NT])
    acc7_trk = fresh(acc7_t, pr.buf_sem())
    rows7 = sb7("rows7", [1, 2 * NT])
    rows7_trk = fresh(rows7)

    def ph7_head(j, gen=None):
        par = j % 2
        h2b, h2bt = h2b_t[par], h2b_trk[par]
        s2, s2t = s2_t[par], s2_trk[par]
        dma("sync", sb_tile3(h2b), dram_tile(h2_v, j), wtrks=(h2bt,),
            rtrks=(dtrk["h2_sp"][j],))
        fh_pts = [None] * 16
        for mc in range(16):
            pb, ptrk = next_psM()
            pt_h = mm_acc(lambda k, mc=mc: wslice(f1w, 2048, k, mc),
                          lambda k, h2b=h2b: chunk(h2b, k), KC, pb, ptrk,
                          deps=[pw7[0]], rtrks=[h2bt, t7[0]])
            pe = sca(lambda e, pb=pb, mc=mc: e.activation(
                chunk(fh_t, mc), pb[0:128, 0:NT], AF.Relu,
                bias=bias_col("L0_ffn_ce_f1b", mc)),
                deps=[pt_h], writes=[fh_trk] if mc == 0 else ())
            fh_trk.write_at(pe)
            ptrk.read_at(pe)
            fh_pts[mc] = pe
            if gen is not None and mc % 2 == 1:
                next(gen, None)
        for m in range(8):
            pb, ptrk = next_psM()
            pt_o = mm_acc(lambda k, m=m: wslice(f2w, 1024, k, m),
                          lambda k: chunk(fh_t, k), 16, pb, ptrk,
                          deps=[pw7[1]], kdeps=fh_pts, rtrks_rec=[fh_trk],
                          rtrks=[t7[1]])
            pt_s = vec(lambda e, pb=pb, m=m, h2b=h2b, s2=s2:
                       e.scalar_tensor_tensor(
                chunk(s2, m), pb[0:128, 0:NT], bias_col("L0_ffn_ce_f2b", m),
                chunk(h2b, m), ALU.add, ALU.add),
                deps=[pt_o], reads=[h2bt] if m == 0 else (),
                writes=[s2t] if m == 0 else ())
            s2t.write_at(pt_s)
            h2bt.read_at(pt_s)
            ptrk.read_at(pt_s)
            if gen is not None:
                next(gen, None)

    def ph7_tail(j):
        par = j % 2
        s2, s2t = s2_t[par], s2_trk[par]
        s2b, s2bt = s2b_t[par], s2b_trk[par]
        c_sc = c_ve = None
        for c in range(KC):
            if c % 2 == 0:
                c_sc = sca(lambda e, c=c, s2=s2, s2b=s2b: e.copy(
                    chunk(s2b, c), chunk(s2, c)),
                    reads=[s2t] if c == 0 else (),
                    writes=[s2bt] if c == 0 else ())
            else:
                c_ve = vec(lambda e, c=c, s2=s2, s2b=s2b: e.tensor_copy(
                    chunk(s2b, c), chunk(s2, c)),
                    reads=[s2t] if c == 1 else (),
                    writes=[s2bt] if c == 1 else ())
        s2bt.write_at(c_sc)
        s2bt.write_at(c_ve)
        s2t.read_at(c_sc)
        s2t.read_at(c_ve)
        yield
        do_b = ln_stats1(s2b, s2bt, rows7[:, 0:NT],
                         rows7[:, NT:2 * NT],
                         rows7_trk, 0, 1, deps=[c_sc, c_ve], defer=True)
        yield
        pt_mub, pt_rsb = do_b()
        pt_xc = None
        for c in range(KC):
            pt_xc = vec(lambda e, c=c, s2=s2: e.tensor_sub(
                chunk(s2, c), chunk(s2, c), psB[0][0:128, 0:NT]),
                deps=[pt_mub] if c == 0 else (),
                writes=[s2t] if c == 0 else ())
        s2t.write_at(pt_xc)
        psB_trk[0].read_at(pt_xc)
        yield
        pt_m = None
        for c in range(KC):
            pt_m = vec(lambda e, c=c, s2=s2: e.tensor_mul(
                chunk(s2, c), chunk(s2, c), psB[1][0:128, 0:NT]),
                deps=[pt_rsb, pt_xc] if c == 0 else ())
        s2t.write_at(pt_m)
        psB_trk[1].read_at(pt_m)
        yield
        hn_pts = [None] * KC
        hn_pt = None
        for c in range(KC):
            hn_pt = sca(lambda e, c=c, s2=s2: e.activation(
                chunk(h2n_t, c), chunk(s2, c), AF.Identity,
                scale=bias_col("L0_ffn_ce_n2g", c),
                bias=bias_col("L0_ffn_ce_n2b", c)),
                deps=[pt_m] if c == 0 else (),
                writes=[h2n_trk] if c == 0 else ())
            hn_pts[c] = hn_pt
        h2n_trk.write_at(hn_pt)
        s2t.read_at(hn_pt)
        yield
        pt_wb = wrow_dma_bcast(j % 2, R_W2, j)
        for m in range(8):
            pb, ptrk = next_psM()
            pt_o = mm_acc(lambda k, m=m: wslice(genw, 1024, k, m),
                          lambda k: chunk(h2n_t, k), KC, pb, ptrk,
                          deps=[pw7[2]], kdeps=hn_pts, rtrks_rec=[h2n_trk],
                          rtrks=[t7[2]])
            acc_evac_rmw(pb, ptrk, m, j, "L0_ffn_ce_gen_b", pt_wb, 0,
                         acc7_t, acc7_trk, pt_o=pt_o)
            if m % 2 == 1:
                yield
        dma("scalar", dram_tile(acc_v[1], j), sb_tile3(acc7_t),
            rtrks=(acc7_trk,), wtrks=(dtrk["acc1_sp"][j],))
        yield

    for j in range(NTIL):
        g7 = ph7_tail(j - 1) if j > 0 else None
        ph7_head(j, g7)
        if g7 is not None:
            for _ in g7:
                pass
    for _ in ph7_tail(NTIL - 1):
        pass

    retire((f1w, t7[0]), (f2w, t7[1]), (genw, t7[2]),
           (h2b_t[0], h2b_trk[0]), (h2b_t[1], h2b_trk[1]), (fh_t, fh_trk),
           (s2_t[0], s2_trk[0]), (s2_t[1], s2_trk[1]),
           (s2b_t[0], s2b_trk[0]), (s2b_t[1], s2b_trk[1]),
           (h2n_t, h2n_trk), (acc7_t, acc7_trk),
           (rows7, rows7_trk))
    ph7.close()

    ph8 = ES()
    sb8 = lambda nm, sh, dt=F32: ph8.enter_context(nc.sbuf_tensor(nm, sh, dt))
    mw1 = sb8("ml_w1", [128, KC * 4096], BF16)
    mw2 = sb8("ml_w2", [128, 32 * 1024], BF16)
    pr.new_phase()
    t8 = [fresh(mw1, pr.w_sem()), fresh(mw2, pr.w_sem())]
    pw8 = [load_weight("L0_ffn_ml_w1", mw1, t8[0], 4096),
           load_weight("L0_ffn_ml_w2", mw2, t8[1], 1024)]
    x8_t = [sb8(f"x8_{i}", [128, KC * NT], BF16) for i in range(2)]
    x8_trk = [fresh(x8_t[0], pr.buf_sem()), fresh(x8_t[1], pr.buf_sem())]
    h3_t = sb8("h3_t", [128, 32 * NT], BF16)
    h3_trk = fresh(h3_t)
    f_t = sb8("f_t", [128, KC * NT])
    f_trk = fresh(f_t, pr.buf_sem())
    x1c_c = [sb8("x1c_c0", [128, NT])] * 2
    x1c_trk = [fresh(x1c_c[0], pr.buf_sem())] * 2
    rows8 = sb8("rows8", [1, 2 * NT])
    rows8_trk = fresh(rows8)

    def ph8_head(j, gen=None):
        par = j % 2
        x8, x8t = x8_t[par], x8_trk[par]
        dma("sync", sb_tile3(x8), dram_tile(x1n_v, j), wtrks=(x8t,),
            rtrks=(dtrk["x1n_sp"][j],))
        pt_wb = wrow_dma_bcast(j % 2, R_W3, j)
        h3_pts = [None] * 32
        for mc in range(32):
            pb, ptrk = next_psM()
            pt_h = mm_acc(lambda k, mc=mc: wslice(mw1, 4096, k, mc),
                          lambda k, x8=x8: chunk(x8, k), KC, pb, ptrk,
                          deps=[pw8[0]], rtrks=[x8t, t8[0]])
            pe = sca(lambda e, pb=pb, mc=mc: e.activation(
                chunk(h3_t, mc), pb[0:128, 0:NT], AF.Gelu,
                bias=bias_col("L0_ffn_ml_b1", mc)),
                deps=[pt_h], writes=[h3_trk] if mc == 0 else ())
            h3_trk.write_at(pe)
            ptrk.read_at(pe)
            h3_pts[mc] = pe
            if gen is not None and mc % 3 == 2:
                next(gen, None)
        return h3_pts, pt_wb

    def ph8_headB(j, h3_pts, pt_wb):
        for m in range(8):
            pb, ptrk = next_psM()
            pt_o = mm_acc(lambda k, m=m: wslice(mw2, 1024, k, m),
                          lambda k: chunk(h3_t, k), 32, pb, ptrk,
                          deps=[pw8[1]], kdeps=h3_pts, rtrks_rec=[h3_trk],
                          rtrks=[t8[1]])
            acc_evac_rmw(pb, ptrk, m, j, "L0_ffn_ml_b2", pt_wb, 1,
                         f_t, f_trk, pt_o=pt_o)

    def ph8_tail(j):
        par = j % 2
        fb, fbt = x8_t[par], x8_trk[par]
        c_sc = c_ve = None
        for c in range(KC):
            if c % 2 == 0:
                c_sc = sca(lambda e, c=c, fb=fb: e.copy(chunk(fb, c),
                                                        chunk(f_t, c)),
                           reads=[f_trk] if c == 0 else (),
                           writes=[fbt] if c == 0 else ())
            else:
                c_ve = vec(lambda e, c=c, fb=fb: e.tensor_copy(
                    chunk(fb, c), chunk(f_t, c)),
                    reads=[f_trk] if c == 1 else (),
                    writes=[fbt] if c == 1 else ())
        fbt.write_at(c_sc)
        fbt.write_at(c_ve)
        f_trk.read_at(c_sc)
        f_trk.read_at(c_ve)
        yield
        do_b = ln_stats1(fb, fbt, rows8[:, 0:NT], rows8[:, NT:2 * NT],
                         rows8_trk, 0, 1,
                         deps=[c_sc, c_ve], src_f32=f_t,
                         src_trk=f_trk, defer=True)
        yield
        pt_mub, pt_rsb = do_b()
        pt_xc = None
        for c in range(KC):
            pt_xc = vec(lambda e, c=c: e.tensor_sub(chunk(f_t, c), chunk(f_t, c),
                                                    psB[0][0:128, 0:NT]),
                        deps=[pt_mub] if c == 0 else (),
                        writes=[f_trk] if c == 0 else ())
        f_trk.write_at(pt_xc)
        psB_trk[0].read_at(pt_xc)
        yield
        pt_o2 = None
        for c in range(KC):
            pt_m = vec(lambda e, c=c: e.tensor_mul(chunk(f_t, c), chunk(f_t, c),
                                                   psB[1][0:128, 0:NT]),
                       deps=[pt_rsb, pt_xc] if c == 0 else ())
            f_trk.write_at(pt_m)
            xc, xct = x1c_c[c % 2], x1c_trk[c % 2]
            dma("sync", xc[:], x1_v[:, c, j * NT:(j + 1) * NT],
                wtrks=(xct,), rtrks=(dtrk["x1_sp"][j],))
            pt_o2 = vec(lambda e, c=c, xc=xc: e.tensor_add(chunk(f_t, c),
                                                           chunk(f_t, c),
                                                           xc[:]),
                        deps=[pt_m], reads=[xct])
            f_trk.write_at(pt_o2)
            if c % 3 == 2:
                yield
        psB_trk[1].read_at(pt_o2)
        dma("sync", dram_tile(out_v, j), sb_tile3(f_t), rtrks=(f_trk,))
        yield

    for j in range(NTIL):
        g8 = ph8_tail(j - 1) if j > 0 else None
        h3_pts, pt_wb8 = ph8_head(j, g8)
        if g8 is not None:
            for _ in g8:
                pass
        ph8_headB(j, h3_pts, pt_wb8)
    for _ in ph8_tail(NTIL - 1):
        pass

    ph8.close()

    for s_, v_ in sorted(pr.counts.items()):
        pr.wait("sync", s_, v_)

    with nc.Block() as block:
        pr.finalize(block)
    ctx.close()
    return nc


def _pack_biases(inputs):
    out = np.zeros((128, NBIAS), np.float32)
    for name, ncol in BIAS_SPEC:
        v = np.asarray(inputs[name], np.float32)
        off = BIAS_OFF[name]
        if v.shape[0] == 4:
            out[0:4, off] = v
        else:
            out[:, off:off + ncol] = v.reshape(ncol, 128).T
    return out


def make_in_maps(inputs):
    x = np.asarray(inputs["L0_attn_x"])
    biasv = _pack_biases(inputs)
    ident = np.eye(128, dtype=np.float32)
    in_maps = []
    for c in range(8):
        bi, si = c // 4, c % 4
        xs = np.ascontiguousarray(x[bi, si * T:(si + 1) * T, :].T)
        fold = np.zeros((128, 16), np.float32)
        for jj in range(8):
            if jj // 4 == bi and (jj % 4) < si:
                fold[:, jj] = 1.0
        fold[:, 8:16] = 1.0 - fold[:, 0:8]
        m = {"x": xs, "biasv": biasv, "foldmask": fold, "identity": ident}
        for n in WEIGHT_NAMES:
            m[n] = np.asarray(inputs[n])
        in_maps.append(m)
    return in_maps


def kernel(**inputs):
    nc = build_nc()
    in_maps = make_in_maps(inputs)
    res = run_bass_kernel_spmd(nc, in_maps, core_ids=list(range(8)))
    outs = [np.asarray(res.results[c]["out"]).T for c in range(8)]
    return np.stack([np.concatenate(outs[b * 4:(b + 1) * 4], axis=0)
                     for b in range(2)], axis=0)
